# version 08
# speedup vs baseline: 1.1118x; 1.0352x over previous
import contextlib
import numpy as np
import ml_dtypes
import concourse.bass as bass
import concourse.mybir as mybir
from concourse.bass_utils import run_bass_kernel_spmd

F32 = mybir.dt.float32
BF16 = mybir.dt.bfloat16
I32 = mybir.dt.int32
AF = mybir.ActivationFunctionType
ALU = mybir.AluOpType
AX = mybir.AxisListType
BF = ml_dtypes.bfloat16

D = 1024
T = 4096
NC = 8
CH = D // 128

ENGS = ("pe", "act", "dve", "pool", "sp")
QUEUES = {"qsp": "sp", "qact": "act", "qpool": "pool"}


def _key(b):
    if isinstance(b, (str, tuple)):
        return b
    n = getattr(b, "name", None)
    return n if n else id(b)


class Sched:

    def __init__(self, nc):
        self.nc = nc
        self.ops = []
        self.deps = []
        self.state = {}
        self.strict_deps = []
        self.children = {}
        self.last_per_chan = {}
        self._fence = []

    def fence(self):
        self._fence = list(self.last_per_chan.values())

    @staticmethod
    def _parse(b):
        if isinstance(b, tuple) and len(b) >= 2 and not isinstance(b[0], str):
            pk = _key(b[0])
            return pk, (pk,) + tuple(b[1:])
        return None, _key(b)

    def _rd_deps(self, pk, k):
        dd = []
        st = self.state.get(k)
        if st:
            dd.extend(st[0])
        if pk is not None:
            st = self.state.get(pk)
            if st:
                dd.extend(st[0])
        else:
            for ck in self.children.get(k, ()):
                st = self.state.get(ck)
                if st:
                    dd.extend(st[0])
        return dd

    def _wr_deps(self, pk, k):
        dd = []
        for kk in ([k, pk] if pk is not None else
                   [k] + list(self.children.get(k, ()))):
            if kk is None:
                continue
            st = self.state.get(kk)
            if st:
                dd.extend(st[1] if st[1] else st[0])
        return dd

    def op(self, eng, fn, reads=(), writes=(), queue=None, extra_deps=(),
           strict=()):
        i = len(self.ops)
        dd = list(extra_deps) + self._fence
        sd = []
        pr = [self._parse(b) for b in reads]
        pw = [self._parse(b) for b in writes]
        for pk, k in pr:
            dd.extend(self._rd_deps(pk, k))
        for b in strict:
            pk, k = self._parse(b)
            ds = self._rd_deps(pk, k)
            dd.extend(ds)
            sd.extend(ds)
            self.state.setdefault(k, ([], []))[1].append(i)
        for pk, k in pw:
            dd.extend(self._wr_deps(pk, k))
        self.ops.append((eng, fn, queue))
        self.deps.append(sorted({d for d in dd if d is not None}))
        self.strict_deps.append({d for d in sd if d is not None})
        self.last_per_chan[queue if queue else eng] = i
        for pk, k in pr:
            self.state.setdefault(k, ([], []))[1].append(i)
        for pk, k in pw:
            self.state[k] = ([i], [])
            if pk is not None:
                self.children.setdefault(pk, set()).add(k)
            else:
                for ck in self.children.pop(k, ()):
                    self.state.pop(ck, None)
        return i

    def dma(self, queue, out, in_, reads=(), writes=(), **kw):
        if queue == "qsp":
            self._rr = getattr(self, "_rr", 0) + 1
            if self._rr % 2:
                queue = "qact"
        def fn(e, out=out, in_=in_, kw=kw):
            return e.dma_start(out=out, in_=in_, **kw)
        return self.op(QUEUES[queue], fn, reads=reads, writes=writes, queue=queue)

    def build(self, es):
        nc = self.nc
        n = len(self.ops)
        chan = [(q if q else e) for (e, f, q) in self.ops]
        is_dma = [bool(q) for (e, f, q) in self.ops]
        need_inc = list(is_dma)
        for i in range(n):
            for d in self.deps[i]:
                if not (chan[d] == chan[i] == "pe" and not is_dma[d]
                        and not is_dma[i]):
                    need_inc[d] = True
        NSLOT = 8
        chans = sorted(set(chan)) or ["sp"]
        sems = {}
        for c in chans:
            if c in QUEUES:
                for sl in range(NSLOT):
                    sems[(c, sl)] = es.enter_context(nc.semaphore(f"s_{c}{sl}"))
            else:
                sems[c] = es.enter_context(nc.semaphore(f"s_{c}"))
        cum = {}
        semkey = [None] * n
        val_after = [0] * n
        qidx = {c: 0 for c in chans if c in QUEUES}
        for i in range(n):
            c = chan[i]
            if is_dma[i]:
                k = (c, qidx[c] % NSLOT)
                qidx[c] += 1
                cum[k] = cum.get(k, 0) + 16
                semkey[i] = k
                val_after[i] = cum[k]
            else:
                if need_inc[i]:
                    cum[c] = cum.get(c, 0) + 1
                semkey[i] = c
                val_after[i] = cum.get(c, 0)
        final = dict(cum)
        per_eng = {e: [] for e in ENGS}
        for i in range(n):
            per_eng[self.ops[i][0]].append(i)

        with nc.Block() as blk:
            def body(eng_name, eobj):
                last_wait = {}
                for i in per_eng[eng_name]:
                    for d in self.deps[i]:
                        if (chan[d] == chan[i] == "pe" and not is_dma[d]
                                and not is_dma[i]):
                            continue
                        sk = semkey[d]
                        v = val_after[d]
                        if v > last_wait.get(sk, 0):
                            eobj.wait_ge(sems[sk], v)
                            last_wait[sk] = v
                    try:
                        inst = self.ops[i][1](eobj)
                    except AssertionError as _ex:
                        raise AssertionError(f"op {i} eng {eng_name}: {_ex}")
                    if need_inc[i]:
                        inst.then_inc(sems[semkey[i]], 16 if is_dma[i] else 1)
                for (sk, v) in final.items():
                    if isinstance(sk, tuple) and QUEUES.get(sk[0]) == eng_name:
                        eobj.wait_ge(sems[sk], v)

            @blk.tensor
            def _(te):
                body("pe", te)

            @blk.scalar
            def _(sc):
                body("act", sc)

            @blk.vector
            def _(ve):
                body("dve", ve)

            @blk.gpsimd
            def _(gp):
                body("pool", gp)

            @blk.sync
            def _(sy):
                body("sp", sy)


class Ctx:
    def __init__(self, nc, es, slim=False):
        self.nc = nc
        self.es = es
        self.s = Sched(nc)
        self.pb = [es.enter_context(nc.psum_tensor(f"pb{i}", [128, 512], F32))
                   for i in range(6)]
        self.pbw = es.enter_context(nc.psum_tensor("pbw", [128, 1024], F32))
        self.pb.append(self.pbw[:, 0:512])
        self.pb.append(self.pbw[:, 512:1024])
        self.ones_col_f = self.sbuf("ones_col_f", [128, 1], F32)
        self.ones_row_f = self.sbuf("ones_row_f", [1, 128], F32)
        self.ones_col_b = self.sbuf("ones_col_b", [128, 1], BF16)
        self.ones_row_b = self.sbuf("ones_row_b", [1, 128], BF16)
        for t in (self.ones_col_f, self.ones_row_f, self.ones_col_b, self.ones_row_b):
            self.s.op("pool", lambda e, t=t: e.memset(t[:], 1.0), writes=[t])
        nx = 1 if slim else 2
        self.xtf = [self.sbuf(f"c_xtf{i}", [128, CH, 512], F32) for i in range(nx)]
        if slim:
            self.xtf.append(self.xtf[0])
        self.sq = self.sbuf("c_sq", [128, CH, 512], BF16)
        self.xnt = [self.sbuf(f"c_xnt{i}", [128, CH, 512], BF16) for i in range(2)]

    def sbuf(self, name, shape, dt):
        return self.es.enter_context(self.nc.sbuf_tensor(name, shape, dt))

    def lnrow(self, i):
        if not hasattr(self, "_lnrows"):
            self._lnrows = [self.sbuf(f"lnrow{j}", [1, 512], F32) for j in range(4)]
        return self._lnrows[i]

    def constcol(self, val, dt=F32):
        if not hasattr(self, "_consts"):
            self._consts = {}
        key = (float(val), dt)
        if key not in self._consts:
            t = self.sbuf(f"cc{len(self._consts)}", [128, 1], dt)
            self.s.op("pool", lambda e, t=t, v=float(val): e.memset(t[:], v),
                      writes=[t])
            self._consts[key] = t
        return self._consts[key]

    def mm(self, out, lhsT, rhs, start, stop, reads, writes, extra_deps=()):
        return self.s.op(
            "pe", lambda e: e.matmul(out, lhsT, rhs, start=start, stop=stop),
            reads=reads, writes=writes, extra_deps=extra_deps)

    def act(self, out, in_, func, reads, writes, bias=None, scale=None):
        kw = {}
        reads = list(reads)
        if bias is not None:
            if isinstance(bias, (int, float)) and func != AF.Copy:
                t = self.constcol(bias)
                bias = t[0:out.shape[0], :]
                reads.append(t)
            kw["bias"] = bias
        if scale is not None:
            kw["scale"] = scale
        return self.s.op(
            "act", lambda e: e.activation(out, in_, func, **kw),
            reads=reads, writes=writes)

    def tt(self, out, a, b, opk, reads, writes, eng="dve"):
        return self.s.op(
            eng, lambda e: e.tensor_tensor(out, a, b, opk),
            reads=reads, writes=writes)


def emit_ln(ctx, x_dram, xn_dram, eps=1e-5, src_dt=F32, gcol=None, bcol=None,
            in_bufs=None):
    s = ctx.s
    TT = 512
    nt = T // TT
    xr = x_dram.rearrange("(c p) t -> p c t", p=128)
    xnr = xn_dram.rearrange("(c p) t -> p c t", p=128)
    XT = ctx.xtf if src_dt == F32 else in_bufs
    ones = ctx.ones_col_f if src_dt == F32 else ctx.ones_col_b
    SQ = ctx.sq
    XNT = ctx.xnt
    mu = ctx.lnrow(0)
    msq = ctx.lnrow(1)
    var = ctx.lnrow(2)
    rstd = ctx.lnrow(3)
    for i in range(nt):
        b = i % 2
        xt, xnt = XT[b], XNT[b]
        ps, pq = ctx.pb[0 + b], ctx.pb[2 + b]
        pmu, prs = ctx.pb[4 + b], ctx.pb[6 + b]
        t0 = i * TT
        s.dma("qsp", xt[:], xr[:, :, t0:t0 + TT], reads=[x_dram], writes=[xt])
        ctx.act(SQ[:, :, :], xt[:, :, :], AF.Square, reads=[xt], writes=[SQ])
        for c in range(CH):
            ctx.mm(ps[0:1, :], ones[:], xt[:, c, :], c == 0, c == CH - 1,
                   reads=[xt, ones], writes=[ps])
        for c in range(CH):
            ctx.mm(pq[0:1, :], ctx.ones_col_b[:], SQ[:, c, :], c == 0, c == CH - 1,
                   reads=[SQ, ctx.ones_col_b], writes=[pq])
        ctx.act(mu[:], ps[0:1, :], AF.Copy, reads=[ps], writes=[mu], scale=1.0 / D)
        ctx.act(msq[:], pq[0:1, :], AF.Copy, reads=[pq], writes=[msq], scale=1.0 / D)
        ctx.tt(var[:], mu[:], mu[:], ALU.mult, reads=[mu], writes=[var])
        ctx.tt(var[:], msq[:], var[:], ALU.subtract, reads=[msq, var], writes=[var])
        ctx.act(var[:], var[:], AF.Sqrt, reads=[var], writes=[var], bias=float(eps))
        s.op("dve", lambda e, o=rstd, a=var: e.reciprocal(o[:], a[:]),
             reads=[var], writes=[rstd])
        ctx.mm(pmu[:], ctx.ones_row_f[:], mu[:], True, True,
               reads=[mu, ctx.ones_row_f], writes=[pmu])
        ctx.mm(prs[:], ctx.ones_row_f[:], rstd[:], True, True,
               reads=[rstd, ctx.ones_row_f], writes=[prs])
        for c in range(CH):
            ctx.tt(xnt[:, c, :], xt[:, c, :], pmu[:], ALU.subtract,
                   reads=[xt, pmu], writes=[xnt])
        for c in range(CH):
            ctx.tt(xnt[:, c, :], xnt[:, c, :], prs[:], ALU.mult,
                   reads=[xnt, prs], writes=[xnt])
        if gcol is not None:
            gb_, go_ = gcol
            bb_, bo_ = bcol
            for c in range(CH):
                s.op("dve", lambda e, o=xnt, c=c: e.tensor_scalar(
                    o[:, c, :], o[:, c, :], gb_[:, go_ + c:go_ + c + 1],
                    bb_[:, bo_ + c:bo_ + c + 1], ALU.mult, ALU.add),
                    reads=[xnt, gb_, bb_], writes=[xnt])
        s.dma("qsp", xnr[:, :, t0:t0 + TT], xnt[:], reads=[xnt], writes=[xn_dram])


def emit_resadd(ctx, x_dram, s_dram, xo_dram):
    s = ctx.s
    xr = x_dram.rearrange("(c p) t -> p c t", p=128)
    sr = s_dram.rearrange("(c p) t -> p c t", p=128)
    xor_ = xo_dram.rearrange("(c p) t -> p c t", p=128)
    for i in range(T // 512):
        b = i % 2
        xt = ctx.xtf[b]
        st = ctx.xnt[b]
        tsl = slice(i * 512, (i + 1) * 512)
        s.dma("qsp", xt[:], xr[:, :, tsl], reads=[x_dram], writes=[xt])
        s.dma("qsp", st[:], sr[:, :, tsl], reads=[s_dram], writes=[st])
        for c in range(CH):
            ctx.tt(xt[:, c, :], xt[:, c, :], st[:, c, :], ALU.add,
                   reads=[xt, st], writes=[xt])
        s.dma("qsp", xor_[:, :, tsl], xt[:], reads=[xt], writes=[xo_dram])


def emit_l0f(ctx, d):
    s = ctx.s
    HID = 3072
    NHC = HID // 128
    SC = 1.0 / 127.0
    xnr = d["xn"].rearrange("(c p) t -> p c t", p=128)
    xr = d["x"].rearrange("(c p) t -> p c t", p=128)
    xor_ = d["xo"].rearrange("(c p) t -> p c t", p=128)
    wgTr = d["wgT"].rearrange("(c p) h -> p c h", p=128)
    wuTr = d["wuT"].rearrange("(c p) h -> p c h", p=128)
    wdTr = d["wdT"].rearrange("(h p) o -> p h o", p=128)
    hr = d["h"].rearrange("(hc p) t -> p hc t", p=128)

    WD = ctx.sbuf("l0f_wd", [128, NHC, D], BF16)
    for g in range(4):
        s.dma("qpool", WD[:, g * 6:(g + 1) * 6, :], wdTr[:, g * 6:(g + 1) * 6, :],
              writes=[WD])
    emit_ln(ctx, d["x"], d["xn"])
    WG = [ctx.sbuf(f"l0f_wg{i}", [128, CH, 512], BF16) for i in range(2)]
    WU = [ctx.sbuf(f"l0f_wu{i}", [128, CH, 512], BF16) for i in range(2)]
    HB = [ctx.sbuf(f"l0f_hb{i}", [128, 4, 512], BF16) for i in range(2)]
    GS = [ctx.sbuf(f"l0f_gs{i}", [128, 512], BF16) for i in range(2)]
    US = [ctx.sbuf(f"l0f_us{i}", [128, 512], BF16) for i in range(2)]
    OUT = [ctx.sbuf(f"l0f_out{i}", [128, 512], F32) for i in range(2)]
    HT = [ctx.sbuf("l0f_ht0", [128, NHC, 512], BF16)]

    for hg in range(HID // 512):
        wg, wu = WG[hg % 2], WU[hg % 2]
        s.dma("qpool", wg[:], wgTr[:, :, hg * 512:(hg + 1) * 512], writes=[wg])
        s.dma("qpool", wu[:], wuTr[:, :, hg * 512:(hg + 1) * 512], writes=[wu])
        for it in range(T // 512):
            tsl = slice(it * 512, (it + 1) * 512)
            xn = ctx.xnt[it % 2]
            s.dma("qsp", xn[:], xnr[:, :, tsl], reads=[d["xn"]], writes=[xn])
            hb = HB[it % 2]
            for hi in range(4):
                pg, pu = ctx.pb[hi % 2], ctx.pb[2 + hi % 2]
                gs, us = GS[hi % 2], US[hi % 2]
                hsl = slice(hi * 128, hi * 128 + 128)
                for c in range(CH):
                    ctx.mm(pg[:], wg[:, c, hsl], xn[:, c, :], c == 0, c == CH - 1,
                           reads=[wg, xn], writes=[pg])
                for c in range(CH):
                    ctx.mm(pu[:], wu[:, c, hsl], xn[:, c, :], c == 0, c == CH - 1,
                           reads=[wu, xn], writes=[pu])
                ctx.act(gs[:], pg[:], AF.Silu, reads=[pg], writes=[gs], scale=SC)
                ctx.act(us[:], pu[:], AF.Copy, reads=[pu], writes=[us], scale=SC)
                ctx.tt(hb[:, hi, :], gs[:], us[:], ALU.mult,
                       reads=[gs, us], writes=[hb])
            s.dma("qsp", hr[:, hg * 4:(hg + 1) * 4, tsl], hb[:],
                  reads=[hb], writes=[d["h"]])

    for it in range(T // 512):
        tsl = slice(it * 512, (it + 1) * 512)
        ht = HT[0]
        xt = ctx.xtf[it % 2]
        s.dma("qsp", ht[:], hr[:, :, tsl], reads=[d["h"]], writes=[ht])
        s.dma("qsp", xt[:], xr[:, :, tsl], reads=[d["x"]], writes=[xt])
        for dc in range(CH):
            pd = ctx.pb[4 + dc % 2]
            out = OUT[dc % 2]
            for hc in range(NHC):
                ctx.mm(pd[:], WD[:, hc, dc * 128:(dc + 1) * 128], ht[:, hc, :],
                       hc == 0, hc == NHC - 1, reads=[WD, ht], writes=[pd])
            s.op("dve", lambda e, o=out, a=pd, x=xt, dc=dc: e.scalar_tensor_tensor(
                o[:], a[:], SC, x[:, dc, :], ALU.mult, ALU.add),
                reads=[pd, xt], writes=[out])
            s.dma("qsp", xor_[:, dc, tsl], out[:], reads=[out], writes=[d["xo"]])


def emit_l1a(ctx, d):
    s = ctx.s
    NH = 16
    emit_ln(ctx, d["x"], d["xn"])
    xnr = d["xn"].rearrange("(c p) t -> p c t", p=128)
    qr = d["q"]
    ar = d["a"].rearrange("(c p) t -> p c t", p=128)
    t1r = d["t1"].rearrange("(c p) t -> p c t", p=128)
    ur = d["u"].rearrange("(c p) t -> p c t", p=128)
    h1r = d["h1"].rearrange("(h p) t -> p h t", p=128)
    v2r = d["v2"].rearrange("(c p) t -> p c t", p=128)

    WBUF = ctx.sbuf("l1a_wbuf", [128, 32, 1024], BF16)
    XBT = [ctx.sbuf(f"l1a_xbt{i}", [128, CH, 512], BF16) for i in range(2)]
    COLS = ctx.sbuf("l1a_cols", [128, 64], F32)
    B1C = ctx.sbuf("l1a_b1c", [128, 32], F32)
    BROW = ctx.sbuf("l1a_brow", [1, 2048], BF16)
    TMP = [ctx.sbuf(f"l1a_tmp{i}", [128, 512], F32) for i in range(4)]
    OUTB = [ctx.sbuf(f"l1a_ob{i}", [128, 512], BF16) for i in range(2)]
    KVS = ctx.sbuf("l1a_kvs", [64, 1024], BF16)
    KSC = ctx.sbuf("l1a_ksc", [64, 16], BF16)
    ZR = ctx.sbuf("l1a_zr", [1, 512], F32)
    HST = ctx.sbuf("l1a_hst", [128, 32, 512], BF16)

    s.dma("qpool", COLS[:], d["cols"][:], writes=[COLS])
    s.dma("qpool", B1C[:], d["cols2"][:], writes=[B1C])
    s.dma("qpool", BROW[:], d["brow"][:], writes=[BROW])
    bq_c = lambda oc: COLS[:, oc:oc + 1]
    bo_c = lambda oc: COLS[:, 8 + oc:9 + oc]
    b2_c = lambda oc: COLS[:, 16 + oc:17 + oc]

    def load_w(name, pat):
        wr = d[name].rearrange(pat, p=128)
        nch = wr.shape[1]
        step = max(1, nch // 4)
        wf = WBUF[:, :, :].rearrange("p a o -> p (a o)")
        width = wr.shape[2]
        for g in range(0, nch, step):
            dst = wf[:, g * width:(g + step) * width].rearrange(
                "p (a b) -> p a b", b=width)
            s.dma("qpool", dst, wr[:, g:g + step, :], writes=[WBUF])

    for i, nm in enumerate(["wq", "wk", "wv", "wo"]):
        wr = d[nm].rearrange("(c p) o -> p c o", p=128)
        dst = WBUF[:, 8 * i:8 * i + 8, :]
        s.dma("qpool", dst, wr, writes=[WBUF])
    WQ = WBUF[:, 0:8, :]
    WK = WBUF[:, 8:16, :]
    WV = WBUF[:, 16:24, :]
    WO = WBUF[:, 24:32, :]
    for it in range(8):
        tsl = slice(it * 512, (it + 1) * 512)
        xn = XBT[it % 2]
        s.dma("qsp", xn[:], xnr[:, :, tsl], reads=[d["xn"]], writes=[xn])
        for oc in range(CH):
            pg = ctx.pb[oc % 2]
            for c in range(CH):
                ctx.mm(pg[:], WQ[:, c, oc * 128:(oc + 1) * 128], xn[:, c, :],
                       c == 0, c == CH - 1, reads=[WBUF, xn], writes=[pg])
            t1_, t2_ = TMP[2 * (oc % 2)], TMP[2 * (oc % 2) + 1]
            s.op("dve", lambda e, p=pg, t=t1_, oc=oc: e.tensor_scalar(
                t[:], p[:], bq_c(oc), 0.0, ALU.add, ALU.min),
                reads=[pg, COLS], writes=[t1_])
            ctx.act(t1_[:], t1_[:], AF.Exp, reads=[t1_], writes=[t1_])
            s.op("dve", lambda e, p=pg, t=t2_, oc=oc: e.tensor_scalar(
                t[:], p[:], bq_c(oc), 0.0, ALU.add, ALU.max),
                reads=[pg, COLS], writes=[t2_])
            ob = OUTB[oc % 2]
            ctx.tt(ob[:], t1_[:], t2_[:], ALU.add, reads=[t1_, t2_], writes=[ob])
            s.dma("qsp", qr[oc * 128:(oc + 1) * 128, tsl], ob[:],
                  reads=[ob], writes=[d["q"]])
        for st_ in range(4):
            ssl = slice(st_ * 128, st_ * 128 + 128)
            trow = it * 512 + st_ * 128
            for mh in range(2):
                msl = slice(mh * 512, mh * 512 + 512)
                pk, pv = ctx.pb[2 + mh], ctx.pb[4 + mh]
                for c in range(CH):
                    ctx.mm(pk[:], xn[:, c, ssl], WK[:, c, msl], c == 0, False,
                           reads=[xn, WBUF], writes=[pk])
                ctx.mm(pk[:], ctx.ones_row_b[:], BROW[0:1, msl], False, True,
                       reads=[ctx.ones_row_b, BROW], writes=[pk])
                for c in range(CH):
                    ctx.mm(pv[:], xn[:, c, ssl], WV[:, c, msl], c == 0, False,
                           reads=[xn, WBUF], writes=[pv])
                ctx.mm(pv[:], ctx.ones_row_b[:],
                       BROW[0:1, 1024 + mh * 512:1536 + mh * 512], False, True,
                       reads=[ctx.ones_row_b, BROW], writes=[pv])
                t1_, t2_ = TMP[0], TMP[1]
                s.op("dve", lambda e, p=pk, t=t1_: e.tensor_scalar(
                    t[:], p[:], 0.0, None, ALU.min), reads=[pk], writes=[t1_])
                ctx.act(t1_[:], t1_[:], AF.Exp, reads=[t1_], writes=[t1_])
                s.op("dve", lambda e, p=pk, t=t2_: e.tensor_scalar(
                    t[:], p[:], 0.0, None, ALU.max), reads=[pk], writes=[t2_])
                okb, ovb = OUTB[0], OUTB[1]
                ctx.tt(okb[:], t1_[:], t2_[:], ALU.add,
                       reads=[t1_, t2_], writes=[okb])
                ctx.act(ovb[:], pv[:], AF.Copy, reads=[pv], writes=[ovb])
                s.dma("qsp", d["k"][trow:trow + 128, msl], okb[:],
                      reads=[okb], writes=[d["k"]])
                s.dma("qsp", d["v"][trow:trow + 128, msl], ovb[:],
                      reads=[ovb], writes=[d["v"]])

    KT = [ctx.sbuf(f"l1a_kt{i}", [128, 1024], BF16) for i in range(2)]
    VT = [ctx.sbuf(f"l1a_vt{i}", [128, 1024], BF16) for i in range(2)]
    pkv = [ctx.pb[0], ctx.pb[1]]
    pks = ctx.pb[2]
    for tc in range(32):
        kt, vt = KT[tc % 2], VT[tc % 2]
        s.dma("qsp", kt[:], d["k"][tc * 128:(tc + 1) * 128, :],
              reads=[d["k"]], writes=[kt])
        s.dma("qsp", vt[:], d["v"][tc * 128:(tc + 1) * 128, :],
              reads=[d["v"]], writes=[vt])
        for h in range(NH):
            hs = slice(h * 64, h * 64 + 64)
            po = pkv[h // 8][0:64, (h % 8) * 64:(h % 8) * 64 + 64]
            ctx.mm(po, kt[:, hs], vt[:, hs], tc == 0, tc == 31,
                   reads=[kt, vt], writes=[pkv[h // 8]])
            ctx.mm(pks[0:64, h:h + 1], kt[:, hs], ctx.ones_col_b[:],
                   tc == 0, tc == 31, reads=[kt, ctx.ones_col_b], writes=[pks])
    for h in range(NH):
        po = pkv[h // 8][0:64, (h % 8) * 64:(h % 8) * 64 + 64]
        ctx.act(KVS[:, h * 64:(h + 1) * 64], po, AF.Copy,
                reads=[pkv[h // 8]], writes=[KVS])
    ctx.act(KSC[:], pks[0:64, 0:16], AF.Copy, reads=[pks], writes=[KSC])

    QH = [ctx.sbuf(f"l1a_qh{i}", [64, 512], BF16) for i in range(2)]
    AH = [ctx.sbuf(f"l1a_ah{i}", [64, 512], BF16) for i in range(2)]
    for h in range(NH):
        for it in range(8):
            tsl = slice(it * 512, (it + 1) * 512)
            qh, ah = QH[it % 2], AH[it % 2]
            s.dma("qsp", qh[:], qr[h * 64:h * 64 + 64, tsl],
                  reads=[d["q"]], writes=[qh])
            pa, pz, pzb = ctx.pb[3], ctx.pb[4], ctx.pb[5]
            ctx.mm(pa[0:64, :], KVS[:, h * 64:(h + 1) * 64], qh[:], True, True,
                   reads=[KVS, qh], writes=[pa])
            ctx.mm(pz[0:1, :], KSC[:, h:h + 1], qh[:], True, True,
                   reads=[KSC, qh], writes=[pz])
            ctx.act(ZR[:], pz[0:1, :], AF.Copy, reads=[pz], writes=[ZR], bias=1e-6)
            s.op("dve", lambda e: e.reciprocal(ZR[:], ZR[:]),
                 reads=[ZR], writes=[ZR])
            ctx.mm(pzb[0:64, :], ctx.ones_row_f[0:1, 0:64], ZR[:], True, True,
                   reads=[ZR, ctx.ones_row_f], writes=[pzb])
            ctx.act(TMP[0][0:64, :], pa[0:64, :], AF.Copy,
                    reads=[pa], writes=[TMP[0]])
            s.op("dve", lambda e, ah=ah: e.tensor_tensor(
                ah[:], TMP[0][0:64, :], pzb[0:64, :], ALU.mult),
                reads=[TMP[0], pzb], writes=[ah])
            s.dma("qsp", d["a"][h * 64:h * 64 + 64, tsl], ah[:],
                  reads=[ah], writes=[d["a"]])

    for it in range(8):
        tsl = slice(it * 512, (it + 1) * 512)
        atile = XBT[it % 2]
        xn2 = ctx.xnt[it % 2]
        s.dma("qsp", atile[:], ar[:, :, tsl], reads=[d["a"]], writes=[atile])
        s.dma("qsp", xn2[:], xnr[:, :, tsl], reads=[d["xn"]], writes=[xn2])
        for oc in range(CH):
            po = ctx.pb[6 + oc % 2]
            for c in range(CH):
                ctx.mm(po[:], WO[:, c, oc * 128:(oc + 1) * 128], atile[:, c, :],
                       c == 0, c == CH - 1, reads=[WBUF, atile], writes=[po])
            tm = TMP[oc % 2]
            s.op("dve", lambda e, p=po, t=tm, oc=oc: e.tensor_scalar(
                t[:], p[:], bo_c(oc), None, ALU.add), reads=[po, COLS], writes=[tm])
            ob = OUTB[oc % 2]
            ctx.tt(ob[:], tm[:], xn2[:, oc, :], ALU.add,
                   reads=[tm, xn2], writes=[ob])
            s.dma("qsp", t1r[:, oc, tsl], ob[:], reads=[ob], writes=[d["t1"]])

    emit_ln(ctx, d["t1"], d["u"], src_dt=BF16, gcol=(COLS, 24), bcol=(COLS, 32),
            in_bufs=XBT)

    load_w("w1", "(c p) o -> p c o")
    W1V = WBUF[:, :, :].rearrange("p a o -> p (a o)").rearrange(
        "p (c o) -> p c o", o=4096)
    for it in range(8):
        tsl = slice(it * 512, (it + 1) * 512)
        ut = XBT[it % 2]
        s.dma("qsp", ut[:], ur[:, :, tsl], reads=[d["u"]], writes=[ut])
        for hc in range(32):
            ph = ctx.pb[hc % 2]
            for c in range(CH):
                ctx.mm(ph[:], W1V[:, c, hc * 128:(hc + 1) * 128], ut[:, c, :],
                       c == 0, c == CH - 1, reads=[WBUF, ut], writes=[ph])
            s.op("dve", lambda e, p=ph, hc=hc: e.tensor_scalar(
                HST[:, hc, :], p[:], B1C[:, hc:hc + 1], 0.0, ALU.add, ALU.max),
                reads=[ph, B1C], writes=[HST])
        s.dma("qsp", h1r[:, :, tsl], HST[:], reads=[HST], writes=[d["h1"]])

    load_w("w2", "(h p) o -> p h o")
    for it in range(8):
        tsl = slice(it * 512, (it + 1) * 512)
        ut = XBT[it % 2]
        s.dma("qsp", HST[:], h1r[:, :, tsl], reads=[d["h1"]], writes=[HST])
        s.dma("qsp", ut[:], ur[:, :, tsl], reads=[d["u"]], writes=[ut])
        for oc in range(CH):
            po = ctx.pb[2 + oc % 2]
            for hc in range(32):
                ctx.mm(po[:], WBUF[:, hc, oc * 128:(oc + 1) * 128], HST[:, hc, :],
                       hc == 0, hc == 31, reads=[WBUF, HST], writes=[po])
            tm = TMP[oc % 2]
            s.op("dve", lambda e, p=po, t=tm, oc=oc: e.tensor_scalar(
                t[:], p[:], b2_c(oc), None, ALU.add), reads=[po, COLS], writes=[tm])
            ob = OUTB[oc % 2]
            ctx.tt(ob[:], tm[:], ut[:, oc, :], ALU.add, reads=[tm, ut], writes=[ob])
            s.dma("qsp", v2r[:, oc, tsl], ob[:], reads=[ob], writes=[d["v2"]])

    emit_ln(ctx, d["v2"], d["s2"], src_dt=BF16, gcol=(COLS, 40), bcol=(COLS, 48),
            in_bufs=XBT)
    emit_resadd(ctx, d["x"], d["s2"], d["xo"])


def _build_l1a():
    nc = _mk_nc()
    d = {}
    d["x"] = nc.dram_tensor("x", [D, T], F32, kind="ExternalInput").ap()
    d["xo"] = nc.dram_tensor("xo", [D, T], F32, kind="ExternalOutput").ap()
    for nm, sh, dt in [("xn", [D, T], BF16), ("q", [D, T], BF16),
                       ("k", [T, D], BF16), ("v", [T, D], BF16),
                       ("a", [D, T], BF16), ("t1", [D, T], BF16),
                       ("u", [D, T], BF16), ("h1", [4096, T], BF16),
                       ("v2", [D, T], BF16), ("s2", [D, T], BF16)]:
        d[nm] = nc.dram_tensor(nm, sh, dt, kind="Internal").ap()
    for nm in ["wq", "wk", "wv", "wo"]:
        d[nm] = nc.dram_tensor(nm, [D, D], F32, kind="ExternalInput").ap()
    d["w1"] = nc.dram_tensor("w1", [D, 4096], F32, kind="ExternalInput").ap()
    d["w2"] = nc.dram_tensor("w2", [4096, D], F32, kind="ExternalInput").ap()
    d["cols"] = nc.dram_tensor("cols", [128, 64], F32, kind="ExternalInput").ap()
    d["cols2"] = nc.dram_tensor("cols2", [128, 32], F32, kind="ExternalInput").ap()
    d["brow"] = nc.dram_tensor("brow", [1, 2048], F32, kind="ExternalInput").ap()
    es = contextlib.ExitStack()
    with es:
        ctx = Ctx(nc, es)
        emit_l1a(ctx, d)
        ctx.s.build(es)
    return nc


def _col(v):
    v = np.asarray(v, np.float32)
    return np.ascontiguousarray(v.reshape(-1, 128).T)


def _prep_l1a(inp, p):
    cols = np.zeros((128, 64), np.float32)
    cols[:, 0:8] = _col(inp["L1_attn_bq"])
    cols[:, 8:16] = _col(inp["L1_attn_bo"])
    cols[:, 16:24] = _col(inp["L1_attn_b2"])
    cols[:, 24:32] = _col(inp["L1_attn_g1"])
    cols[:, 32:40] = _col(inp["L1_attn_beta1"])
    cols[:, 40:48] = _col(inp["L1_attn_g2"])
    cols[:, 48:56] = _col(inp["L1_attn_beta2"])
    p["l1a_cols"] = cols
    p["l1a_cols2"] = _col(inp["L1_attn_b1"])
    brow = np.zeros((1, 2048), np.float32)
    brow[0, 0:1024] = np.asarray(inp["L1_attn_bk"])
    brow[0, 1024:2048] = np.asarray(inp["L1_attn_bv"])
    p["l1a_brow"] = brow
    for nm, key in [("wq", "L1_attn_Wq"), ("wk", "L1_attn_Wk"),
                    ("wv", "L1_attn_Wv"), ("wo", "L1_attn_Wo"),
                    ("w1", "L1_attn_W1"), ("w2", "L1_attn_W2")]:
        p[f"l1a_{nm}"] = np.ascontiguousarray(np.asarray(inp[key], np.float32))
    return p


def run_l1a(xs, prep):
    n = len(xs)
    in_maps = [{"x": xs[i], "cols": prep["l1a_cols"], "cols2": prep["l1a_cols2"],
                "brow": prep["l1a_brow"],
                **{nm: prep[f"l1a_{nm}"] for nm in
                   ["wq", "wk", "wv", "wo", "w1", "w2"]}} for i in range(n)]
    res = _run_phase("l1a", _build_l1a, in_maps)
    return [np.asarray(res.results[i]["xo"]) for i in range(n)]


def emit_l3a(ctx, d):
    s = ctx.s
    emit_ln(ctx, d["x"], d["xn"])
    xnr = d["xn"].rearrange("(c p) t -> p c t", p=128)
    xr = d["x"].rearrange("(c p) t -> p c t", p=128)
    xor_ = d["xo"].rearrange("(c p) t -> p c t", p=128)

    WBUF = ctx.sbuf("l3a_wbuf", [128, 32, 1024], BF16)
    PJ = ctx.sbuf("l3a_pj", [64, 2048], BF16)
    QP = ctx.sbuf("l3a_qp", [128, 2, CH, 512], BF16)
    KP = ctx.sbuf("l3a_kp", [128, 32, 256], BF16)
    KVB = ctx.sbuf("l3a_kvb", [128, 128], BF16)
    KSB = ctx.sbuf("l3a_ksb", [128, 2], BF16)
    QF = [ctx.sbuf(f"l3a_qf{i}", [64, T], BF16) for i in range(2)]
    VT = [ctx.sbuf(f"l3a_vt{i}", [128, 128], BF16) for i in range(2)]
    TMP = [ctx.sbuf(f"l3a_tmp{i}", [128, 512], F32) for i in range(4)]
    OUTB = [ctx.sbuf(f"l3a_ob{i}", [128, 512], BF16) for i in range(2)]
    RZC = ctx.sbuf("l3a_rzc", [128, 64], F32)
    ATOK = ctx.sbuf("l3a_atok", [128, 64], BF16)
    EYEB3 = ctx.sbuf("l3a_eyeb", [128, 128], BF16)
    s.dma("qpool", EYEB3[:], d["eyeb"][:], writes=[EYEB3])
    AH = [ctx.sbuf(f"l3a_ah{i}", [64, 512], BF16) for i in range(2)]
    COLS = ctx.sbuf("l3a_cols", [128, 8], F32)

    s.dma("qpool", COLS[:], d["cols"][:], writes=[COLS])
    s.dma("qpool", PJ[:].rearrange("d (h f) -> d h f", f=128),
          d["proj"].rearrange("(h d) f -> d h f", d=64), writes=[PJ])
    wr = d["wqkvT"].rearrange("(c p) o -> p c o", p=128)
    for c in range(CH):
        s.dma("qpool", WBUF[:, 3 * c:3 * c + 3, :],
              wr[:, c, :].rearrange("p (a o) -> p a o", o=1024), writes=[WBUF])
    woT = d["woutT"].rearrange("(c p) o -> p c o", p=128)
    WOV = WBUF[:, 24:32, :]
    s.dma("qpool", WOV, woT[:], writes=[WBUF])

    def qkv_lhsT(c, col0, n):
        a, b = divmod(col0, 1024)
        return WBUF[:, 3 * c + a, b:b + n]

    def elu1_into(dst, psrc, reads_p):
        t1_, t2_ = TMP[0], TMP[1]
        a1, a2 = t1_[:], t2_[:]
        if dst.ndim == 3:
            a1 = a1.rearrange("p (a b) -> p a b", b=dst.shape[2])
            a2 = a2.rearrange("p (a b) -> p a b", b=dst.shape[2])
        s.op("dve", lambda e: e.tensor_scalar(t1_[:], psrc, 0.0, None, ALU.min),
             reads=reads_p, writes=[t1_])
        ctx.act(t1_[:], t1_[:], AF.Exp, reads=[t1_], writes=[t1_])
        s.op("dve", lambda e: e.tensor_scalar(t2_[:], psrc, 0.0, None, ALU.max),
             reads=reads_p, writes=[t2_])
        ctx.tt(dst, a1, a2, ALU.add, reads=[t1_, t2_], writes=[dst.tensor])

    for it in range(8):
        tsl = slice(it * 512, (it + 1) * 512)
        xn = ctx.xnt[it % 2]
        s.dma("qsp", xn[:], xnr[:, :, tsl], reads=[d["xn"]], writes=[xn])
        for which, dst in [(0, d["q"]), (1, d["k"])]:
            for oc in range(CH):
                pg = ctx.pb[oc % 2]
                for c in range(CH):
                    ctx.mm(pg[:], qkv_lhsT(c, which * 1024 + oc * 128, 128),
                           xn[:, c, :], c == 0, c == CH - 1,
                           reads=[WBUF, xn], writes=[pg])
                ob = OUTB[oc % 2]
                ctx.act(ob[:], pg[:], AF.Copy, reads=[pg], writes=[ob])
                s.dma("qsp", dst[oc * 128:(oc + 1) * 128, tsl], ob[:],
                      reads=[ob], writes=[dst])
        for st_ in range(4):
            ssl = slice(st_ * 128, st_ * 128 + 128)
            trow = it * 512 + st_ * 128
            for mh in range(2):
                pv = ctx.pb[4 + mh]
                for c in range(CH):
                    ctx.mm(pv[:], xn[:, c, ssl],
                           qkv_lhsT(c, 2048 + mh * 512, 512), c == 0, c == CH - 1,
                           reads=[xn, WBUF], writes=[pv])
                ob = OUTB[mh]
                ctx.act(ob[:], pv[:], AF.Copy, reads=[pv], writes=[ob])
                s.dma("qsp", d["v"][trow:trow + 128, mh * 512:(mh + 1) * 512],
                      ob[:], reads=[ob], writes=[d["v"]])

    for hg in range(8):
        for h2 in range(2):
            h = 2 * hg + h2
            qf = QF[0]
            s.dma("qsp", qf[:], d["q"][h * 64:h * 64 + 64, :],
                  reads=[d["q"]], writes=[qf])
            pj = PJ[0:64, h * 128:(h + 1) * 128]
            for it in range(8):
                tsl = slice(it * 512, (it + 1) * 512)
                pg = ctx.pb[it % 2]
                ctx.mm(pg[:], pj, qf[:, tsl], True, True,
                       reads=[PJ, qf], writes=[pg])
                elu1_into(QP[:, h2, it, :], pg[:], [pg])
            kf = QF[1]
            s.dma("qsp", kf[:], d["k"][h * 64:h * 64 + 64, :],
                  reads=[d["k"]], writes=[kf])
            for tg in range(8):
                pg = ctx.pb[2 + tg % 2]
                for j in range(4):
                    tc = 4 * tg + j
                    ctx.mm(pg[:, j * 128:(j + 1) * 128], kf[:, tc * 128:
                           (tc + 1) * 128], pj, True, True,
                           reads=[PJ, kf], writes=[pg])
                elu1_into(KP[:, 4 * tg:4 * tg + 4, h2 * 128:(h2 + 1) * 128], pg[:],
                          [pg])
        pkv, pks = ctx.pb[4], ctx.pb[5]
        for tc in range(32):
            vt = VT[tc % 2]
            s.dma("qsp", vt[:], d["v"][tc * 128:(tc + 1) * 128,
                  hg * 128:(hg + 1) * 128], reads=[d["v"]], writes=[vt])
            for h2 in range(2):
                ctx.mm(pkv[:, h2 * 64:(h2 + 1) * 64],
                       KP[:, tc, h2 * 128:(h2 + 1) * 128],
                       vt[:, h2 * 64:(h2 + 1) * 64], tc == 0, tc == 31,
                       reads=[KP, vt], writes=[pkv])
                ctx.mm(pks[:, h2:h2 + 1], KP[:, tc, h2 * 128:(h2 + 1) * 128],
                       ctx.ones_col_b[:], tc == 0, tc == 31,
                       reads=[KP, ctx.ones_col_b], writes=[pks])
        ctx.act(KVB[:], pkv[:, 0:128], AF.Copy, reads=[pkv], writes=[KVB])
        ctx.act(KSB[:], pks[:, 0:2], AF.Copy, reads=[pks], writes=[KSB])
        pzc = ctx.pb[6]
        for h2 in range(2):
            for it in range(8):
                for j in range(4):
                    idx = h2 * 32 + it * 4 + j
                    ctx.mm(pzc[:, idx:idx + 1],
                           QP[:, h2, it, j * 128:(j + 1) * 128],
                           KSB[:, h2:h2 + 1], True, True,
                           reads=[QP, KSB], writes=[pzc])
        ctx.act(RZC[:], pzc[:, 0:64], AF.Copy, reads=[pzc], writes=[RZC],
                bias=1e-8)
        s.op("dve", lambda e: e.reciprocal(RZC[:], RZC[:]), reads=[RZC],
             writes=[RZC])
        for h2 in range(2):
            h = 2 * hg + h2
            for it in range(8):
                tsl = slice(it * 512, (it + 1) * 512)
                ah = AH[it % 2]
                for j in range(4):
                    idx = h2 * 32 + it * 4 + j
                    pa = ctx.pb[2 + j % 2]
                    ctx.mm(pa[:, 0:64], QP[:, h2, it, j * 128:(j + 1) * 128],
                           KVB[:, h2 * 64:(h2 + 1) * 64], True, True,
                           reads=[QP, KVB], writes=[pa])
                    s.op("dve", lambda e, pa=pa, idx=idx: e.tensor_scalar(
                        ATOK[:], pa[:, 0:64], RZC[:, idx:idx + 1], None,
                        ALU.mult), reads=[pa, RZC], writes=[ATOK])
                    ptr = ctx.pb[j % 2]
                    pv = ptr[:, 0:64].bitcast(BF16)
                    s.op("pe", lambda e, pv=pv: e.transpose(
                        pv[0:64, 0:128], ATOK[:], EYEB3[:]),
                        reads=[ATOK, EYEB3], writes=[ptr])
                    ctx.act(ah[:, j * 128:(j + 1) * 128], pv[0:64, 0:128],
                            AF.Copy, reads=[ptr], writes=[(ah, j)])
                s.dma("qsp", d["a"][h * 64:h * 64 + 64, tsl], ah[:],
                      reads=[(ah, j_) for j_ in range(4)], writes=[d["a"]])

    ar = d["a"].rearrange("(c p) t -> p c t", p=128)
    for it in range(8):
        tsl = slice(it * 512, (it + 1) * 512)
        atile = ctx.xnt[it % 2]
        xt = ctx.xtf[it % 2]
        s.dma("qsp", atile[:], ar[:, :, tsl], reads=[d["a"]], writes=[atile])
        s.dma("qsp", xt[:], xr[:, :, tsl], reads=[d["x"]], writes=[xt])
        for oc in range(CH):
            po = ctx.pb[2 + oc % 2]
            for c in range(CH):
                ctx.mm(po[:], WBUF[:, 24 + c, oc * 128:(oc + 1) * 128],
                       atile[:, c, :], c == 0, c == CH - 1,
                       reads=[WBUF, atile], writes=[po])
            tm = TMP[oc % 2]
            s.op("dve", lambda e, p=po, t=tm, oc=oc: e.tensor_scalar(
                t[:], p[:], COLS[:, oc:oc + 1], None, ALU.add),
                reads=[po, COLS], writes=[tm])
            s.op("dve", lambda e, t=tm, x=xt, oc=oc: e.tensor_tensor(
                t[:], t[:], x[:, oc, :], ALU.add), reads=[tm, xt], writes=[tm])
            s.dma("qsp", xor_[:, oc, tsl], tm[:], reads=[tm], writes=[d["xo"]])


def _build_l3a():
    nc = _mk_nc()
    d = {}
    d["x"] = nc.dram_tensor("x", [D, T], F32, kind="ExternalInput").ap()
    d["xo"] = nc.dram_tensor("xo", [D, T], F32, kind="ExternalOutput").ap()
    for nm, sh in [("xn", [D, T]), ("q", [D, T]), ("k", [D, T]),
                   ("v", [T, D]), ("a", [D, T])]:
        d[nm] = nc.dram_tensor(nm, sh, BF16, kind="Internal").ap()
    d["wqkvT"] = nc.dram_tensor("wqkvT", [D, 3072], F32, kind="ExternalInput").ap()
    d["woutT"] = nc.dram_tensor("woutT", [D, D], F32, kind="ExternalInput").ap()
    d["proj"] = nc.dram_tensor("proj", [D, 128], F32, kind="ExternalInput").ap()
    d["cols"] = nc.dram_tensor("cols", [128, 8], F32, kind="ExternalInput").ap()
    d["eyeb"] = nc.dram_tensor("eyeb", [128, 128], BF16,
                               kind="ExternalInput").ap()
    es = contextlib.ExitStack()
    with es:
        ctx = Ctx(nc, es)
        emit_l3a(ctx, d)
        ctx.s.build(es)
    return nc


def _prep_l3a(inp, p):
    p["l3a_wqkvT"] = np.ascontiguousarray(np.asarray(inp["L3_attn_W_qkv"],
                                                     np.float32).T)
    p["l3a_woutT"] = np.ascontiguousarray(np.asarray(inp["L3_attn_W_out"],
                                                     np.float32).T)
    p["l3a_proj"] = np.ascontiguousarray(np.asarray(inp["L3_attn_proj"],
                                                    np.float32).reshape(1024, 128))
    p["l3a_cols"] = _col(inp["L3_attn_b_out"])
    p["l3a_eyeb"] = np.eye(128, dtype=BF)
    return p


def run_l3a(xs, prep):
    n = len(xs)
    in_maps = [{"x": xs[i], "wqkvT": prep["l3a_wqkvT"], "woutT": prep["l3a_woutT"],
                "proj": prep["l3a_proj"], "cols": prep["l3a_cols"],
                "eyeb": prep["l3a_eyeb"]}
               for i in range(n)]
    res = _run_phase("l3a", _build_l3a, in_maps)
    return [np.asarray(res.results[i]["xo"]) for i in range(n)]


def emit_l0a(ctx, d):
    s = ctx.s
    NH = 16
    emit_ln(ctx, d["x"], d["xn"])
    xnr = d["xn"].rearrange("(c p) t -> p c t", p=128)

    WT3 = ctx.sbuf("l0a_wt3", [128, CH, 3072], BF16)
    wr = d["wT"].rearrange("(c p) o -> p c o", p=128)
    for g in range(4):
        s.dma("qpool", WT3[:, :, g * 768:(g + 1) * 768],
              wr[:, :, g * 768:(g + 1) * 768], writes=[WT3])
    QKC = ctx.sbuf("l0a_qkc", [128, 16], F32)
    VROW = ctx.sbuf("l0a_vrow", [1, 1024], BF16)
    s.dma("qpool", QKC[:], d["qkcols"][:], writes=[QKC])
    s.dma("qpool", VROW[:], d["vrow"][:], writes=[VROW])
    MSK = ctx.sbuf("l0a_msk", [128, 4, 512], BF16)
    s.dma("qsp", MSK[:], d["mask"].rearrange("(a p) q -> p a q", p=128),
          writes=[MSK])
    ONF = ctx.sbuf("l0a_onf", [128, 64], F32)
    s.op("pool", lambda e: e.memset(ONF[:], 1.0), writes=[ONF])
    VRES = ctx.sbuf("l0a_vres", [128, 32, 1040], BF16)
    s.op("pool", lambda e: e.memset(VRES[:], 1.0), writes=[VRES])
    NRM = ctx.sbuf("l0a_nrm", [1, 2], F32)
    s.op("pool", lambda e: e.memset(NRM[:], 0.0), writes=[NRM])
    BIASC = ctx.sbuf("l0a_biasc", [128, 1], F32)
    R2 = ctx.sbuf("l0a_r2", [1, 2], F32)
    TMP0 = ctx.sbuf("l0a_tmp0", [128, 512], F32)
    TMP = [TMP0, TMP0]
    SQB = ctx.sbuf("l0a_sqb", [128, 512], BF16)
    OUTB = [ctx.sbuf(f"l0a_ob{i}", [128, 512], BF16) for i in range(2)]
    YH = [ctx.sbuf(f"l0a_yh{i}", [64, 512], BF16) for i in range(2)]

    for it in range(8):
        tsl = slice(it * 512, (it + 1) * 512)
        xn = ctx.xnt[it % 2]
        s.dma("qsp", xn[:], xnr[:, :, tsl], reads=[d["xn"]], writes=[xn])
        for wp, dst in ((0, d["q"]), (1, d["k"])):
            for oc in range(CH):
                pg = ctx.pb[oc % 2]
                for c in range(CH):
                    ctx.mm(pg[:], WT3[:, c, wp * 1024 + oc * 128:
                           wp * 1024 + (oc + 1) * 128],
                           xn[:, c, :], c == 0, c == CH - 1,
                           reads=[WT3, xn], writes=[pg])
                ob = OUTB[oc % 2]
                s.op("dve", lambda e, p=pg, o=ob, oc=oc, w=wp: e.tensor_scalar(
                    o[:], p[:], QKC[:, 8 * w + oc:8 * w + oc + 1], None,
                    ALU.add), reads=[pg, QKC], writes=[ob])
                s.dma("qsp", dst[oc * 128:(oc + 1) * 128, tsl], ob[:],
                      reads=[ob], writes=[dst])
                ctx.act(SQB[:], ob[:], AF.Square, reads=[ob], writes=[SQB])
                pn = ctx.pb[2]
                ctx.mm(pn[0:1, :], ctx.ones_col_b[:], SQB[:], True, True,
                       reads=[SQB, ctx.ones_col_b], writes=[pn])
                s.op("dve", lambda e, p=pn: e.reduce_max(
                    R2[0:1, 0:1], p[0:1, :], axis=AX.X),
                    reads=[pn], writes=[R2])
                ctx.tt(NRM[0:1, wp:wp + 1], NRM[0:1, wp:wp + 1],
                       R2[0:1, 0:1], ALU.max, reads=[NRM, R2], writes=[NRM])
        for st_ in range(4):
            ssl = slice(st_ * 128, st_ * 128 + 128)
            tc = it * 4 + st_
            for mh in range(2):
                pv = ctx.pb[4 + mh]
                for c in range(CH):
                    ctx.mm(pv[:], xn[:, c, ssl],
                           WT3[:, c, 2048 + mh * 512:2048 + (mh + 1) * 512],
                           c == 0, False, reads=[xn, WT3], writes=[pv])
                ctx.mm(pv[:], ctx.ones_row_b[:],
                       VROW[0:1, mh * 512:(mh + 1) * 512], False, True,
                       reads=[ctx.ones_row_b, VROW], writes=[pv])
                dd = VRES[:, tc, :].rearrange("p (h m) -> p h m", m=65)[
                    :, mh * 8:mh * 8 + 8, 0:64]
                ctx.act(dd, pv[:].rearrange("p (h m) -> p h m", m=64),
                        AF.Copy, reads=[pv], writes=[(VRES, tc, mh)])

    ctx.tt(R2[0:1, 0:1], NRM[0:1, 0:1], NRM[0:1, 1:2],
           ALU.mult, reads=[NRM], writes=[R2])
    ctx.act(R2[0:1, 0:1], R2[0:1, 0:1], AF.Sqrt, reads=[R2], writes=[R2],
            bias=0.0)
    s.op("dve", lambda e: e.tensor_scalar(
        R2[0:1, 0:1], R2[0:1, 0:1], -0.125, None, ALU.mult),
        reads=[R2], writes=[R2])
    pn = ctx.pb[2]
    ctx.mm(pn[:, 0:1], ctx.ones_row_f[:], R2[0:1, 0:1], True, True,
           reads=[R2, ctx.ones_row_f], writes=[pn])
    ctx.act(BIASC[:], pn[:, 0:1], AF.Copy, reads=[pn], writes=[BIASC])

    def pts(kt, n=1):
        buf = ctx.xtf[kt // 16]
        v = buf[:, :, :].rearrange("p c t -> p (c t)").bitcast(BF16)
        return v.rearrange("p (k q) -> p k q", q=512)[:, kt % 16:kt % 16 + n, :]

    QF = ctx.sbuf("l0a_qf", [64, T], BF16)
    KF = ctx.sbuf("l0a_kf", [64, T], BF16)
    for h in range(NH):
        s.dma("qsp", QF[:], d["q"][h * 64:h * 64 + 64, :],
              reads=[d["q"]], writes=[QF])
        s.dma("qsp", KF[:], d["k"][h * 64:h * 64 + 64, :],
              reads=[d["k"]], writes=[KF])
        for qg in range(8):
            qsl = slice(qg * 512, (qg + 1) * 512)
            nkt = 4 * (qg + 1)
            pa, pzb = ctx.pb[0], ctx.pb[1]

            def emit_qk_exp(kt2, pi):
                if pi % 2 == 0:
                    for j in range(2):
                        kt = kt2 + j
                        ctx.mm(ctx.pbw[:, j * 512:(j + 1) * 512],
                               KF[:, kt * 128:(kt + 1) * 128], QF[:, qsl],
                               True, True, reads=[KF, QF],
                               writes=[(ctx.pbw, j)])
                    ctx.act(pts(kt2, 2),
                            ctx.pbw[:].rearrange("p (a q) -> p a q", q=512),
                            AF.Exp,
                            reads=[(ctx.pbw, 0), (ctx.pbw, 1), BIASC],
                            writes=[(ctx.xtf[kt2 // 16], kt2),
                                    (ctx.xtf[kt2 // 16], kt2 + 1)],
                            bias=BIASC[:, 0:1], scale=0.125)
                else:
                    for j in range(2):
                        kt = kt2 + j
                        pq = ctx.pb[2 + j]
                        ctx.mm(pq[:], KF[:, kt * 128:(kt + 1) * 128],
                               QF[:, qsl], True, True, reads=[KF, QF],
                               writes=[pq])
                        ctx.act(pts(kt)[:, 0, :], pq[:], AF.Exp,
                                reads=[pq, BIASC],
                                writes=[(ctx.xtf[kt // 16], kt)],
                                bias=BIASC[:, 0:1], scale=0.125)

            def emit_av(kt2):
                for j in range(2):
                    kt = kt2 + j
                    pt = pts(kt)[:, 0, :]
                    if kt >= 4 * qg:
                        jj = kt - 4 * qg
                        ctx.tt(pt, pt, MSK[:, jj, :], ALU.mult,
                               reads=[(ctx.xtf[kt // 16], kt), MSK],
                               writes=[(ctx.xtf[kt // 16], kt)])
                    ctx.mm(pa[0:65, :], VRES[:, kt, h * 65:h * 65 + 65], pt,
                           kt == 0, kt == nkt - 1,
                           reads=[(ctx.xtf[kt // 16], kt), VRES], writes=[pa])

            prev = None
            for pi, kt2 in enumerate(range(0, nkt, 2)):
                emit_qk_exp(kt2, pi)
                if prev is not None:
                    emit_av(prev)
                prev = kt2
            emit_av(prev)
            ctx.act(TMP[0][64:65, :], pa[64:65, :], AF.Copy,
                    reads=[pa], writes=[TMP[0]])
            s.op("dve", lambda e: e.reciprocal(TMP[0][64:65, :],
                 TMP[0][64:65, :]), reads=[TMP[0]], writes=[TMP[0]])
            ctx.mm(pzb[0:64, :], ONF[64:65, 0:64], TMP[0][64:65, :], True, True,
                   reads=[TMP[0], ONF], writes=[pzb])
            ctx.act(TMP[1][0:64, :], pa[0:64, :], AF.Copy, reads=[pa],
                    writes=[TMP[1]])
            yh = YH[qg % 2]
            s.op("dve", lambda e, yh=yh: e.tensor_tensor(
                yh[:], TMP[1][0:64, :], pzb[0:64, :], ALU.mult),
                reads=[TMP[1], pzb], writes=[yh])
            s.dma("qsp", d["y"][h * 64:h * 64 + 64, qsl], yh[:],
                  reads=[yh], writes=[d["y"]])

    emit_resadd(ctx, d["x"], d["y"], d["xo"])


def _build_l0a():
    nc = _mk_nc()
    d = {}
    d["x"] = nc.dram_tensor("x", [D, T], F32, kind="ExternalInput").ap()
    d["xo"] = nc.dram_tensor("xo", [D, T], F32, kind="ExternalOutput").ap()
    for nm, sh in [("xn", [D, T]), ("q", [D, T]), ("k", [D, T]), ("y", [D, T])]:
        d[nm] = nc.dram_tensor(nm, sh, BF16, kind="Internal").ap()
    d["wT"] = nc.dram_tensor("wT", [D, 3072], F32, kind="ExternalInput").ap()
    d["qkcols"] = nc.dram_tensor("qkcols", [128, 16], F32,
                                 kind="ExternalInput").ap()
    d["vrow"] = nc.dram_tensor("vrow", [1, 1024], F32, kind="ExternalInput").ap()
    d["mask"] = nc.dram_tensor("mask", [512, 512], BF16,
                               kind="ExternalInput").ap()
    es = contextlib.ExitStack()
    with es:
        ctx = Ctx(nc, es)
        emit_l0a(ctx, d)
        ctx.s.build(es)
    return nc


def _prep_l0a(inp, p):
    p["l0a_wT"] = np.ascontiguousarray(np.asarray(inp["L0_attn_W"], np.float32).T)
    b = np.asarray(inp["L0_attn_b"], np.float32)
    qk = np.zeros((128, 16), np.float32)
    qk[:, 0:8] = _col(b[0:1024])
    qk[:, 8:16] = _col(b[1024:2048])
    p["l0a_qkcols"] = qk
    p["l0a_vrow"] = np.ascontiguousarray(b[2048:3072].reshape(1, 1024))
    m = np.zeros((4, 128, 512), np.float32)
    for jj in range(4):
        for i_ in range(128):
            m[jj, i_, 128 * jj + i_:] = 1.0
    p["l0a_mask"] = m.reshape(512, 512).astype(BF)
    return p


def run_l0a(xs, prep):
    n = len(xs)
    in_maps = [{"x": xs[i], "wT": prep["l0a_wT"], "qkcols": prep["l0a_qkcols"],
                "vrow": prep["l0a_vrow"], "mask": prep["l0a_mask"]}
               for i in range(n)]
    res = _run_phase("l0a", _build_l0a, in_maps)
    return [np.asarray(res.results[i]["xo"]) for i in range(n)]


def emit_l3f(ctx, d):
    s = ctx.s
    NE = 32
    emit_ln(ctx, d["x"], d["xn"])
    xnr = d["xn"].rearrange("(c p) t -> p c t", p=128)
    xr = d["x"].rearrange("(c p) t -> p c t", p=128)
    w1r = d["w1"].rearrange("e (c p) h -> p e c h", p=128)
    w2r = d["w2"].rearrange("e (hc p) o -> p e hc o", p=128)

    RWT = ctx.sbuf("l3f_rwt", [128, CH, NE], BF16)
    s.dma("qpool", RWT[:], d["rwT"].rearrange("(c p) e -> p c e", p=128),
          writes=[RWT])
    RBR = ctx.sbuf("l3f_rbr", [1, NE], BF16)
    s.dma("qpool", RBR[:], d["rbr"][:], writes=[RBR])
    B1A = ctx.sbuf("l3f_b1a", [128, 64], F32)
    s.dma("qpool", B1A[:], d["b1a"][:], writes=[B1A])
    EYEF = ctx.sbuf("l3f_eyef", [128, 128], F32)
    s.dma("qsp", EYEF[:], d["eyef"][:], writes=[EYEF])
    G = ctx.sbuf("l3f_g", [128, 32, NE], F32)
    LG = ctx.sbuf("l3f_lg", [128, NE], F32)
    MX = ctx.sbuf("l3f_mx", [128, 1], F32)
    XNH = ctx.sbuf("l3f_xnh", [128, CH, 2048], BF16)
    ACC = ctx.sbuf("l3f_acc", [128, 16, 1024], BF16)
    HB = ctx.sbuf("l3f_hb", [128, 2, 2048], BF16)
    W1E = [ctx.sbuf(f"l3f_w1e{i}", [128, CH, 256], BF16) for i in range(2)]
    W2E = [ctx.sbuf(f"l3f_w2e{i}", [128, 2, 1024], BF16) for i in range(2)]
    B2R = [ctx.sbuf(f"l3f_b2r{i}", [1, 1024], BF16) for i in range(2)]
    TMPB = ctx.sbuf("l3f_tmpb", [128, 1024], BF16)
    B2T = [ctx.sbuf(f"l3f_b2t{i}", [128, 1024], F32) for i in range(2)]
    TMPF = [ctx.sbuf(f"l3f_tmpf{i}", [128, 1024], F32) for i in range(2)]
    XTOK = ctx.sbuf("l3f_xtok", [128, 1024], F32)
    OUTF = [ctx.sbuf(f"l3f_outf{i}", [128, 1024], F32) for i in range(2)]

    for it in range(8):
        tsl = slice(it * 512, (it + 1) * 512)
        xn = ctx.xnt[it % 2]
        s.dma("qsp", xn[:], xnr[:, :, tsl], reads=[d["xn"]], writes=[xn])
        for st_ in range(4):
            tc = it * 4 + st_
            ssl = slice(st_ * 128, st_ * 128 + 128)
            pl = ctx.pb[st_ % 2]
            for c in range(CH):
                ctx.mm(pl[:, 0:NE], xn[:, c, ssl], RWT[:, c, :], c == 0, False,
                       reads=[xn, RWT], writes=[pl])
            ctx.mm(pl[:, 0:NE], ctx.ones_row_b[:], RBR[:], False, True,
                   reads=[ctx.ones_row_b, RBR], writes=[pl])
            ctx.act(LG[:], pl[:, 0:NE], AF.Copy, reads=[pl], writes=[LG])
            s.op("dve", lambda e: e.reduce_max(MX[:], LG[:], axis=AX.X),
                 reads=[LG], writes=[MX])
            s.op("dve", lambda e, tc=tc: e.tensor_scalar(
                G[:, tc, :], LG[:], MX[:, 0:1], None, ALU.is_ge),
                reads=[LG], strict=[MX], writes=[(G, tc)])
            s.op("dve", lambda e, tc=tc: e.tensor_scalar(
                G[:, tc, :], G[:, tc, :], 1.0, 1e30, ALU.subtract, ALU.mult),
                reads=[(G, tc)], writes=[(G, tc)])

    for th in range(2):
        hsl = slice(th * 2048, (th + 1) * 2048)
        s.dma("qsp", XNH[:], xnr[:, :, hsl], reads=[d["xn"]], writes=[XNH])
        for e in range(NE):
            w1, w2, b2 = W1E[e % 2], W2E[e % 2], B2R[e % 2]
            s.dma("qpool", w1[:], w1r[:, e, :, :], writes=[w1])
            s.dma("qpool", w2[:], w2r[:, e, :, :], writes=[w2])
            s.dma("qpool", b2[:], d["b2a"][e:e + 1, :], writes=[b2])
            b2t = B2T[e % 2]
            for oh in range(2):
                osl = slice(oh * 512, (oh + 1) * 512)
                pg = ctx.pb[oh]
                ctx.mm(pg[:], ctx.ones_row_b[:], b2[0:1, osl], True, True,
                       reads=[ctx.ones_row_b, b2], writes=[pg])
                ctx.act(b2t[:, osl], pg[:], AF.Copy, reads=[pg],
                        writes=[(b2t, oh)])
            for hc in range(2):
                for tq in range(4):
                    pg = ctx.pb[2 + tq % 2]
                    qsl = slice(tq * 512, (tq + 1) * 512)
                    for c in range(CH):
                        ctx.mm(pg[:], w1[:, c, hc * 128:(hc + 1) * 128],
                               XNH[:, c, qsl], c == 0, c == CH - 1,
                               reads=[w1, XNH], writes=[pg])
                    ctx.act(HB[:, hc, qsl], pg[:], AF.Relu, reads=[pg, B1A],
                            writes=[(HB, hc, tq)],
                            bias=B1A[:, 2 * e + hc:2 * e + hc + 1])
            for ts2 in range(16):
                ssl = slice(ts2 * 128, (ts2 + 1) * 128)
                for oh in range(2):
                    osl = slice(oh * 512, (oh + 1) * 512)
                    for hc in range(2):
                        ctx.mm(ctx.pbw[:, osl], HB[:, hc, ssl], w2[:, hc, osl],
                               hc == 0, hc == 1,
                               reads=[(HB, hc, ts2 // 4), w2],
                               writes=[(ctx.pbw, oh)])
                gcol = G[:, (th * 2048 + ts2 * 128) // 128, e:e + 1]
                tmpf = TMPF[ts2 % 2]
                ctx.tt(tmpf[:], ctx.pbw[:], b2t[:], ALU.add,
                       reads=[(ctx.pbw, 0), (ctx.pbw, 1), (b2t, 0), (b2t, 1)],
                       writes=[tmpf])
                if e == 0:
                    ctx.act(ACC[:, ts2, :], tmpf[:], AF.Relu,
                            reads=[tmpf, G], writes=[(ACC, ts2)], bias=gcol)
                else:
                    ctx.act(TMPB[:], tmpf[:], AF.Relu,
                            reads=[tmpf, G], writes=[TMPB], bias=gcol)
                    ctx.tt(ACC[:, ts2, :], ACC[:, ts2, :], TMPB[:], ALU.add,
                           reads=[(ACC, ts2), TMPB], writes=[(ACC, ts2)])
        for tt4 in range(4):
            xt = ctx.xtf[tt4 % 2]
            csl = slice(th * 2048 + tt4 * 512, th * 2048 + (tt4 + 1) * 512)
            s.dma("qsp", xt[:], xr[:, :, csl], reads=[d["x"]], writes=[xt])
            for j in range(4):
                ts2 = tt4 * 4 + j
                for c in range(CH):
                    ptr = ctx.pb[4 + c % 2]
                    ctx.s.op("pe", lambda e, p=ptr, c=c, xt=xt, j=j: e.transpose(
                        p[:, 0:128], xt[:, c, j * 128:(j + 1) * 128], EYEF[:]),
                        reads=[xt, EYEF], writes=[ptr])
                    ctx.act(XTOK[:, c * 128:(c + 1) * 128],
                            ptr[:, 0:128], AF.Copy,
                            reads=[ptr], writes=[(XTOK, c)])
                of = OUTF[j % 2]
                s.op("dve", lambda e, of=of, ts2=ts2: e.tensor_tensor(
                    of[:], ACC[:, ts2, :], XTOK[:], ALU.add),
                    reads=[(ACC, ts2)] + [(XTOK, c) for c in range(CH)],
                    writes=[of])
                trow = th * 2048 + ts2 * 128
                s.dma("qsp", d["xo"][trow:trow + 128, :], of[:],
                      reads=[of], writes=[d["xo"]])


def _build_l3f():
    nc = _mk_nc()
    d = {}
    d["x"] = nc.dram_tensor("x", [D, T], F32, kind="ExternalInput").ap()
    d["xo"] = nc.dram_tensor("xo", [T, D], F32, kind="ExternalOutput").ap()
    d["xn"] = nc.dram_tensor("xn", [D, T], BF16, kind="Internal").ap()
    d["w1"] = nc.dram_tensor("w1", [32, D, 256], F32, kind="ExternalInput").ap()
    d["w2"] = nc.dram_tensor("w2", [32, 256, D], F32, kind="ExternalInput").ap()
    d["rwT"] = nc.dram_tensor("rwT", [D, 32], F32, kind="ExternalInput").ap()
    d["rbr"] = nc.dram_tensor("rbr", [1, 32], F32, kind="ExternalInput").ap()
    d["b1a"] = nc.dram_tensor("b1a", [128, 64], F32, kind="ExternalInput").ap()
    d["b2a"] = nc.dram_tensor("b2a", [32, D], F32, kind="ExternalInput").ap()
    d["eyef"] = nc.dram_tensor("eyef", [128, 128], F32, kind="ExternalInput").ap()
    es = contextlib.ExitStack()
    with es:
        ctx = Ctx(nc, es)
        emit_l3f(ctx, d)
        ctx.s.build(es)
    return nc


def _prep_l3f(inp, p):
    p["l3f_w1"] = np.ascontiguousarray(np.asarray(inp["L3_ffn_weights1"],
                                                  np.float32))
    p["l3f_w2"] = np.ascontiguousarray(np.asarray(inp["L3_ffn_weights2"],
                                                  np.float32))
    p["l3f_rwT"] = np.ascontiguousarray(np.asarray(inp["L3_ffn_router_w"],
                                                   np.float32).T)
    p["l3f_rbr"] = np.asarray(inp["L3_ffn_router_b"],
                              np.float32).reshape(1, 32).copy()
    b1 = np.asarray(inp["L3_ffn_bias1"], np.float32)
    b1a = np.zeros((128, 64), np.float32)
    for e in range(32):
        b1a[:, 2 * e:2 * e + 2] = b1[e].reshape(2, 128).T
    p["l3f_b1a"] = b1a
    p["l3f_b2a"] = np.ascontiguousarray(np.asarray(inp["L3_ffn_bias2"],
                                                   np.float32))
    p["l3f_eyef"] = np.eye(128, dtype=np.float32)
    return p


def run_l3f(xs, prep):
    n = len(xs)
    in_maps = [{"x": xs[i], **{k: prep[f"l3f_{k}"] for k in
                ["w1", "w2", "rwT", "rbr", "b1a", "b2a", "eyef"]}}
               for i in range(n)]
    res = _run_phase("l3f", _build_l3f, in_maps)
    return [np.asarray(res.results[i]["xo"]) for i in range(n)]


def emit_l1f(ctx, d):
    s = ctx.s
    NE = 8
    emit_ln(ctx, d["x"], d["xn"])
    xnr = d["xn"].rearrange("(c p) t -> p c t", p=128)
    xr = d["x"].rearrange("(c p) t -> p c t", p=128)
    xor_ = d["xo"].rearrange("(c p) t -> p c t", p=128)
    hr = d["hs"].rearrange("(hc p) (e t) -> p hc e t", p=128, e=9)

    XN = ctx.sbuf("l1f_xn", [128, CH, T], BF16)
    for g in range(2):
        s.dma("qsp", XN[:, 4 * g:4 * g + 4, :], xnr[:, 4 * g:4 * g + 4, :],
              reads=[d["xn"]], writes=[XN])
    LNS = ctx.sbuf("l1f_lns", [128, 16], F32)
    s.dma("qpool", LNS[:], d["lns"][:], writes=[LNS])
    GWT = ctx.sbuf("l1f_gwt", [128, CH, NE], F32)
    s.dma("qpool", GWT[:], d["gwT"].rearrange("(c p) e -> p c e", p=128),
          writes=[GWT])
    EYEF = ctx.sbuf("l1f_eyef", [128, 128], F32)
    s.dma("qpool", EYEF[:], d["eyef"][:], writes=[EYEF])
    UNITS = ctx.sbuf("l1f_units", [8, 1024], F32)
    s.dma("qpool", UNITS[:], d["units"][:], writes=[UNITS])
    SGW = ctx.sbuf("l1f_sgw", [128, CH, NE], BF16)
    BGW = ctx.sbuf("l1f_bgw", [128, CH, NE], BF16)
    for c in range(CH):
        s.op("dve", lambda e, c=c: e.tensor_scalar(
            SGW[:, c, :], GWT[:, c, :], LNS[:, c:c + 1], None, ALU.mult),
            reads=[GWT, LNS], writes=[(SGW, c)])
        s.op("dve", lambda e, c=c: e.tensor_scalar(
            BGW[:, c, :], GWT[:, c, :], LNS[:, 8 + c:9 + c], None, ALU.mult),
            reads=[GWT, LNS], writes=[(BGW, c)])
    pS = ctx.pb[0]
    pC = ctx.pb[1]
    for c in range(CH):
        ctx.mm(pS[0:1, 0:NE], ctx.ones_col_b[:], SGW[:, c, :], c == 0,
               c == CH - 1, reads=[(SGW, c), ctx.ones_col_b], writes=[pS])
        ctx.mm(pC[0:1, 0:NE], ctx.ones_col_b[:], BGW[:, c, :], c == 0,
               c == CH - 1, reads=[(BGW, c), ctx.ones_col_b], writes=[pC])
    NS1 = ctx.sbuf("l1f_ns1", [1, NE], BF16)
    s.op("dve", lambda e: e.tensor_scalar(NS1[:], pS[0:1, 0:NE], -1.0, None,
         ALU.mult), reads=[pS], writes=[NS1])
    CER = ctx.sbuf("l1f_cer", [1, NE], F32)
    ctx.act(CER[:], pC[0:1, 0:NE], AF.Copy, reads=[pC], writes=[CER])
    CEB = ctx.sbuf("l1f_ceb", [128, NE], F32)
    ctx.mm(pS[:, 0:NE], ctx.ones_row_f[:], CER[:], True, True,
           reads=[CER, ctx.ones_row_f], writes=[pS])
    ctx.act(CEB[:], pS[:, 0:NE], AF.Copy, reads=[pS], writes=[CEB])

    _r3 = ctx.lnrow(3)
    _r3v = _r3[:].bitcast(BF16)
    MU2Bt = _r3v[:, 0:512]
    RSBt = _r3v[:, 512:1024]
    LG8 = ctx.sbuf("l1f_lg8", [128, NE], F32)
    RC = ctx.sbuf("l1f_rc", [128, 1], F32)
    M1 = ctx.sbuf("l1f_m1", [128, 1], F32)
    T8 = ctx.sbuf("l1f_t8", [128, NE], F32)
    GW = ctx.sbuf("l1f_gw", [128, 32, NE], F32)
    WFM = ctx.xtf[1][0:8, :, :].rearrange("p c t -> p (c t)")
    r0, r1, r2 = ctx.lnrow(0), ctx.lnrow(1), ctx.lnrow(2)
    for it in range(8):
        tsl = slice(it * 512, (it + 1) * 512)
        ps_, pq_ = ctx.pb[2], ctx.pb[3]
        ctx.act(ctx.sq[:, :, :], XN[:, :, tsl], AF.Square, reads=[XN],
                writes=[ctx.sq])
        for c in range(CH):
            ctx.mm(ps_[0:1, :], ctx.ones_col_b[:], XN[:, c, tsl], c == 0,
                   c == CH - 1, reads=[XN, ctx.ones_col_b], writes=[ps_])
        for c in range(CH):
            ctx.mm(pq_[0:1, :], ctx.ones_col_b[:], ctx.sq[:, c, :], c == 0,
                   c == CH - 1, reads=[ctx.sq, ctx.ones_col_b], writes=[pq_])
        ctx.act(r0[:], ps_[0:1, :], AF.Copy, reads=[ps_], writes=[r0],
                scale=1.0 / D)
        ctx.act(r1[:], pq_[0:1, :], AF.Copy, reads=[pq_], writes=[r1],
                scale=1.0 / D)
        ctx.tt(r2[:], r0[:], r0[:], ALU.mult, reads=[r0], writes=[r2])
        ctx.tt(r1[:], r1[:], r2[:], ALU.subtract, reads=[r1, r2], writes=[r1])
        ctx.act(r1[:], r1[:], AF.Sqrt, reads=[r1], writes=[r1], bias=1e-5)
        s.op("dve", lambda e: e.reciprocal(r1[:], r1[:]), reads=[r1],
             writes=[r1])
        s.op("dve", lambda e: e.tensor_copy(MU2Bt, r0[:]), reads=[r0],
             writes=[(_r3, 0)])
        s.op("dve", lambda e: e.tensor_copy(RSBt, r1[:]), reads=[r1],
             writes=[(_r3, 1)])
        for st_ in range(4):
            tc = it * 4 + st_
            ssl = slice(st_ * 128, st_ * 128 + 128)
            tgl = slice(tc * 128, (tc + 1) * 128)
            pl, pr = ctx.pb[4], ctx.pb[5]
            for c in range(CH):
                ctx.mm(pl[:, 0:NE], XN[:, c, tgl], SGW[:, c, :], c == 0, False,
                       reads=[XN, (SGW, c)], writes=[pl])
            ctx.mm(pl[:, 0:NE], MU2Bt[0:1, ssl], NS1[:], False, True,
                   reads=[(_r3, 0), NS1], writes=[pl])
            ctx.mm(pr[:, 0:1], RSBt[0:1, ssl], ctx.ones_col_b[0:1, :], True,
                   True, reads=[(_r3, 1), ctx.ones_col_b], writes=[pr])
            ctx.act(RC[:], pr[:, 0:1], AF.Copy, reads=[pr], writes=[RC])
            s.op("dve", lambda e: e.tensor_scalar(
                LG8[:], pl[:, 0:NE], RC[:, 0:1], None, ALU.mult),
                reads=[pl, RC], writes=[LG8])
            ctx.tt(LG8[:], LG8[:], CEB[:], ALU.add, reads=[LG8, CEB],
                   writes=[LG8])
            s.op("dve", lambda e: e.reduce_max(M1[:], LG8[:], axis=AX.X),
                 reads=[LG8], writes=[M1])
            s.op("dve", lambda e: e.tensor_scalar(M1[:], M1[:], -1.0, None,
                 ALU.mult), reads=[M1], writes=[M1])
            ctx.act(T8[:], LG8[:], AF.Exp, reads=[LG8, M1], writes=[T8],
                    bias=M1[:, 0:1])
            s.op("dve", lambda e: e.reduce_sum(RC[:], T8[:], axis=AX.X),
                 reads=[T8], writes=[RC])
            s.op("dve", lambda e: e.reciprocal(RC[:], RC[:]), reads=[RC],
                 writes=[RC])
            s.op("dve", lambda e: e.tensor_scalar(T8[:], T8[:], RC[:, 0:1],
                 None, ALU.mult), reads=[T8, RC], writes=[T8])
            s.op("dve", lambda e: e.reduce_max(M1[:], T8[:], axis=AX.X),
                 reads=[T8], writes=[M1])
            s.op("dve", lambda e: e.tensor_scalar(LG8[:], T8[:], M1[:, 0:1],
                 None, ALU.is_ge), reads=[T8, M1], writes=[LG8])
            s.op("dve", lambda e: e.scalar_tensor_tensor(LG8[:], LG8[:], -1e30,
                 T8[:], ALU.mult, ALU.add), reads=[LG8, T8], writes=[LG8])
            s.op("dve", lambda e: e.reduce_max(RC[:], LG8[:], axis=AX.X),
                 reads=[LG8], writes=[RC])
            s.op("dve", lambda e: e.tensor_scalar(LG8[:], T8[:], RC[:, 0:1],
                 None, ALU.is_ge), reads=[T8, RC], writes=[LG8])
            s.op("dve", lambda e: e.tensor_tensor(LG8[:], LG8[:], T8[:],
                 ALU.mult), reads=[LG8, T8], writes=[LG8])
            s.op("dve", lambda e: e.tensor_tensor(M1[:], M1[:], RC[:, 0:1],
                 ALU.add), reads=[M1, RC], writes=[M1])
            s.op("dve", lambda e: e.tensor_scalar(M1[:], M1[:], 1e-5, None,
                 ALU.add), reads=[M1], writes=[M1])
            s.op("dve", lambda e: e.reciprocal(M1[:], M1[:]), reads=[M1],
                 writes=[M1])
            s.op("dve", lambda e, tc=tc: e.tensor_scalar(GW[:, tc, :], LG8[:],
                 M1[:, 0:1], None, ALU.mult), reads=[LG8, M1],
                 writes=[(GW, tc)])
            pw_ = ctx.pb[6]
            s.op("pe", lambda e, tc=tc: e.transpose(pw_[0:8, 0:128],
                 GW[:, tc, :], EYEF[:]), reads=[(GW, tc), EYEF], writes=[pw_])
            ctx.act(WFM[0:8, tgl], pw_[0:8, 0:128], AF.Copy, reads=[pw_],
                    writes=[(ctx.xtf[1], "wfm", tc)])

    WG = [ctx.sbuf(f"l1f_wg{i}", [128, CH, 512], BF16) for i in range(2)]
    WU = [ctx.sbuf(f"l1f_wu{i}", [128, CH, 512], BF16) for i in range(2)]
    WBALL = ctx.sbuf("l1f_wball", [128, CH, 512], F32)
    HB4S = [ctx.sbuf("l1f_hb40", [128, 4, 512], BF16)] * 2
    T1S = [ctx.sbuf(f"l1f_t1{i}", [128, 512], BF16) for i in range(2)] * 2
    T2S = [ctx.sbuf(f"l1f_t2{i}", [128, 512], BF16) for i in range(2)] * 2
    for e in range(9):
        shared = (e == 8)
        if not shared:
            for it in range(8):
                tsl = slice(it * 512, (it + 1) * 512)
                pw_ = ctx.pb[6]
                ctx.mm(pw_[:], UNITS[:, e * 128:(e + 1) * 128], WFM[0:8, tsl],
                       True, True,
                       reads=[UNITS] + [(ctx.xtf[1], "wfm", t)
                                        for t in range(32)],
                       writes=[pw_])
                ctx.act(WBALL[:, it, :].rearrange("p t -> p t"), pw_[:],
                        AF.Copy, reads=[pw_], writes=[(WBALL, it)])
        for hg in range(4):
            wgb, wub = WG[hg % 2], WU[hg % 2]
            hsl = slice(hg * 512, (hg + 1) * 512)
            if shared:
                s.dma("qpool", wgb[:], d["swg"].rearrange(
                    "(c p) h -> p c h", p=128)[:, :, hsl], writes=[wgb])
                s.dma("qpool", wub[:], d["swu"].rearrange(
                    "(c p) h -> p c h", p=128)[:, :, hsl], writes=[wub])
            else:
                s.dma("qpool", wgb[:], d["wg"].rearrange(
                    "e (c p) h -> p e c h", p=128)[:, e, :, hsl], writes=[wgb])
                s.dma("qpool", wub[:], d["wu"].rearrange(
                    "e (c p) h -> p e c h", p=128)[:, e, :, hsl], writes=[wub])
            for it in range(8):
                tsl = slice(it * 512, (it + 1) * 512)
                HB4 = HB4S[it % 2]
                for hi in range(4):
                    pg, pu = ctx.pb[hi % 3], ctx.pb[3 + hi % 3]
                    T1, T2 = T1S[hi % 3], T2S[hi % 3]
                    hs2 = slice(hi * 128, hi * 128 + 128)
                    for c in range(CH):
                        ctx.mm(pg[:], wgb[:, c, hs2], XN[:, c, tsl], c == 0,
                               c == CH - 1, reads=[wgb, XN], writes=[pg])
                    for c in range(CH):
                        ctx.mm(pu[:], wub[:, c, hs2], XN[:, c, tsl], c == 0,
                               c == CH - 1, reads=[wub, XN], writes=[pu])
                    ctx.act(T1[:], pg[:], AF.Silu, reads=[pg], writes=[T1])
                    ctx.act(T2[:], pu[:], AF.Copy, reads=[pu], writes=[T2])
                    ctx.tt(HB4[:, hi, :], T1[:], T2[:], ALU.mult,
                           reads=[T1, T2], writes=[(HB4, it % 2, hi)])
                    if not shared:
                        ctx.tt(HB4[:, hi, :], HB4[:, hi, :], WBALL[:, it, :],
                               ALU.mult,
                               reads=[(HB4, it % 2, hi), (WBALL, it)],
                               writes=[(HB4, it % 2, hi)], eng="pool")
                s.dma("qsp", hr[:, hg * 4:hg * 4 + 4, e, tsl],
                      HB4[:], reads=[(HB4, it % 2, hi_) for hi_ in range(4)],
                      writes=[(d["hs"], e, hg, it)])

    WD = [ctx.sbuf(f"l1f_wd{i}", [128, 4, 1024], BF16) for i in range(2)]
    HT = ctx.xtf[1][:, :, :].rearrange("p c t -> p (c t)").bitcast(
        BF16).rearrange("p (c t) -> p c t", t=512)
    for tq in range(8):
        tsl = slice(tq * 512, (tq + 1) * 512)
        for e in range(9):
            s.dma("qsp", HT, hr[:, :, e, tsl],
                  reads=[(d["hs"], e, hg_, tq) for hg_ in range(4)],
                  writes=[ctx.xtf[1]])
            for q4 in range(4):
                wd = WD[q4 % 2]
                if e == 8:
                    src_ = d["swd"].rearrange("(hc p) o -> p hc o", p=128)[
                        :, q4 * 4:q4 * 4 + 4, :]
                else:
                    src_ = d["wd"].rearrange("e (hc p) o -> p e hc o", p=128)[
                        :, e, q4 * 4:q4 * 4 + 4, :]
                s.dma("qpool", wd[:], src_, writes=[wd])
                for oc in range(CH):
                    for h4 in range(4):
                        hc = q4 * 4 + h4
                        ctx.mm(ctx.pb[oc][:], wd[:, h4,
                               oc * 128:(oc + 1) * 128], HT[:, hc, :],
                               e == 0 and hc == 0, e == 8 and hc == 15,
                               reads=[wd, ctx.xtf[1]],
                               writes=[ctx.pb[oc]] if oc < 6 else
                               [(ctx.pbw, oc - 6)])
        xt = ctx.xtf[0]
        s.dma("qsp", xt[:], xr[:, :, tsl], reads=[d["x"]], writes=[xt])
        for oc in range(CH):
            s.op("dve", lambda e, oc=oc: e.tensor_tensor(
                xt[:, oc, :], xt[:, oc, :], ctx.pb[oc][:], ALU.add),
                reads=[xt] + ([ctx.pb[oc]] if oc < 6 else [(ctx.pbw, oc - 6)]),
                writes=[xt])
        s.dma("qsp", xor_[:, :, tsl], xt[:], reads=[xt], writes=[d["xo"]])


def _build_l1f():
    nc = _mk_nc()
    d = {}
    d["x"] = nc.dram_tensor("x", [D, T], F32, kind="ExternalInput").ap()
    d["xo"] = nc.dram_tensor("xo", [D, T], F32, kind="ExternalOutput").ap()
    d["xn"] = nc.dram_tensor("xn", [D, T], BF16, kind="Internal").ap()
    d["hs"] = nc.dram_tensor("hs", [2048, 9 * T], BF16, kind="Internal").ap()
    d["wg"] = nc.dram_tensor("wg", [8, D, 2048], F32, kind="ExternalInput").ap()
    d["wu"] = nc.dram_tensor("wu", [8, D, 2048], F32, kind="ExternalInput").ap()
    d["wd"] = nc.dram_tensor("wd", [8, 2048, D], F32, kind="ExternalInput").ap()
    d["swg"] = nc.dram_tensor("swg", [D, 2048], F32, kind="ExternalInput").ap()
    d["swu"] = nc.dram_tensor("swu", [D, 2048], F32, kind="ExternalInput").ap()
    d["swd"] = nc.dram_tensor("swd", [2048, D], F32, kind="ExternalInput").ap()
    d["gwT"] = nc.dram_tensor("gwT", [D, 8], F32, kind="ExternalInput").ap()
    d["lns"] = nc.dram_tensor("lns", [128, 16], F32, kind="ExternalInput").ap()
    d["eyeb"] = nc.dram_tensor("eyeb", [128, 128], BF16,
                               kind="ExternalInput").ap()
    d["units"] = nc.dram_tensor("units", [8, 1024], F32,
                                kind="ExternalInput").ap()
    d["eyef"] = nc.dram_tensor("eyef", [128, 128], F32,
                               kind="ExternalInput").ap()
    es = contextlib.ExitStack()
    with es:
        ctx = Ctx(nc, es)
        emit_l1f(ctx, d)
        ctx.s.build(es)
    return nc


def _prep_l1f(inp, p):
    p["l1f_wg"] = np.ascontiguousarray(np.asarray(inp["L1_ffn_Wg"], np.float32))
    p["l1f_wu"] = np.ascontiguousarray(np.asarray(inp["L1_ffn_Wu"], np.float32))
    p["l1f_wd"] = np.ascontiguousarray(np.asarray(inp["L1_ffn_Wd"], np.float32))
    p["l1f_swg"] = np.ascontiguousarray(np.asarray(inp["L1_ffn_sWg"], np.float32))
    p["l1f_swu"] = np.ascontiguousarray(np.asarray(inp["L1_ffn_sWu"], np.float32))
    p["l1f_swd"] = np.ascontiguousarray(np.asarray(inp["L1_ffn_sWd"], np.float32))
    p["l1f_gwT"] = np.ascontiguousarray(np.asarray(inp["L1_ffn_gate_w"],
                                                   np.float32).T)
    lns = np.zeros((128, 16), np.float32)
    lns[:, 0:8] = _col(inp["L1_ffn_ln_scale"])
    lns[:, 8:16] = _col(inp["L1_ffn_ln_bias"])
    p["l1f_lns"] = lns
    p["l1f_eyeb"] = np.eye(128, dtype=BF)
    p["l1f_units"] = np.repeat(np.eye(8, dtype=np.float32), 128, axis=1)
    p["l1f_eyef"] = np.eye(128, dtype=np.float32)
    return p


def run_l1f(xs, prep):
    n = len(xs)
    in_maps = [{"x": xs[i], **{k: prep[f"l1f_{k}"] for k in
                ["wg", "wu", "wd", "swg", "swu", "swd", "gwT", "lns", "eyeb",
                 "units", "eyef"]}}
               for i in range(n)]
    res = _run_phase("l1f", _build_l1f, in_maps)
    return [np.asarray(res.results[i]["xo"]) for i in range(n)]


def emit_l2a(ctx, d):
    s = ctx.s
    emit_ln(ctx, d["x"], d["xn"])
    COLS = ctx.sbuf("l2a_cols", [128, 32], F32)
    s.dma("qpool", COLS[:], d["cols"][:], writes=[COLS])
    xb0 = ctx.sbuf("l2a_xb0", [128, CH, 512], BF16)
    emit_ln(ctx, d["xn"], d["xn2"], src_dt=BF16, gcol=(COLS, 0), bcol=(COLS, 8),
            in_bufs=[xb0, xb0])
    xnr = d["xn"].rearrange("(c p) t -> p c t", p=128)
    xn2r = d["xn2"].rearrange("(c p) t -> p c t", p=128)
    xr = d["x"].rearrange("(c p) t -> p c t", p=128)
    xor_ = d["xo"].rearrange("(c p) t -> p c t", p=128)
    uvwr = d["uvwT"].rearrange("(c p) o -> p c o", p=128)
    ur = d["u"].rearrange("(m p) t -> p m t", p=128)
    p2r = d["p2"].rearrange("(m p) t -> p m t", p=128)

    UVW = [ctx.sbuf(f"l2a_uvw{i}", [128, CH, 512], BF16) for i in range(2)]
    UCOL = ctx.sbuf("l2a_ucol", [128, 16], F32)
    s.dma("qpool", UCOL[:], d["ucol"][:], writes=[UCOL])
    VBROW = ctx.sbuf("l2a_vbrow", [1, 2048], BF16)
    s.dma("qpool", VBROW[:], d["vbrow"][:], writes=[VBROW])
    BASE = ctx.sbuf("l2a_base", [128, T], BF16)
    QR = ctx.sbuf("l2a_q", [128, T], BF16)
    KR = ctx.sbuf("l2a_k", [128, T], BF16)
    RC2 = ctx.sbuf("l2a_rc2", [128, T], BF16)
    RSS = ctx.sbuf("l2a_rss", [128, T], BF16)
    SWP = ctx.sbuf("l2a_swp", [128, 128], BF16)
    s.dma("qpool", RC2[:], d["ropec"][:], writes=[RC2])
    s.dma("qpool", RSS[:], d["ropes"][:], writes=[RSS])
    s.dma("qpool", SWP[:], d["swp"][:], writes=[SWP])
    TMPB = ctx.sbuf("l2a_tmpb", [128, 512], BF16)
    TM = [ctx.sbuf(f"l2a_tm{i}", [128, 512], F32) for i in range(3)]
    OUTB = [ctx.sbuf(f"l2a_ob{i}", [128, 512], BF16) for i in range(2)]
    BIAS = [ctx.sbuf(f"l2a_bias{i}", [128, 512], F32) for i in range(2)]
    ARENA2 = ctx.sbuf("l2a_arena", [128, 16, 1024], BF16)
    VCOL = ARENA2[:].rearrange("p a b -> p (a b)").rearrange(
        "p (k n) -> p k n", n=512)
    UT = [ctx.sbuf(f"l2a_ut{i}", [128, 512], BF16) for i in range(2)]

    for ug in range(4):
        uvw = UVW[ug % 2]
        s.dma("qpool", uvw[:], uvwr[:, :, ug * 512:(ug + 1) * 512], writes=[uvw])
        for it in range(8):
            tsl = slice(it * 512, (it + 1) * 512)
            xb = ctx.xnt[it % 2]
            s.dma("qsp", xb[:], xn2r[:, :, tsl], reads=[d["xn2"]], writes=[xb])
            for ui in range(4):
                uc = ug * 4 + ui
                pg = ctx.pb[ui % 2]
                for c in range(CH):
                    ctx.mm(pg[:], uvw[:, c, ui * 128:(ui + 1) * 128],
                           xb[:, c, :], c == 0, c == CH - 1,
                           reads=[uvw, xb], writes=[pg])
                ob = OUTB[ui % 2]
                ctx.act(ob[:], pg[:], AF.Silu, reads=[pg, UCOL], writes=[ob],
                        bias=UCOL[:, uc:uc + 1])
                s.dma("qsp", ur[:, uc, tsl], ob[:], reads=[ob], writes=[d["u"]])
    for vg in range(4):
        uvw = UVW[vg % 2]
        s.dma("qpool", uvw[:], uvwr[:, :, 2048 + vg * 512:2560 + vg * 512],
              writes=[uvw])
        for it in range(8):
            tsl = slice(it * 512, (it + 1) * 512)
            xb = ctx.xnt[it % 2]
            s.dma("qsp", xb[:], xn2r[:, :, tsl], reads=[d["xn2"]], writes=[xb])
            for st_ in range(4):
                trow = it * 512 + st_ * 128
                pv = ctx.pb[2 + st_ % 2]
                for c in range(CH):
                    ctx.mm(pv[:], xb[:, c, st_ * 128:st_ * 128 + 128],
                           uvw[:, c, :], c == 0, False, reads=[xb, uvw],
                           writes=[pv])
                ctx.mm(pv[:], ctx.ones_row_b[:],
                       VBROW[0:1, vg * 512:(vg + 1) * 512], False, True,
                       reads=[ctx.ones_row_b, VBROW], writes=[pv])
                ob = OUTB[st_ % 2]
                ctx.act(ob[:], pv[:], AF.Silu, reads=[pv], writes=[ob])
                s.dma("qsp", d["v"][trow:trow + 128, vg * 512:(vg + 1) * 512],
                      ob[:], reads=[ob], writes=[d["v"]])
    uvw = UVW[0]
    s.dma("qpool", uvw[:, :, 0:128], uvwr[:, :, 4096:4224], writes=[uvw])
    for it in range(8):
        tsl = slice(it * 512, (it + 1) * 512)
        xb = ctx.xnt[it % 2]
        s.dma("qsp", xb[:], xn2r[:, :, tsl], reads=[d["xn2"]], writes=[xb])
        pg = ctx.pb[4]
        for c in range(CH):
            ctx.mm(pg[:], uvw[:, c, 0:128], xb[:, c, :], c == 0, c == CH - 1,
                   reads=[uvw, xb], writes=[pg])
        ctx.act(BASE[:, tsl], pg[:], AF.Silu, reads=[pg, COLS],
                writes=[(BASE, it)], bias=COLS[:, 16:17])
    for which, dst in [(0, QR), (1, KR)]:
        gc = COLS[:, 18 + 2 * which:19 + 2 * which]
        bc = COLS[:, 19 + 2 * which:20 + 2 * which]
        for it in range(8):
            tsl = slice(it * 512, (it + 1) * 512)
            s.op("dve", lambda e, tsl=tsl, gc=gc, bc=bc: e.tensor_scalar(
                TMPB[:], BASE[:, tsl], gc, bc, ALU.mult, ALU.add),
                reads=[(BASE, it), COLS], writes=[TMPB])
            psw = ctx.pb[5]
            ctx.mm(psw[:], SWP[:], TMPB[:], True, True, reads=[SWP, TMPB],
                   writes=[psw])
            ctx.tt(TM[0][:], TMPB[:], RC2[:, tsl], ALU.mult,
                   reads=[TMPB, RC2], writes=[TM[0]])
            ctx.tt(TM[1][:], psw[:], RSS[:, tsl], ALU.mult,
                   reads=[psw, RSS], writes=[TM[1]])
            s.op("dve", lambda e, dst=dst, tsl=tsl: e.tensor_tensor(
                dst[:, tsl], TM[0][:], TM[1][:], ALU.add),
                reads=[TM[0], TM[1]], writes=[(dst, it)])

    def pts(kt, n=1):
        buf = ctx.xtf[kt // 16]
        v = buf[:, :, :].rearrange("p c t -> p (c t)").bitcast(BF16)
        return v.rearrange("p (k q) -> p k q", q=512)[:, kt % 16:kt % 16 + n, :]

    ISQ = 1.0 / float(np.sqrt(128.0))
    for qg in range(8):
        qsl = slice(qg * 512, (qg + 1) * 512)
        for kt in range(32):
            pst = ctx.pbw[:, (kt % 2) * 512:(kt % 2) * 512 + 512]
            ctx.mm(pst, KR[:, kt * 128:(kt + 1) * 128], QR[:, qsl], True, True,
                   reads=[(KR, kt // 4), (QR, qg)], writes=[(ctx.pbw, kt % 2)])
            bt = BIAS[kt % 2]
            s.dma("qsp", bt[:], d["bias"][kt * 128:(kt + 1) * 128, qsl],
                  reads=[d["bias"]], writes=[bt])
            tm = TM[2]
            s.op("dve", lambda e, pst=pst, bt=bt, tm=tm: e.scalar_tensor_tensor(
                tm[:], pst, ISQ, bt[:], ALU.mult, ALU.add),
                reads=[(ctx.pbw, kt % 2), bt], writes=[tm])
            s.op("dve", lambda e, tm=tm: e.tensor_scalar(
                tm[:], tm[:], 0.0, None, ALU.max), reads=[tm], writes=[tm])
            ctx.act(pts(kt)[:, 0, :], tm[:], AF.Square, reads=[tm],
                    writes=[(ctx.xtf[kt // 16], kt)])
        for mg in range(4):
            vc = VCOL
            s.dma("qsp", vc[:], d["v"].rearrange("(a p) m -> p a m", p=128)[
                  :, :, mg * 512:(mg + 1) * 512], reads=[d["v"]], writes=[vc])
            for ms in range(4):
                mc = mg * 4 + ms
                pa = ctx.pb[2 + ms % 2]
                for kt in range(32):
                    ctx.mm(pa[:], vc[:, kt, ms * 128:(ms + 1) * 128],
                           pts(kt)[:, 0, :], kt == 0, kt == 31,
                           reads=[vc, (ctx.xtf[kt // 16], kt)], writes=[pa])
                ut = UT[mc % 2]
                s.dma("qsp", ut[:], ur[:, mc, qsl], reads=[d["u"]],
                      writes=[ut])
                ob = OUTB[mc % 2]
                s.op("dve", lambda e, pa=pa, ut=ut, ob=ob: e.tensor_tensor(
                    ob[:], ut[:], pa[:], ALU.mult), reads=[pa, ut],
                    writes=[ob])
                s.dma("qsp", p2r[:, mc, qsl], ob[:], reads=[ob],
                      writes=[d["p2"]])

    s.fence()
    OWT = ARENA2
    owr = d["owT"].rearrange("(m p) o -> p m o", p=128)
    for g in range(2):
        s.dma("qpool", OWT[:, 8 * g:8 * g + 8, :], owr[:, 8 * g:8 * g + 8, :],
              writes=[OWT])
    P2T = ctx.sbuf("l2a_p2t", [128, 16, 512], BF16)
    for it in range(8):
        tsl = slice(it * 512, (it + 1) * 512)
        s.dma("qsp", P2T[:], p2r[:, :, tsl], reads=[d["p2"]], writes=[P2T])
        xb = ctx.xnt[it % 2]
        s.dma("qsp", xb[:], xnr[:, :, tsl], reads=[d["xn"]], writes=[xb])
        xt = ctx.xtf[it % 2]
        s.dma("qsp", xt[:], xr[:, :, tsl], reads=[d["x"]], writes=[xt])
        for oc in range(CH):
            po = ctx.pb[4 + oc % 2]
            for m in range(16):
                ctx.mm(po[:], OWT[:, m, oc * 128:(oc + 1) * 128], P2T[:, m, :],
                       m == 0, m == 15, reads=[OWT, P2T], writes=[po])
            tm = TM[oc % 2]
            s.op("dve", lambda e, po=po, tm=tm, oc=oc: e.tensor_scalar(
                tm[:], po[:], COLS[:, 24 + oc:25 + oc], None, ALU.add),
                reads=[po, COLS], writes=[tm])
            ctx.tt(tm[:], tm[:], xb[:, oc, :], ALU.add, reads=[tm, xb],
                   writes=[tm])
            ctx.tt(xt[:, oc, :], xt[:, oc, :], tm[:], ALU.add,
                   reads=[xt, tm], writes=[xt])
        s.dma("qsp", xor_[:, :, tsl], xt[:], reads=[xt], writes=[d["xo"]])


def _build_l2a():
    nc = _mk_nc()
    d = {}
    d["x"] = nc.dram_tensor("x", [D, T], F32, kind="ExternalInput").ap()
    d["xo"] = nc.dram_tensor("xo", [D, T], F32, kind="ExternalOutput").ap()
    for nm, sh in [("xn", [D, T]), ("xn2", [D, T]), ("u", [2048, T]),
                   ("v", [T, 2048]), ("p2", [2048, T])]:
        d[nm] = nc.dram_tensor(nm, sh, BF16, kind="Internal").ap()
    d["uvwT"] = nc.dram_tensor("uvwT", [D, 4224], F32, kind="ExternalInput").ap()
    d["owT"] = nc.dram_tensor("owT", [2048, D], F32, kind="ExternalInput").ap()
    d["cols"] = nc.dram_tensor("cols", [128, 32], F32, kind="ExternalInput").ap()
    d["ucol"] = nc.dram_tensor("ucol", [128, 16], F32, kind="ExternalInput").ap()
    d["vbrow"] = nc.dram_tensor("vbrow", [1, 2048], F32, kind="ExternalInput").ap()
    d["ropec"] = nc.dram_tensor("ropec", [128, T], F32, kind="ExternalInput").ap()
    d["ropes"] = nc.dram_tensor("ropes", [128, T], F32, kind="ExternalInput").ap()
    d["swp"] = nc.dram_tensor("swp", [128, 128], BF16, kind="ExternalInput").ap()
    d["bias"] = nc.dram_tensor("bias", [T, T], F32, kind="ExternalInput").ap()
    es = contextlib.ExitStack()
    with es:
        ctx = Ctx(nc, es)
        emit_l2a(ctx, d)
        ctx.s.build(es)
    return nc


def _prep_l2a(inp, p):
    p["l2a_uvwT"] = np.ascontiguousarray(np.asarray(inp["L2_attn_uv_w"],
                                                    np.float32).T)
    p["l2a_owT"] = np.ascontiguousarray(np.asarray(inp["L2_attn_o_w"],
                                                   np.float32).T)
    uvb = np.asarray(inp["L2_attn_uv_b"], np.float32)
    cols = np.zeros((128, 32), np.float32)
    cols[:, 0:8] = _col(inp["L2_attn_ln_w"])
    cols[:, 8:16] = _col(inp["L2_attn_ln_b"])
    cols[:, 16:17] = uvb[4096:4224].reshape(1, 128).T
    g = np.asarray(inp["L2_attn_gamma"], np.float32)
    b = np.asarray(inp["L2_attn_beta"], np.float32)
    cols[:, 18] = g[0]
    cols[:, 19] = b[0]
    cols[:, 20] = g[1]
    cols[:, 21] = b[1]
    cols[:, 24:32] = _col(inp["L2_attn_o_b"])
    p["l2a_cols"] = cols
    p["l2a_ucol"] = _col(uvb[0:2048])
    p["l2a_vbrow"] = np.ascontiguousarray(uvb[2048:4096].reshape(1, 2048))
    pos = np.arange(T, dtype=np.float64)
    half = 64
    inv_freq = 10000.0 ** (np.arange(half, dtype=np.float64) / half)
    sinu = pos[None, :] * inv_freq[:, None]
    cosv = np.cos(sinu)
    sinv = np.sin(sinu)
    p["l2a_ropec"] = np.concatenate([cosv, cosv]).astype(np.float32)
    p["l2a_ropes"] = np.concatenate([-sinv, sinv]).astype(np.float32)
    swp = np.zeros((128, 128), np.float32)
    for i_ in range(128):
        swp[i_, (i_ + 64) % 128] = 1.0
    p["l2a_swp"] = swp.astype(BF)
    w = np.asarray(inp["L2_attn_w"], np.float64)
    idx = np.clip(np.arange(T)[:, None] - np.arange(T)[None, :] + 511, 0, 1022)
    p["l2a_bias"] = w[idx].astype(np.float32)
    return p


def run_l2a(xs, prep):
    n = len(xs)
    in_maps = [{"x": xs[i], **{k: prep[f"l2a_{k}"] for k in
                ["uvwT", "owT", "cols", "ucol", "vbrow", "ropec", "ropes",
                 "swp", "bias"]}} for i in range(n)]
    res = _run_phase("l2a", _build_l2a, in_maps)
    return [np.asarray(res.results[i]["xo"]) for i in range(n)]


def emit_rms(ctx, src, dst, gcol, src_dt=BF16, eps=1e-12, also_tok=None,
             eyeb=None, blk=False, dst2=None):
    s = ctx.s
    sr = src.rearrange("(c p) t -> p c t", p=128)
    dr = None if blk else dst.rearrange("(c p) t -> p c t", p=128)
    gb, go = gcol
    r0 = ctx.lnrow(0)
    XMS = ctx.xnt
    for i in range(T // 512):
        b = i % 2
        tsl = slice(i * 512, (i + 1) * 512)
        xt = XMS[b]
        ps, pbc = ctx.pb[0 + b], ctx.pb[2 + b]
        s.dma("qpool" if src_dt == F32 else "qsp", xt[:], sr[:, :, tsl],
              reads=[src], writes=[xt])
        ctx.act(ctx.sq[:, :, :], xt[:, :, :], AF.Square, reads=[xt],
                writes=[ctx.sq])
        for c in range(CH):
            ctx.mm(ps[0:1, :], ctx.ones_col_b[:], ctx.sq[:, c, :], c == 0,
                   c == CH - 1, reads=[ctx.sq, ctx.ones_col_b], writes=[ps])
        ctx.act(r0[:], ps[0:1, :], AF.Sqrt, reads=[ps], writes=[r0], bias=0.0)
        s.op("dve", lambda e: e.tensor_scalar(r0[:], r0[:], float(eps), None,
             ALU.max), reads=[r0], writes=[r0])
        s.op("dve", lambda e: e.reciprocal(r0[:], r0[:]), reads=[r0],
             writes=[r0])
        s.op("dve", lambda e: e.tensor_scalar(r0[:], r0[:], 32.0, None,
             ALU.mult), reads=[r0], writes=[r0])
        ctx.mm(pbc[:], ctx.ones_row_f[:], r0[:], True, True,
               reads=[r0, ctx.ones_row_f], writes=[pbc])
        for c in range(CH):
            ctx.tt(xt[:, c, :], xt[:, c, :], pbc[:], ALU.mult,
                   reads=[xt, pbc], writes=[xt])
        for c in range(CH):
            s.op("dve", lambda e, c=c, xt=xt: e.tensor_scalar(
                xt[:, c, :], xt[:, c, :], gb[:, go + c:go + c + 1], None,
                ALU.mult), reads=[xt, gb], writes=[xt])
        if blk:
            for j in range(4):
                dv = dst[i * 4 + j, :, :].rearrange("p (c n) -> p c n", n=128)
                s.dma("qsp", dv, xt[:, :, j * 128:(j + 1) * 128],
                      reads=[xt], writes=[dst])
            if dst2 is not None:
                s.dma("qsp", dst2.rearrange("(c p) t -> p c t", p=128)[
                      :, :, tsl], xt[:], reads=[xt], writes=[dst2])
        else:
            s.dma("qsp", dr[:, :, tsl], xt[:], reads=[xt], writes=[dst])
        if also_tok is not None:
            for j in range(4):
                for c in range(CH):
                    ptr = ctx.pb[4 + c % 2]
                    pv = ptr[:, 0:64].bitcast(BF16)
                    s.op("pe", lambda e, pv=pv, c=c, xt=xt, j=j: e.transpose(
                        pv[:, 0:128], xt[:, c, j * 128:(j + 1) * 128],
                        eyeb[:]), reads=[xt, eyeb], writes=[ptr])
                    ctx.act(TTK[:, c, :], pv[:, 0:128], AF.Copy,
                            reads=[ptr], writes=[(TTK, c)])
                s.dma("qsp", also_tok[i * 512 + j * 128:
                      i * 512 + j * 128 + 128, :],
                      TTK[:].rearrange("p c t -> p (c t)"),
                      reads=[(TTK, c_) for c_ in range(CH)], writes=[also_tok])


TTK = None


def emit_l2f(ctx, d):
    s = ctx.s
    COLS = ctx.sbuf("l2f_cols", [128, 16], F32)
    s.dma("qpool", COLS[:], d["cols"][:], writes=[COLS])
    EYEB = ctx.sbuf("l2f_eyeb", [128, 128], BF16)
    s.dma("qpool", EYEB[:], d["eyeb"][:], writes=[EYEB])
    EYEF = ctx.sbuf("l2f_eyef", [128, 128], F32)
    s.dma("qpool", EYEF[:], d["eyef"][:], writes=[EYEF])
    XMF = ctx.sbuf("l2f_xmf", [128, CH, T], BF16)
    XMT = ctx.sbuf("l2f_xmt", [128, 32, 1024], BF16)
    HB = lambda hs: XMT[:, hs, 0:512]
    CS = lambda nb: XMT[:, nb, 512:1024]
    xf0 = ctx.xtf[0][:].rearrange("p c t -> p (c t)")
    xf1 = ctx.xtf[1][:].rearrange("p c t -> p (c t)")
    xb0 = xf0.bitcast(BF16)
    xb1 = xf1.bitcast(BF16)
    PRB = [xb0[:, 0:1024], xb0[:, 1024:2048]]
    ETM = xb0[:, 2048:2560]
    BCT = xf0[:, 1536:2048]
    CTF = xf0[:, 2048:2560]
    RZB = xf0[:, 2560:3072]
    W1S = [xb1[:, 0:1024].rearrange("p (c n) -> p c n", n=128),
           xb1[:, 1024:2048].rearrange("p (c n) -> p c n", n=128)]
    W2C = [xb1[:, 2048:3072], xb1[:, 3072:4096]]
    SLE = [xb1[:, 4096:5120], xb1[:, 5120:6144]]

    w1r = d["w1"].rearrange("(c p) h -> p c h", p=128)
    w2r = d["w2"].rearrange("(hc p) o -> p hc o", p=128)
    w2br = d["w2b"].rearrange("(hc p) o -> p hc o", p=128)
    for g in range(8):
        t0 = ctx.xnt[g % 2]
        s.dma("qpool", t0[:], w1r[:, :, g * 512:(g + 1) * 512],
              reads=[d["w1"]], writes=[t0])
        for j in range(4):
            dv = d["w1b"][g * 4 + j, :, :].rearrange("p (c n) -> p c n", n=128)
            s.dma("qpool", dv, t0[:, :, j * 128:(j + 1) * 128],
                  reads=[t0], writes=[d["w1b"]])
    for g in range(8):
        t0 = ctx.xnt[g % 2]
        v = t0[:].rearrange("p c t -> p (c t)").rearrange(
            "p (a b) -> p a b", b=1024)
        s.dma("qpool", v, w2r[:, g * 4:(g + 1) * 4, :],
              reads=[d["w2"]], writes=[t0])
        s.dma("qpool", w2br[:, g * 4:(g + 1) * 4, :], v,
              reads=[t0], writes=[d["w2b"]])

    xr = d["x"].rearrange("(c p) t -> p c t", p=128)
    mu = ctx.lnrow(0)
    msq = ctx.lnrow(1)
    var = ctx.lnrow(2)
    rstd = ctx.lnrow(3)
    for i in range(8):
        b = i % 2
        xt = ctx.xtf[b]
        ps, pq2 = ctx.pb[0 + b], ctx.pb[2 + b]
        pmu, prs = ctx.pb[4 + b], ctx.pb[6 + b]
        tsl = slice(i * 512, (i + 1) * 512)
        s.dma("qsp", xt[:], xr[:, :, tsl], reads=[d["x"]], writes=[xt])
        ctx.act(ctx.sq[:, :, :], xt[:, :, :], AF.Square, reads=[xt],
                writes=[ctx.sq])
        for c in range(CH):
            ctx.mm(ps[0:1, :], ctx.ones_col_f[:], xt[:, c, :], c == 0,
                   c == CH - 1, reads=[xt, ctx.ones_col_f], writes=[ps])
        for c in range(CH):
            ctx.mm(pq2[0:1, :], ctx.ones_col_b[:], ctx.sq[:, c, :], c == 0,
                   c == CH - 1, reads=[ctx.sq, ctx.ones_col_b], writes=[pq2])
        ctx.act(mu[:], ps[0:1, :], AF.Copy, reads=[ps], writes=[mu],
                scale=1.0 / D)
        ctx.act(msq[:], pq2[0:1, :], AF.Copy, reads=[pq2], writes=[msq],
                scale=1.0 / D)
        ctx.tt(var[:], mu[:], mu[:], ALU.mult, reads=[mu], writes=[var])
        ctx.tt(var[:], msq[:], var[:], ALU.subtract, reads=[msq, var],
               writes=[var])
        ctx.act(var[:], var[:], AF.Sqrt, reads=[var], writes=[var], bias=1e-12)
        s.op("dve", lambda e: e.reciprocal(rstd[:], var[:]), reads=[var],
             writes=[rstd])
        ctx.mm(pmu[:], ctx.ones_row_f[:], mu[:], True, True,
               reads=[mu, ctx.ones_row_f], writes=[pmu])
        ctx.mm(prs[:], ctx.ones_row_f[:], rstd[:], True, True,
               reads=[rstd, ctx.ones_row_f], writes=[prs])
        for c in range(CH):
            ctx.tt(xt[:, c, :], xt[:, c, :], pmu[:], ALU.subtract,
                   reads=[xt, pmu], writes=[xt])
        for c in range(CH):
            ctx.tt(xt[:, c, :], xt[:, c, :], prs[:], ALU.mult,
                   reads=[xt, prs], writes=[xt])
        for c in range(CH):
            s.op("dve", lambda e, c=c, xt=xt, tsl=tsl: e.tensor_scalar(
                XMF[:, c, tsl], xt[:, c, :], COLS[:, c:c + 1], None, ALU.mult),
                reads=[xt, COLS], writes=[(XMF, i, c)])
        for j in range(4):
            tc = i * 4 + j
            for c in range(CH):
                ptr = ctx.pb[4 + c % 2]
                pv = ptr[:, 0:64].bitcast(BF16)
                s.op("pe", lambda e, pv=pv, c=c, i=i, j=j: e.transpose(
                    pv[:, 0:128],
                    XMF[:, c, i * 512 + j * 128:i * 512 + (j + 1) * 128],
                    EYEB[:]), reads=[(XMF, i, c), EYEB], writes=[ptr])
                ctx.act(XMT[:, tc, c * 128:(c + 1) * 128], pv[:, 0:128],
                        AF.Copy, reads=[ptr], writes=[(XMT, tc, c)])

    emit_rms(ctx, d["seT"], d["se"], (COLS, 8), src_dt=F32, blk=True,
             dst2=d["sef"])
    sefr = d["sef"].rearrange("(c p) n -> p c n", p=128)

    def seblk(nb):
        return d["se"][nb, :, :].rearrange("p (c n) -> p c n", n=128)

    SEB = [ctx.sbuf(f"l2f_seb{i}", [128, CH, 128], BF16) for i in range(2)]
    PTM = [ctx.sbuf(f"l2f_ptm{i}", [128, 128], BF16) for i in range(2)]
    TM = ctx.sbuf("l2f_tm", [128, 8], F32)
    ZT8 = ctx.sbuf("l2f_zt8", [128, 8], F32)
    MC = ctx.sbuf("l2f_mc", [128, 1], F32)
    NMC = ctx.sbuf("l2f_nmc", [128, 1], F32)
    RZ = ctx.sbuf("l2f_rz", [128, 1], F32)
    LS = ctx.xtf[0]
    PS = ctx.sq

    for nb in range(32):
        seb = SEB[nb % 2]
        s.dma("qsp", seb[:], seblk(nb), reads=[d["se"]], writes=[seb])
        for tt in range(8):
            pl = ctx.pb[tt % 2]
            for c in range(CH):
                ctx.mm(pl[:], seb[:, c, :], XMF[:, c, tt * 512:(tt + 1) * 512],
                       c == 0, c == CH - 1, reads=[seb, XMF], writes=[pl])
            ctx.act(LS[:, tt, :], pl[:], AF.Copy, reads=[pl], writes=[(LS, tt)])
            s.op("dve", lambda e, tt=tt: e.reduce_max(
                TM[:, tt:tt + 1], LS[:, tt, :], axis=AX.X),
                reads=[(LS, tt)], writes=[(TM, tt)])
        s.op("dve", lambda e: e.reduce_max(MC[:], TM[:], axis=AX.X),
             reads=[(TM, t_) for t_ in range(8)], writes=[MC])
        s.op("dve", lambda e: e.tensor_scalar(NMC[:], MC[:], -1.0, None,
             ALU.mult), reads=[MC], writes=[NMC])
        for tt in range(8):
            ctx.act(PS[:, tt, :], LS[:, tt, :], AF.Exp,
                    reads=[(LS, tt), NMC], writes=[(PS, tt)], bias=NMC[:, 0:1])
            s.op("dve", lambda e, tt=tt: e.reduce_sum(
                ZT8[:, tt:tt + 1], PS[:, tt, :], axis=AX.X),
                reads=[(PS, tt)], writes=[(ZT8, tt)])
        s.op("dve", lambda e: e.reduce_sum(MC[:], ZT8[:], axis=AX.X),
             reads=[(ZT8, t_) for t_ in range(8)], writes=[MC])
        s.op("dve", lambda e: e.reciprocal(RZ[:], MC[:]), reads=[MC],
             writes=[RZ])
        for tt in range(8):
            s.op("dve", lambda e, tt=tt: e.tensor_scalar(
                PS[:, tt, :], PS[:, tt, :], RZ[:, 0:1], None, ALU.mult),
                reads=[(PS, tt), RZ], writes=[(PS, tt)])
        for tt in range(8):
            for j in range(4):
                tc = tt * 4 + j
                ptr = ctx.pb[2 + tc % 2]
                pv = ptr[:, 0:64].bitcast(BF16)
                s.op("pe", lambda e, pv=pv, tt=tt, j=j: e.transpose(
                    pv[:, 0:128], PS[:, tt, j * 128:(j + 1) * 128], EYEB[:]),
                    reads=[(PS, tt), EYEB], writes=[ptr])
                pt = PTM[tc % 2]
                ctx.act(pt[:], pv[:, 0:128], AF.Copy, reads=[ptr], writes=[pt])
                ctx.mm(ctx.pb[4][:], pt[:], XMT[:, tc, 0:512], tc == 0,
                       tc == 31, reads=[pt, XMT], writes=[ctx.pb[4]])
                ctx.mm(ctx.pb[5][:], pt[:], XMT[:, tc, 512:1024], tc == 0,
                       tc == 31, reads=[pt, XMT], writes=[ctx.pb[5]])
        sle = SLE[nb % 2]
        ctx.act(sle[:, 0:512], ctx.pb[4][:], AF.Copy, reads=[ctx.pb[4]],
                writes=[(ctx.xtf[1], "sle", nb % 2, 0)])
        ctx.act(sle[:, 512:1024], ctx.pb[5][:], AF.Copy, reads=[ctx.pb[5]],
                writes=[(ctx.xtf[1], "sle", nb % 2, 1)])
        s.dma("qsp", d["sl"][nb * 128:(nb + 1) * 128, :], sle,
              reads=[(ctx.xtf[1], "sle", nb % 2, 0),
                     (ctx.xtf[1], "sle", nb % 2, 1)], writes=[d["sl"]])

    s.fence()
    for ng in range(8):
        sls = ctx.xnt[ng % 2]
        for k in range(4):
            slb = SLE[k % 2]
            s.dma("qsp", slb, d["sl"][(ng * 4 + k) * 128:
                  (ng * 4 + k + 1) * 128, :], reads=[d["sl"]],
                  writes=[(ctx.xtf[1], "sle", k % 2, 0),
                          (ctx.xtf[1], "sle", k % 2, 1)])
            for c in range(CH):
                ptr = ctx.pb[2 + c % 2]
                pv = ptr[:, 0:64].bitcast(BF16)
                s.op("pe", lambda e, pv=pv, slb=slb, c=c: e.transpose(
                    pv[:, 0:128], slb[:, c * 128:(c + 1) * 128], EYEB[:]),
                    reads=[(ctx.xtf[1], "sle", k % 2, 0),
                           (ctx.xtf[1], "sle", k % 2, 1), EYEB], writes=[ptr])
                ctx.act(sls[:, c, k * 128:(k + 1) * 128], pv[:, 0:128],
                        AF.Copy, reads=[ptr], writes=[(sls, k, c)])
        for hs in range(32):
            w1s = W1S[hs % 2]
            s.dma("qpool", w1s,
                  d["w1b"][hs, :, :].rearrange("p (c n) -> p c n", n=128),
                  reads=[d["w1b"]], writes=[(ctx.xtf[1], "w1", hs % 2)])
            pl = ctx.pb[hs % 2]
            for c in range(CH):
                ctx.mm(pl[:], w1s[:, c, :], sls[:, c, :], c == 0, c == CH - 1,
                       reads=[(ctx.xtf[1], "w1", hs % 2), sls], writes=[pl])
            ctx.act(HB(hs), pl[:], AF.Gelu, reads=[pl],
                    writes=[(XMT, "hb", hs)])
        for hc in range(32):
            w2c = W2C[hc % 2]
            s.dma("qpool", w2c, w2br[:, hc, :], reads=[d["w2b"]],
                  writes=[(ctx.xtf[1], "w2", hc % 2)])
            for ns in range(4):
                for half in range(2):
                    idx = ns * 2 + half
                    ap = (ctx.pb[idx][:] if idx < 6 else
                          ctx.pbw[:, (idx - 6) * 512:(idx - 5) * 512])
                    ctx.mm(ap, XMT[:, hc, ns * 128:(ns + 1) * 128],
                           w2c[:, half * 512:(half + 1) * 512],
                           hc == 0, hc == 31,
                           reads=[(XMT, "hb", hc), (ctx.xtf[1], "w2", hc % 2)],
                           writes=[ctx.pb[idx]] if idx < 6 else
                           [(ctx.pbw, idx - 6)])
        for ns in range(4):
            prb = PRB[ns % 2]
            for half in range(2):
                idx = ns * 2 + half
                ap = (ctx.pb[idx][:] if idx < 6 else
                      ctx.pbw[:, (idx - 6) * 512:(idx - 5) * 512])
                ctx.act(prb[:, half * 512:(half + 1) * 512], ap, AF.Copy,
                        reads=[ctx.pb[idx]] if idx < 6 else
                        [(ctx.pbw, idx - 6)],
                        writes=[(ctx.xtf[0], "prb", ns % 2, half)])
            s.dma("qsp", d["proc"][ng * 512 + ns * 128:
                  ng * 512 + (ns + 1) * 128, :], prb,
                  reads=[(ctx.xtf[0], "prb", ns % 2, 0),
                         (ctx.xtf[0], "prb", ns % 2, 1)], writes=[d["proc"]])

    M4 = ctx.sbuf("l2f_m4", [128, 4], F32)
    Z4 = ctx.sbuf("l2f_z4", [128, 4], F32)
    RM = ctx.sbuf("l2f_rmax", [128, 1], F32)
    SC1 = ctx.sbuf("l2f_sc1", [128, 1], F32)
    RS = ctx.sbuf("l2f_rs", [128, 1], F32)
    LSETR = ctx.sbuf("l2f_lsetr", [1, 512], F32)
    HIB = ctx.sbuf("l2f_hib", [1, 512], BF16)
    HIF = ctx.sbuf("l2f_hif", [1, 512], F32)
    LOB = ctx.sbuf("l2f_lob", [1, 512], BF16)
    xor_ = d["xo"].rearrange("(c p) t -> p c t", p=128)
    for tg in range(8):
        tsl = slice(tg * 512, (tg + 1) * 512)
        s.op("pool", lambda e: e.memset(M4[:], -1e30), writes=[M4])
        for ng2 in range(8):
            seg = ctx.xnt[ng2 % 2]
            s.dma("qsp", seg[:], sefr[:, :, ng2 * 512:(ng2 + 1) * 512],
                  reads=[d["sef"]], writes=[seg])
            for q in range(4):
                pl = ctx.pb[(ng2 * 4 + q) % 2]
                for c in range(CH):
                    ctx.mm(pl[:],
                           XMF[:, c, tg * 512 + q * 128:tg * 512 + (q + 1) * 128],
                           seg[:, c, :], c == 0, c == CH - 1,
                           reads=[XMF, seg], writes=[pl])
                s.op("dve", lambda e, pl=pl: e.reduce_max(RM[:], pl[:],
                     axis=AX.X), reads=[pl], writes=[RM])
                s.op("dve", lambda e, q=q: e.tensor_tensor(
                    M4[:, q:q + 1], M4[:, q:q + 1], RM[:], ALU.max),
                    reads=[(M4, q), RM], writes=[(M4, q)])
        for q in range(4):
            pq2 = ctx.pb[2]
            ctx.mm(pq2[0:1, 0:128], M4[:, q:q + 1], EYEF[:], True, True,
                   reads=[(M4, q), EYEF], writes=[pq2])
            ctx.act(LSETR[0:1, q * 128:(q + 1) * 128], pq2[0:1, 0:128],
                    AF.Copy, reads=[pq2], writes=[(LSETR, q)])
        ctx.act(HIB[:], LSETR[:], AF.Copy,
                reads=[(LSETR, q_) for q_ in range(4)], writes=[HIB])
        ctx.act(HIF[:], HIB[:], AF.Copy, reads=[HIB], writes=[HIF])
        s.op("dve", lambda e: e.tensor_tensor(
            HIF[:], LSETR[:], HIF[:], ALU.subtract),
            reads=[(LSETR, q_) for q_ in range(4)] + [HIF], writes=[HIF])
        ctx.act(LOB[:], HIF[:], AF.Copy, reads=[HIF], writes=[LOB])
        pq2 = ctx.pb[3]
        ctx.mm(pq2[:], ctx.ones_row_b[:], HIB[:], True, False,
               reads=[ctx.ones_row_b, HIB], writes=[pq2])
        ctx.mm(pq2[:], ctx.ones_row_b[:], LOB[:], False, True,
               reads=[ctx.ones_row_b, LOB], writes=[pq2])
        ctx.act(BCT, pq2[:], AF.Copy, reads=[pq2],
                writes=[(ctx.xtf[0], "bct")])
        for nb in range(32):
            seb = SEB[nb % 2]
            s.dma("qsp", seb[:], seblk(nb), reads=[d["se"]], writes=[seb])
            pl = ctx.pb[nb % 2]
            for c in range(CH):
                ctx.mm(pl[:], seb[:, c, :], XMF[:, c, tsl], c == 0,
                       c == CH - 1, reads=[seb, XMF], writes=[pl])
            s.op("dve", lambda e, pl=pl: e.tensor_tensor(
                CTF, pl[:], BCT, ALU.subtract),
                reads=[pl, (ctx.xtf[0], "bct")], writes=[(ctx.xtf[0], "ctf")])
            ctx.act(CS(nb), CTF, AF.Exp, reads=[(ctx.xtf[0], "ctf")],
                    writes=[(XMT, "cs", nb)])
            ctx.mm(ctx.pb[2][0:1, :], ctx.ones_col_b[:], CS(nb), nb == 0,
                   nb == 31, reads=[(XMT, "cs", nb), ctx.ones_col_b],
                   writes=[ctx.pb[2]])
        ctx.act(HIF[:], ctx.pb[2][0:1, :], AF.Copy, reads=[ctx.pb[2]],
                writes=[HIF])
        s.op("dve", lambda e: e.reciprocal(HIF[:], HIF[:]), reads=[HIF],
             writes=[HIF])
        ctx.act(HIB[:], HIF[:], AF.Copy, reads=[HIF], writes=[HIB])
        ctx.act(LSETR[:], HIB[:], AF.Copy, reads=[HIB], writes=[LSETR])
        s.op("dve", lambda e: e.tensor_tensor(
            LSETR[:], HIF[:], LSETR[:], ALU.subtract),
            reads=[HIF, LSETR], writes=[LSETR])
        ctx.act(LOB[:], LSETR[:], AF.Copy, reads=[LSETR], writes=[LOB])
        pq2 = ctx.pb[3]
        ctx.mm(pq2[:], ctx.ones_row_b[:], HIB[:], True, False,
               reads=[ctx.ones_row_b, HIB], writes=[pq2])
        ctx.mm(pq2[:], ctx.ones_row_b[:], LOB[:], False, True,
               reads=[ctx.ones_row_b, LOB], writes=[pq2])
        ctx.act(RZB, pq2[:], AF.Copy, reads=[pq2],
                writes=[(ctx.xtf[0], "rzb")])
        for nb in range(32):
            prb = PRB[nb % 2]
            s.dma("qsp", prb, d["proc"][nb * 128:(nb + 1) * 128, :],
                  reads=[d["proc"]],
                  writes=[(ctx.xtf[0], "prb", nb % 2, 0),
                          (ctx.xtf[0], "prb", nb % 2, 1)])
            for dc in range(CH):
                ap = (ctx.pb[dc][:] if dc < 6 else
                      ctx.pbw[:, (dc - 6) * 512:(dc - 5) * 512])
                ctx.mm(ap, prb[:, dc * 128:(dc + 1) * 128], CS(nb),
                       nb == 0, nb == 31,
                       reads=[(ctx.xtf[0], "prb", nb % 2, 0),
                              (ctx.xtf[0], "prb", nb % 2, 1), (XMT, "cs", nb)],
                       writes=[ctx.pb[dc]] if dc < 6 else [(ctx.pbw, dc - 6)])
        xt = ctx.xtf[1]
        s.dma("qsp", xt[:], xr[:, :, tsl], reads=[d["x"]], writes=[xt])
        for dc in range(CH):
            ap = (ctx.pb[dc][:] if dc < 6 else
                  ctx.pbw[:, (dc - 6) * 512:(dc - 5) * 512])
            s.op("dve", lambda e, ap=ap: e.tensor_tensor(
                CTF, ap, RZB, ALU.mult),
                reads=[(ctx.xtf[0], "rzb")] +
                ([ctx.pb[dc]] if dc < 6 else [(ctx.pbw, dc - 6)]),
                writes=[(ctx.xtf[0], "ctf")])
            ctx.tt(xt[:, dc, :], xt[:, dc, :], CTF, ALU.add,
                   reads=[xt, (ctx.xtf[0], "ctf")], writes=[xt])
        s.dma("qsp", xor_[:, :, tsl], xt[:], reads=[xt], writes=[d["xo"]])


def _build_l2f():
    nc = _mk_nc()
    d = {}
    d["x"] = nc.dram_tensor("x", [D, T], F32, kind="ExternalInput").ap()
    d["xo"] = nc.dram_tensor("xo", [D, T], F32, kind="ExternalOutput").ap()
    for nm, sh, dt in [("se", [32, 128, 1024], BF16), ("sef", [D, T], BF16),
                       ("sl", [T, D], BF16),
                       ("proc", [T, D], BF16), ("w1b", [32, 128, 1024], BF16),
                       ("w2b", [4096, D], BF16)]:
        d[nm] = nc.dram_tensor(nm, sh, dt, kind="Internal").ap()
    d["seT"] = nc.dram_tensor("seT", [D, T], F32, kind="ExternalInput").ap()
    d["w1"] = nc.dram_tensor("w1", [D, 4096], F32, kind="ExternalInput").ap()
    d["w2"] = nc.dram_tensor("w2", [4096, D], F32, kind="ExternalInput").ap()
    d["cols"] = nc.dram_tensor("cols", [128, 16], F32, kind="ExternalInput").ap()
    d["eyeb"] = nc.dram_tensor("eyeb", [128, 128], BF16, kind="ExternalInput").ap()
    d["eyef"] = nc.dram_tensor("eyef", [128, 128], F32, kind="ExternalInput").ap()
    es = contextlib.ExitStack()
    with es:
        ctx = Ctx(nc, es)
        emit_l2f(ctx, d)
        ctx.s.build(es)
    return nc


def _prep_l2f(inp, p):
    se = np.asarray(inp["L2_ffn_slot_embeds"], np.float32).reshape(T, D)
    p["l2f_seT"] = np.ascontiguousarray(se.T)
    p["l2f_w1"] = np.ascontiguousarray(np.asarray(inp["L2_ffn_w1"], np.float32))
    p["l2f_w2"] = np.ascontiguousarray(np.asarray(inp["L2_ffn_w2"], np.float32))
    cols = np.zeros((128, 16), np.float32)
    cols[:, 0:8] = _col(inp["L2_ffn_gamma_x"])
    cols[:, 8:16] = _col(inp["L2_ffn_gamma_slot"])
    p["l2f_cols"] = cols
    p["l2f_eyeb"] = np.eye(128, dtype=BF)
    p["l2f_eyef"] = np.eye(128, dtype=np.float32)
    return p


def run_l2f(xs, prep):
    n = len(xs)
    in_maps = [{"x": xs[i], **{k: prep[f"l2f_{k}"] for k in
                ["seT", "w1", "w2", "cols", "eyeb", "eyef"]}}
               for i in range(n)]
    res = _run_phase("l2f", _build_l2f, in_maps)
    return [np.asarray(res.results[i]["xo"]) for i in range(n)]


def _prep_shared(inp):
    p = {}
    p["l0f_wgT"] = np.ascontiguousarray(np.asarray(inp["L0_ffn_w_gate"]).T)
    p["l0f_wuT"] = np.ascontiguousarray(np.asarray(inp["L0_ffn_w_up"]).T)
    p["l0f_wdT"] = np.ascontiguousarray(np.asarray(inp["L0_ffn_w_down"]).T)
    return p


def _mk_nc():
    return bass.Bass("TRN2", target_bir_lowering=False, debug=False)


def _build_l0f():
    nc = _mk_nc()
    d = {}
    d["x"] = nc.dram_tensor("x", [D, T], F32, kind="ExternalInput").ap()
    d["xo"] = nc.dram_tensor("xo", [D, T], F32, kind="ExternalOutput").ap()
    d["xn"] = nc.dram_tensor("xn", [D, T], BF16, kind="Internal").ap()
    d["h"] = nc.dram_tensor("h", [3072, T], BF16, kind="Internal").ap()
    d["wgT"] = nc.dram_tensor("wgT", [D, 3072], I32, kind="ExternalInput").ap()
    d["wuT"] = nc.dram_tensor("wuT", [D, 3072], I32, kind="ExternalInput").ap()
    d["wdT"] = nc.dram_tensor("wdT", [3072, D], I32, kind="ExternalInput").ap()
    es = contextlib.ExitStack()
    with es:
        ctx = Ctx(nc, es)
        emit_l0f(ctx, d)
        ctx.s.build(es)
    return nc


_PROG_CACHE = {}


def _run_phase(name, builder, in_maps):
    if name not in _PROG_CACHE:
        _PROG_CACHE[name] = builder()
    return run_bass_kernel_spmd(_PROG_CACHE[name], in_maps,
                                core_ids=list(range(len(in_maps))))


def run_l0f(xs, prep):
    n = len(xs)
    in_maps = [{"x": xs[i], "wgT": prep["l0f_wgT"], "wuT": prep["l0f_wuT"],
                "wdT": prep["l0f_wdT"]} for i in range(n)]
    res = _run_phase("l0f", _build_l0f, in_maps)
    return [np.asarray(res.results[i]["xo"]) for i in range(n)]


_PH_INPUTS = {
    "l0a": [("wT", [D, 3072], F32), ("qkcols", [128, 16], F32),
            ("vrow", [1, 1024], F32), ("mask", [512, 512], BF16)],
    "l0f": [("wgT", [D, 3072], I32), ("wuT", [D, 3072], I32),
            ("wdT", [3072, D], I32)],
    "l1a": [("wq", [D, D], F32), ("wk", [D, D], F32), ("wv", [D, D], F32),
            ("wo", [D, D], F32), ("w1", [D, 4096], F32), ("w2", [4096, D], F32),
            ("cols", [128, 64], F32), ("cols2", [128, 32], F32),
            ("brow", [1, 2048], F32)],
    "l1f": [("wg", [8, D, 2048], F32), ("wu", [8, D, 2048], F32),
            ("wd", [8, 2048, D], F32), ("swg", [D, 2048], F32),
            ("swu", [D, 2048], F32), ("swd", [2048, D], F32),
            ("gwT", [D, 8], F32), ("lns", [128, 16], F32),
            ("eyeb", [128, 128], BF16), ("units", [8, 1024], F32),
            ("eyef", [128, 128], F32)],
    "l2a": [("uvwT", [D, 4224], F32), ("owT", [2048, D], F32),
            ("cols", [128, 32], F32), ("ucol", [128, 16], F32),
            ("vbrow", [1, 2048], F32), ("ropec", [128, T], F32),
            ("ropes", [128, T], F32), ("swp", [128, 128], BF16),
            ("bias", [T, T], F32)],
    "l2f": [("seT", [D, T], F32), ("w1", [D, 4096], F32),
            ("w2", [4096, D], F32), ("cols", [128, 16], F32),
            ("eyeb", [128, 128], BF16), ("eyef", [128, 128], F32)],
    "l3a": [("wqkvT", [D, 3072], F32), ("woutT", [D, D], F32),
            ("proj", [D, 128], F32), ("cols", [128, 8], F32),
            ("eyeb", [128, 128], BF16)],
    "l3f": [("w1", [32, D, 256], F32), ("w2", [32, 256, D], F32),
            ("rwT", [D, 32], F32), ("rbr", [1, 32], F32),
            ("b1a", [128, 64], F32), ("b2a", [32, D], F32),
            ("eyef", [128, 128], F32)],
}
_PH_SCRATCH = {
    "l0a": [("xn", [D, T], BF16), ("q", [D, T], BF16), ("k", [D, T], BF16),
            ("y", [D, T], BF16)],
    "l0f": [("xn", [D, T], BF16), ("h", [3072, T], BF16)],
    "l1a": [("xn", [D, T], BF16), ("q", [D, T], BF16), ("k", [T, D], BF16),
            ("v", [T, D], BF16), ("a", [D, T], BF16), ("t1", [D, T], BF16),
            ("u", [D, T], BF16), ("h1", [4096, T], BF16),
            ("v2", [D, T], BF16), ("s2", [D, T], BF16)],
    "l1f": [("xn", [D, T], BF16), ("hs", [2048, 9 * T], BF16)],
    "l2a": [("xn", [D, T], BF16), ("xn2", [D, T], BF16), ("u", [2048, T], BF16),
            ("v", [T, 2048], BF16), ("p2", [2048, T], BF16)],
    "l2f": [("se", [32, 128, 1024], BF16), ("sef", [D, T], BF16),
            ("sl", [T, D], BF16),
            ("proc", [T, D], BF16), ("w1b", [32, 128, 1024], BF16),
            ("w2b", [4096, D], BF16)],
    "l3a": [("xn", [D, T], BF16), ("q", [D, T], BF16), ("k", [D, T], BF16),
            ("v", [T, D], BF16), ("a", [D, T], BF16)],
    "l3f": [("xn", [D, T], BF16)],
}
_PH_EMIT = None


def _build_fused():
    nc = _mk_nc()
    emits = {"l0a": emit_l0a, "l0f": emit_l0f, "l1a": emit_l1a,
             "l1f": emit_l1f, "l2a": emit_l2a, "l2f": emit_l2f,
             "l3a": emit_l3a, "l3f": emit_l3f}
    order = ["l0a", "l0f", "l1a", "l1f", "l2a", "l2f", "l3a", "l3f"]
    X = [nc.dram_tensor("x", [D, T], F32, kind="ExternalInput").ap()]
    for i in range(7):
        X.append(nc.dram_tensor(f"xmid{i}", [D, T], F32, kind="Internal").ap())
    XO = nc.dram_tensor("xo", [T, D], F32, kind="ExternalOutput").ap()
    ds = {}
    for ph in order:
        dd = {}
        for nm, sh, dt in _PH_INPUTS[ph]:
            dd[nm] = nc.dram_tensor(f"i_{ph}_{nm}", sh, dt,
                                    kind="ExternalInput").ap()
        for nm, sh, dt in _PH_SCRATCH[ph]:
            dd[nm] = nc.dram_tensor(f"{ph}s_{nm}", sh, dt, kind="Internal").ap()
        ds[ph] = dd
    es = contextlib.ExitStack()
    with es:
        ctx = Ctx(nc, es)
        for v in (0.0, 1e-5, 1e-6, 1e-8):
            ctx.constcol(v)
        ctx.lnrow(0)
        for i, ph in enumerate(order):
            dd = ds[ph]
            dd["x"] = X[i]
            dd["xo"] = XO if ph == "l3f" else X[i + 1]
            ctx.s.fence()
            with contextlib.ExitStack() as pes:
                ctx.es = pes
                emits[ph](ctx, dd)
        ctx.es = es
        ctx.s.build(es)
    return nc


def run_fused(xs, prep):
    n = len(xs)
    in_maps = []
    for i in range(n):
        m = {"x": xs[i]}
        for ph, keys in _PH_INPUTS.items():
            for nm, _, _ in keys:
                m[f"i_{ph}_{nm}"] = prep[f"{ph}_{nm}"]
        in_maps.append(m)
    res = _run_phase("fused", _build_fused, in_maps)
    return [np.asarray(res.results[i]["xo"]) for i in range(n)]


def kernel(**inp):
    prep = _prep_shared(inp)
    _prep_l0a(inp, prep)
    _prep_l1a(inp, prep)
    _prep_l1f(inp, prep)
    _prep_l2a(inp, prep)
    _prep_l2f(inp, prep)
    _prep_l3a(inp, prep)
    _prep_l3f(inp, prep)
    x = np.asarray(inp["L0_attn_x"])
    xs = [np.ascontiguousarray(x[i].T).astype(np.float32) for i in range(NC)]
    xs = run_fused(xs, prep)
    return np.ascontiguousarray(np.stack(xs)).astype(np.float32)


# version 10 (final)
# speedup vs baseline: 1.1396x; 1.0250x over previous
import contextlib
import numpy as np
import ml_dtypes
import concourse.bass as bass
import concourse.mybir as mybir
from concourse.bass_utils import run_bass_kernel_spmd

F32 = mybir.dt.float32
BF16 = mybir.dt.bfloat16
I32 = mybir.dt.int32
AF = mybir.ActivationFunctionType
ALU = mybir.AluOpType
AX = mybir.AxisListType
BF = ml_dtypes.bfloat16

D = 1024
T = 4096
NC = 8
CH = D // 128

ENGS = ("pe", "act", "dve", "pool", "sp")
QUEUES = {"qsp": "sp", "qact": "act", "qpool": "pool"}


def _key(b):
    if isinstance(b, (str, tuple)):
        return b
    n = getattr(b, "name", None)
    return n if n else id(b)


class Sched:

    def __init__(self, nc):
        self.nc = nc
        self.ops = []
        self.deps = []
        self.state = {}
        self.strict_deps = []
        self.children = {}
        self.last_per_chan = {}
        self._fence = []

    def fence(self):
        self._fence = list(self.last_per_chan.values())

    @staticmethod
    def _parse(b):
        if isinstance(b, tuple) and len(b) >= 2 and not isinstance(b[0], str):
            pk = _key(b[0])
            return pk, (pk,) + tuple(b[1:])
        return None, _key(b)

    def _rd_deps(self, pk, k):
        dd = []
        st = self.state.get(k)
        if st:
            dd.extend(st[0])
        if pk is not None:
            st = self.state.get(pk)
            if st:
                dd.extend(st[0])
        else:
            for ck in self.children.get(k, ()):
                st = self.state.get(ck)
                if st:
                    dd.extend(st[0])
        return dd

    def _wr_deps(self, pk, k):
        dd = []
        for kk in ([k, pk] if pk is not None else
                   [k] + list(self.children.get(k, ()))):
            if kk is None:
                continue
            st = self.state.get(kk)
            if st:
                dd.extend(st[1] if st[1] else st[0])
        return dd

    def op(self, eng, fn, reads=(), writes=(), queue=None, extra_deps=(),
           strict=()):
        i = len(self.ops)
        dd = list(extra_deps) + self._fence
        sd = []
        pr = [self._parse(b) for b in reads]
        pw = [self._parse(b) for b in writes]
        for pk, k in pr:
            dd.extend(self._rd_deps(pk, k))
        for b in strict:
            pk, k = self._parse(b)
            ds = self._rd_deps(pk, k)
            dd.extend(ds)
            sd.extend(ds)
            self.state.setdefault(k, ([], []))[1].append(i)
        for pk, k in pw:
            dd.extend(self._wr_deps(pk, k))
        self.ops.append((eng, fn, queue))
        self.deps.append(sorted({d for d in dd if d is not None}))
        self.strict_deps.append({d for d in sd if d is not None})
        self.last_per_chan[queue if queue else eng] = i
        for pk, k in pr:
            self.state.setdefault(k, ([], []))[1].append(i)
        for pk, k in pw:
            self.state[k] = ([i], [])
            if pk is not None:
                self.children.setdefault(pk, set()).add(k)
            else:
                for ck in self.children.pop(k, ()):
                    self.state.pop(ck, None)
        return i

    def dma(self, queue, out, in_, reads=(), writes=(), **kw):
        if queue == "qsp":
            self._rr = getattr(self, "_rr", 0) + 1
            if self._rr % 2:
                queue = "qact"
        def fn(e, out=out, in_=in_, kw=kw):
            return e.dma_start(out=out, in_=in_, **kw)
        return self.op(QUEUES[queue], fn, reads=reads, writes=writes, queue=queue)

    def build(self, es):
        nc = self.nc
        n = len(self.ops)
        chan = [(q if q else e) for (e, f, q) in self.ops]
        is_dma = [bool(q) for (e, f, q) in self.ops]
        need_inc = list(is_dma)
        for i in range(n):
            for d in self.deps[i]:
                if not (chan[d] == chan[i] == "pe" and not is_dma[d]
                        and not is_dma[i]):
                    need_inc[d] = True
        NSLOT = 8
        chans = sorted(set(chan)) or ["sp"]
        sems = {}
        for c in chans:
            if c in QUEUES:
                for sl in range(NSLOT):
                    sems[(c, sl)] = es.enter_context(nc.semaphore(f"s_{c}{sl}"))
            else:
                sems[c] = es.enter_context(nc.semaphore(f"s_{c}"))
        cum = {}
        semkey = [None] * n
        val_after = [0] * n
        qidx = {c: 0 for c in chans if c in QUEUES}
        for i in range(n):
            c = chan[i]
            if is_dma[i]:
                k = (c, qidx[c] % NSLOT)
                qidx[c] += 1
                cum[k] = cum.get(k, 0) + 16
                semkey[i] = k
                val_after[i] = cum[k]
            else:
                if need_inc[i]:
                    cum[c] = cum.get(c, 0) + 1
                semkey[i] = c
                val_after[i] = cum.get(c, 0)
        final = dict(cum)
        per_eng = {e: [] for e in ENGS}
        for i in range(n):
            per_eng[self.ops[i][0]].append(i)

        with nc.Block() as blk:
            def body(eng_name, eobj):
                last_wait = {}
                for i in per_eng[eng_name]:
                    for d in self.deps[i]:
                        if (chan[d] == chan[i] == "pe" and not is_dma[d]
                                and not is_dma[i]):
                            continue
                        sk = semkey[d]
                        v = val_after[d]
                        if v > last_wait.get(sk, 0):
                            eobj.wait_ge(sems[sk], v)
                            last_wait[sk] = v
                    try:
                        inst = self.ops[i][1](eobj)
                    except AssertionError as _ex:
                        raise AssertionError(f"op {i} eng {eng_name}: {_ex}")
                    if need_inc[i]:
                        inst.then_inc(sems[semkey[i]], 16 if is_dma[i] else 1)
                for (sk, v) in final.items():
                    if isinstance(sk, tuple) and QUEUES.get(sk[0]) == eng_name:
                        eobj.wait_ge(sems[sk], v)

            @blk.tensor
            def _(te):
                body("pe", te)

            @blk.scalar
            def _(sc):
                body("act", sc)

            @blk.vector
            def _(ve):
                body("dve", ve)

            @blk.gpsimd
            def _(gp):
                body("pool", gp)

            @blk.sync
            def _(sy):
                body("sp", sy)


class Ctx:
    def __init__(self, nc, es, slim=False):
        self.nc = nc
        self.es = es
        self.s = Sched(nc)
        self.pb = [es.enter_context(nc.psum_tensor(f"pb{i}", [128, 512], F32))
                   for i in range(6)]
        self.pbw = es.enter_context(nc.psum_tensor("pbw", [128, 1024], F32))
        self.pb.append(self.pbw[:, 0:512])
        self.pb.append(self.pbw[:, 512:1024])
        self.ones_col_f = self.sbuf("ones_col_f", [128, 1], F32)
        self.ones_row_f = self.sbuf("ones_row_f", [1, 128], F32)
        self.ones_col_b = self.sbuf("ones_col_b", [128, 1], BF16)
        self.ones_row_b = self.sbuf("ones_row_b", [1, 128], BF16)
        for t in (self.ones_col_f, self.ones_row_f, self.ones_col_b, self.ones_row_b):
            self.s.op("pool", lambda e, t=t: e.memset(t[:], 1.0), writes=[t])
        nx = 1 if slim else 2
        self.xtf = [self.sbuf(f"c_xtf{i}", [128, CH, 512], F32) for i in range(nx)]
        if slim:
            self.xtf.append(self.xtf[0])
        self.sq = self.sbuf("c_sq", [128, CH, 512], BF16)
        self.xnt = [self.sbuf(f"c_xnt{i}", [128, CH, 512], BF16) for i in range(2)]

    def sbuf(self, name, shape, dt):
        return self.es.enter_context(self.nc.sbuf_tensor(name, shape, dt))

    def lnrow(self, i):
        if not hasattr(self, "_lnrows"):
            self._lnrows = [self.sbuf(f"lnrow{j}", [1, 512], F32) for j in range(4)]
        return self._lnrows[i]

    def constcol(self, val, dt=F32):
        if not hasattr(self, "_consts"):
            self._consts = {}
        key = (float(val), dt)
        if key not in self._consts:
            t = self.sbuf(f"cc{len(self._consts)}", [128, 1], dt)
            self.s.op("pool", lambda e, t=t, v=float(val): e.memset(t[:], v),
                      writes=[t])
            self._consts[key] = t
        return self._consts[key]

    def mm(self, out, lhsT, rhs, start, stop, reads, writes, extra_deps=()):
        return self.s.op(
            "pe", lambda e: e.matmul(out, lhsT, rhs, start=start, stop=stop),
            reads=reads, writes=writes, extra_deps=extra_deps)

    def act(self, out, in_, func, reads, writes, bias=None, scale=None):
        kw = {}
        reads = list(reads)
        if bias is not None:
            if isinstance(bias, (int, float)) and func != AF.Copy:
                t = self.constcol(bias)
                bias = t[0:out.shape[0], :]
                reads.append(t)
            kw["bias"] = bias
        if scale is not None:
            kw["scale"] = scale
        return self.s.op(
            "act", lambda e: e.activation(out, in_, func, **kw),
            reads=reads, writes=writes)

    def tt(self, out, a, b, opk, reads, writes, eng="dve"):
        return self.s.op(
            eng, lambda e: e.tensor_tensor(out, a, b, opk),
            reads=reads, writes=writes)


def emit_ln(ctx, x_dram, xn_dram, eps=1e-5, src_dt=F32, gcol=None, bcol=None,
            in_bufs=None):
    s = ctx.s
    TT = 512
    nt = T // TT
    xr = x_dram.rearrange("(c p) t -> p c t", p=128)
    xnr = xn_dram.rearrange("(c p) t -> p c t", p=128)
    XT = ctx.xtf if src_dt == F32 else in_bufs
    ones = ctx.ones_col_f if src_dt == F32 else ctx.ones_col_b
    SQ = ctx.sq
    XNT = ctx.xnt
    mu = ctx.lnrow(0)
    msq = ctx.lnrow(1)
    var = ctx.lnrow(2)
    rstd = ctx.lnrow(3)
    for i in range(nt):
        b = i % 2
        xt, xnt = XT[b], XNT[b]
        ps, pq = ctx.pb[0 + b], ctx.pb[2 + b]
        pmu, prs = ctx.pb[4 + b], ctx.pb[6 + b]
        t0 = i * TT
        s.dma("qsp", xt[:], xr[:, :, t0:t0 + TT], reads=[x_dram], writes=[xt])
        ctx.act(SQ[:, :, :], xt[:, :, :], AF.Square, reads=[xt], writes=[SQ])
        for c in range(CH):
            ctx.mm(ps[0:1, :], ones[:], xt[:, c, :], c == 0, c == CH - 1,
                   reads=[xt, ones], writes=[ps])
        for c in range(CH):
            ctx.mm(pq[0:1, :], ctx.ones_col_b[:], SQ[:, c, :], c == 0, c == CH - 1,
                   reads=[SQ, ctx.ones_col_b], writes=[pq])
        ctx.act(mu[:], ps[0:1, :], AF.Copy, reads=[ps], writes=[mu], scale=1.0 / D)
        ctx.act(msq[:], pq[0:1, :], AF.Copy, reads=[pq], writes=[msq], scale=1.0 / D)
        ctx.tt(var[:], mu[:], mu[:], ALU.mult, reads=[mu], writes=[var])
        ctx.tt(var[:], msq[:], var[:], ALU.subtract, reads=[msq, var], writes=[var])
        ctx.act(var[:], var[:], AF.Sqrt, reads=[var], writes=[var], bias=float(eps))
        s.op("dve", lambda e, o=rstd, a=var: e.reciprocal(o[:], a[:]),
             reads=[var], writes=[rstd])
        ctx.mm(pmu[:], ctx.ones_row_f[:], mu[:], True, True,
               reads=[mu, ctx.ones_row_f], writes=[pmu])
        ctx.mm(prs[:], ctx.ones_row_f[:], rstd[:], True, True,
               reads=[rstd, ctx.ones_row_f], writes=[prs])
        for c in range(CH):
            ctx.tt(xnt[:, c, :], xt[:, c, :], pmu[:], ALU.subtract,
                   reads=[xt, pmu], writes=[xnt])
        for c in range(CH):
            ctx.tt(xnt[:, c, :], xnt[:, c, :], prs[:], ALU.mult,
                   reads=[xnt, prs], writes=[xnt])
        if gcol is not None:
            gb_, go_ = gcol
            bb_, bo_ = bcol
            for c in range(CH):
                s.op("dve", lambda e, o=xnt, c=c: e.tensor_scalar(
                    o[:, c, :], o[:, c, :], gb_[:, go_ + c:go_ + c + 1],
                    bb_[:, bo_ + c:bo_ + c + 1], ALU.mult, ALU.add),
                    reads=[xnt, gb_, bb_], writes=[xnt])
        s.dma("qsp", xnr[:, :, t0:t0 + TT], xnt[:], reads=[xnt], writes=[xn_dram])


def emit_resadd(ctx, x_dram, s_dram, xo_dram):
    s = ctx.s
    xr = x_dram.rearrange("(c p) t -> p c t", p=128)
    sr = s_dram.rearrange("(c p) t -> p c t", p=128)
    xor_ = xo_dram.rearrange("(c p) t -> p c t", p=128)
    for i in range(T // 512):
        b = i % 2
        xt = ctx.xtf[b]
        st = ctx.xnt[b]
        tsl = slice(i * 512, (i + 1) * 512)
        s.dma("qsp", xt[:], xr[:, :, tsl], reads=[x_dram], writes=[xt])
        s.dma("qsp", st[:], sr[:, :, tsl], reads=[s_dram], writes=[st])
        for c in range(CH):
            ctx.tt(xt[:, c, :], xt[:, c, :], st[:, c, :], ALU.add,
                   reads=[xt, st], writes=[xt])
        s.dma("qsp", xor_[:, :, tsl], xt[:], reads=[xt], writes=[xo_dram])


def emit_l0f(ctx, d):
    s = ctx.s
    HID = 3072
    NHC = HID // 128
    SC = 1.0 / 127.0
    xnr = d["xn"].rearrange("(c p) t -> p c t", p=128)
    xr = d["x"].rearrange("(c p) t -> p c t", p=128)
    xor_ = d["xo"].rearrange("(c p) t -> p c t", p=128)
    wgTr = d["wgT"].rearrange("(c p) h -> p c h", p=128)
    wuTr = d["wuT"].rearrange("(c p) h -> p c h", p=128)
    wdTr = d["wdT"].rearrange("(h p) o -> p h o", p=128)
    hr = d["h"].rearrange("(hc p) t -> p hc t", p=128)

    WD = ctx.sbuf("l0f_wd", [128, NHC, D], BF16)
    for g in range(4):
        s.dma("qpool", WD[:, g * 6:(g + 1) * 6, :], wdTr[:, g * 6:(g + 1) * 6, :],
              writes=[WD])
    emit_ln(ctx, d["x"], d["xn"])
    WG = [ctx.sbuf(f"l0f_wg{i}", [128, CH, 512], BF16) for i in range(2)]
    WU = [ctx.sbuf(f"l0f_wu{i}", [128, CH, 512], BF16) for i in range(2)]
    HB = [ctx.sbuf(f"l0f_hb{i}", [128, 4, 512], BF16) for i in range(2)]
    GS = [ctx.sbuf(f"l0f_gs{i}", [128, 512], BF16) for i in range(2)]
    US = [ctx.sbuf(f"l0f_us{i}", [128, 512], BF16) for i in range(2)]
    OUT = [ctx.sbuf(f"l0f_out{i}", [128, 512], F32) for i in range(2)]
    HT = [ctx.sbuf("l0f_ht0", [128, NHC, 512], BF16)]

    for hg in range(HID // 512):
        wg, wu = WG[hg % 2], WU[hg % 2]
        s.dma("qpool", wg[:], wgTr[:, :, hg * 512:(hg + 1) * 512], writes=[wg])
        s.dma("qpool", wu[:], wuTr[:, :, hg * 512:(hg + 1) * 512], writes=[wu])
        for it in range(T // 512):
            tsl = slice(it * 512, (it + 1) * 512)
            xn = ctx.xnt[it % 2]
            s.dma("qsp", xn[:], xnr[:, :, tsl], reads=[d["xn"]], writes=[xn])
            hb = HB[it % 2]
            for hi in range(4):
                pg, pu = ctx.pb[hi % 2], ctx.pb[2 + hi % 2]
                gs, us = GS[hi % 2], US[hi % 2]
                hsl = slice(hi * 128, hi * 128 + 128)
                for c in range(CH):
                    ctx.mm(pg[:], wg[:, c, hsl], xn[:, c, :], c == 0, c == CH - 1,
                           reads=[wg, xn], writes=[pg])
                for c in range(CH):
                    ctx.mm(pu[:], wu[:, c, hsl], xn[:, c, :], c == 0, c == CH - 1,
                           reads=[wu, xn], writes=[pu])
                ctx.act(gs[:], pg[:], AF.Silu, reads=[pg], writes=[gs], scale=SC)
                ctx.act(us[:], pu[:], AF.Copy, reads=[pu], writes=[us], scale=SC)
                ctx.tt(hb[:, hi, :], gs[:], us[:], ALU.mult,
                       reads=[gs, us], writes=[hb])
            s.dma("qsp", hr[:, hg * 4:(hg + 1) * 4, tsl], hb[:],
                  reads=[hb], writes=[d["h"]])

    for it in range(T // 512):
        tsl = slice(it * 512, (it + 1) * 512)
        ht = HT[0]
        xt = ctx.xtf[it % 2]
        s.dma("qsp", ht[:], hr[:, :, tsl], reads=[d["h"]], writes=[ht])
        s.dma("qsp", xt[:], xr[:, :, tsl], reads=[d["x"]], writes=[xt])
        for dc in range(CH):
            pd = ctx.pb[4 + dc % 2]
            out = OUT[dc % 2]
            for hc in range(NHC):
                ctx.mm(pd[:], WD[:, hc, dc * 128:(dc + 1) * 128], ht[:, hc, :],
                       hc == 0, hc == NHC - 1, reads=[WD, ht], writes=[pd])
            s.op("dve", lambda e, o=out, a=pd, x=xt, dc=dc: e.scalar_tensor_tensor(
                o[:], a[:], SC, x[:, dc, :], ALU.mult, ALU.add),
                reads=[pd, xt], writes=[out])
            s.dma("qsp", xor_[:, dc, tsl], out[:], reads=[out], writes=[d["xo"]])


def emit_l1a(ctx, d):
    s = ctx.s
    NH = 16
    emit_ln(ctx, d["x"], d["xn"])
    xnr = d["xn"].rearrange("(c p) t -> p c t", p=128)
    qr = d["q"]
    ar = d["a"].rearrange("(c p) t -> p c t", p=128)
    t1r = d["t1"].rearrange("(c p) t -> p c t", p=128)
    ur = d["u"].rearrange("(c p) t -> p c t", p=128)
    h1r = d["h1"].rearrange("(h p) t -> p h t", p=128)
    v2r = d["v2"].rearrange("(c p) t -> p c t", p=128)

    WBUF = ctx.sbuf("l1a_wbuf", [128, 32, 1024], BF16)
    XBT = [ctx.sbuf(f"l1a_xbt{i}", [128, CH, 512], BF16) for i in range(2)]
    COLS = ctx.sbuf("l1a_cols", [128, 64], F32)
    B1C = ctx.sbuf("l1a_b1c", [128, 32], F32)
    BROW = ctx.sbuf("l1a_brow", [1, 2048], BF16)
    TMP = [ctx.sbuf(f"l1a_tmp{i}", [128, 512], F32) for i in range(4)]
    OUTB = [ctx.sbuf(f"l1a_ob{i}", [128, 512], BF16) for i in range(2)]
    KVS = ctx.sbuf("l1a_kvs", [64, 1024], BF16)
    KSC = ctx.sbuf("l1a_ksc", [64, 16], BF16)
    ZR = ctx.sbuf("l1a_zr", [1, 512], F32)
    HST = ctx.sbuf("l1a_hst", [128, 32, 512], BF16)

    s.dma("qpool", COLS[:], d["cols"][:], writes=[COLS])
    s.dma("qpool", B1C[:], d["cols2"][:], writes=[B1C])
    s.dma("qpool", BROW[:], d["brow"][:], writes=[BROW])
    bq_c = lambda oc: COLS[:, oc:oc + 1]
    bo_c = lambda oc: COLS[:, 8 + oc:9 + oc]
    b2_c = lambda oc: COLS[:, 16 + oc:17 + oc]

    def load_w(name, pat):
        wr = d[name].rearrange(pat, p=128)
        nch = wr.shape[1]
        step = max(1, nch // 4)
        wf = WBUF[:, :, :].rearrange("p a o -> p (a o)")
        width = wr.shape[2]
        for g in range(0, nch, step):
            dst = wf[:, g * width:(g + step) * width].rearrange(
                "p (a b) -> p a b", b=width)
            s.dma("qpool", dst, wr[:, g:g + step, :], writes=[WBUF])

    for i, nm in enumerate(["wq", "wk", "wv", "wo"]):
        wr = d[nm].rearrange("(c p) o -> p c o", p=128)
        dst = WBUF[:, 8 * i:8 * i + 8, :]
        s.dma("qpool", dst, wr, writes=[WBUF])
    WQ = WBUF[:, 0:8, :]
    WK = WBUF[:, 8:16, :]
    WV = WBUF[:, 16:24, :]
    WO = WBUF[:, 24:32, :]
    for it in range(8):
        tsl = slice(it * 512, (it + 1) * 512)
        xn = XBT[it % 2]
        s.dma("qsp", xn[:], xnr[:, :, tsl], reads=[d["xn"]], writes=[xn])
        for oc in range(CH):
            pg = ctx.pb[oc % 2]
            for c in range(CH):
                ctx.mm(pg[:], WQ[:, c, oc * 128:(oc + 1) * 128], xn[:, c, :],
                       c == 0, c == CH - 1, reads=[WBUF, xn], writes=[pg])
            t1_, t2_ = TMP[2 * (oc % 2)], TMP[2 * (oc % 2) + 1]
            s.op("dve", lambda e, p=pg, t=t1_, oc=oc: e.tensor_scalar(
                t[:], p[:], bq_c(oc), 0.0, ALU.add, ALU.min),
                reads=[pg, COLS], writes=[t1_])
            ctx.act(t1_[:], t1_[:], AF.Exp, reads=[t1_], writes=[t1_])
            s.op("dve", lambda e, p=pg, t=t2_, oc=oc: e.tensor_scalar(
                t[:], p[:], bq_c(oc), 0.0, ALU.add, ALU.max),
                reads=[pg, COLS], writes=[t2_])
            ob = OUTB[oc % 2]
            ctx.tt(ob[:], t1_[:], t2_[:], ALU.add, reads=[t1_, t2_], writes=[ob])
            s.dma("qsp", qr[oc * 128:(oc + 1) * 128, tsl], ob[:],
                  reads=[ob], writes=[d["q"]])
        for st_ in range(4):
            ssl = slice(st_ * 128, st_ * 128 + 128)
            trow = it * 512 + st_ * 128
            for mh in range(2):
                msl = slice(mh * 512, mh * 512 + 512)
                pk, pv = ctx.pb[2 + mh], ctx.pb[4 + mh]
                for c in range(CH):
                    ctx.mm(pk[:], xn[:, c, ssl], WK[:, c, msl], c == 0, False,
                           reads=[xn, WBUF], writes=[pk])
                ctx.mm(pk[:], ctx.ones_row_b[:], BROW[0:1, msl], False, True,
                       reads=[ctx.ones_row_b, BROW], writes=[pk])
                for c in range(CH):
                    ctx.mm(pv[:], xn[:, c, ssl], WV[:, c, msl], c == 0, False,
                           reads=[xn, WBUF], writes=[pv])
                ctx.mm(pv[:], ctx.ones_row_b[:],
                       BROW[0:1, 1024 + mh * 512:1536 + mh * 512], False, True,
                       reads=[ctx.ones_row_b, BROW], writes=[pv])
                t1_, t2_ = TMP[0], TMP[1]
                s.op("dve", lambda e, p=pk, t=t1_: e.tensor_scalar(
                    t[:], p[:], 0.0, None, ALU.min), reads=[pk], writes=[t1_])
                ctx.act(t1_[:], t1_[:], AF.Exp, reads=[t1_], writes=[t1_])
                s.op("dve", lambda e, p=pk, t=t2_: e.tensor_scalar(
                    t[:], p[:], 0.0, None, ALU.max), reads=[pk], writes=[t2_])
                okb, ovb = OUTB[0], OUTB[1]
                ctx.tt(okb[:], t1_[:], t2_[:], ALU.add,
                       reads=[t1_, t2_], writes=[okb])
                ctx.act(ovb[:], pv[:], AF.Copy, reads=[pv], writes=[ovb])
                s.dma("qsp", d["k"][trow:trow + 128, msl], okb[:],
                      reads=[okb], writes=[d["k"]])
                s.dma("qsp", d["v"][trow:trow + 128, msl], ovb[:],
                      reads=[ovb], writes=[d["v"]])

    KT = [ctx.sbuf(f"l1a_kt{i}", [128, 1024], BF16) for i in range(2)]
    VT = [ctx.sbuf(f"l1a_vt{i}", [128, 1024], BF16) for i in range(2)]
    pkv = [ctx.pb[0], ctx.pb[1]]
    pks = ctx.pb[2]
    for tc in range(32):
        kt, vt = KT[tc % 2], VT[tc % 2]
        s.dma("qsp", kt[:], d["k"][tc * 128:(tc + 1) * 128, :],
              reads=[d["k"]], writes=[kt])
        s.dma("qsp", vt[:], d["v"][tc * 128:(tc + 1) * 128, :],
              reads=[d["v"]], writes=[vt])
        for h in range(NH):
            hs = slice(h * 64, h * 64 + 64)
            po = pkv[h // 8][0:64, (h % 8) * 64:(h % 8) * 64 + 64]
            ctx.mm(po, kt[:, hs], vt[:, hs], tc == 0, tc == 31,
                   reads=[kt, vt], writes=[pkv[h // 8]])
            ctx.mm(pks[0:64, h:h + 1], kt[:, hs], ctx.ones_col_b[:],
                   tc == 0, tc == 31, reads=[kt, ctx.ones_col_b], writes=[pks])
    for h in range(NH):
        po = pkv[h // 8][0:64, (h % 8) * 64:(h % 8) * 64 + 64]
        ctx.act(KVS[:, h * 64:(h + 1) * 64], po, AF.Copy,
                reads=[pkv[h // 8]], writes=[KVS])
    ctx.act(KSC[:], pks[0:64, 0:16], AF.Copy, reads=[pks], writes=[KSC])

    QH = [ctx.sbuf(f"l1a_qh{i}", [64, 512], BF16) for i in range(2)]
    AH = [ctx.sbuf(f"l1a_ah{i}", [64, 512], BF16) for i in range(2)]
    for h in range(NH):
        for it in range(8):
            tsl = slice(it * 512, (it + 1) * 512)
            qh, ah = QH[it % 2], AH[it % 2]
            s.dma("qsp", qh[:], qr[h * 64:h * 64 + 64, tsl],
                  reads=[d["q"]], writes=[qh])
            pa, pz, pzb = ctx.pb[3], ctx.pb[4], ctx.pb[5]
            ctx.mm(pa[0:64, :], KVS[:, h * 64:(h + 1) * 64], qh[:], True, True,
                   reads=[KVS, qh], writes=[pa])
            ctx.mm(pz[0:1, :], KSC[:, h:h + 1], qh[:], True, True,
                   reads=[KSC, qh], writes=[pz])
            ctx.act(ZR[:], pz[0:1, :], AF.Copy, reads=[pz], writes=[ZR], bias=1e-6)
            s.op("dve", lambda e: e.reciprocal(ZR[:], ZR[:]),
                 reads=[ZR], writes=[ZR])
            ctx.mm(pzb[0:64, :], ctx.ones_row_f[0:1, 0:64], ZR[:], True, True,
                   reads=[ZR, ctx.ones_row_f], writes=[pzb])
            ctx.act(TMP[0][0:64, :], pa[0:64, :], AF.Copy,
                    reads=[pa], writes=[TMP[0]])
            s.op("dve", lambda e, ah=ah: e.tensor_tensor(
                ah[:], TMP[0][0:64, :], pzb[0:64, :], ALU.mult),
                reads=[TMP[0], pzb], writes=[ah])
            s.dma("qsp", d["a"][h * 64:h * 64 + 64, tsl], ah[:],
                  reads=[ah], writes=[d["a"]])

    for it in range(8):
        tsl = slice(it * 512, (it + 1) * 512)
        atile = XBT[it % 2]
        xn2 = ctx.xnt[it % 2]
        s.dma("qsp", atile[:], ar[:, :, tsl], reads=[d["a"]], writes=[atile])
        s.dma("qsp", xn2[:], xnr[:, :, tsl], reads=[d["xn"]], writes=[xn2])
        for oc in range(CH):
            po = ctx.pb[6 + oc % 2]
            for c in range(CH):
                ctx.mm(po[:], WO[:, c, oc * 128:(oc + 1) * 128], atile[:, c, :],
                       c == 0, c == CH - 1, reads=[WBUF, atile], writes=[po])
            tm = TMP[oc % 2]
            s.op("dve", lambda e, p=po, t=tm, oc=oc: e.tensor_scalar(
                t[:], p[:], bo_c(oc), None, ALU.add), reads=[po, COLS], writes=[tm])
            ob = OUTB[oc % 2]
            ctx.tt(ob[:], tm[:], xn2[:, oc, :], ALU.add,
                   reads=[tm, xn2], writes=[ob])
            s.dma("qsp", t1r[:, oc, tsl], ob[:], reads=[ob], writes=[d["t1"]])

    emit_ln(ctx, d["t1"], d["u"], src_dt=BF16, gcol=(COLS, 24), bcol=(COLS, 32),
            in_bufs=XBT)

    load_w("w1", "(c p) o -> p c o")
    W1V = WBUF[:, :, :].rearrange("p a o -> p (a o)").rearrange(
        "p (c o) -> p c o", o=4096)
    for it in range(8):
        tsl = slice(it * 512, (it + 1) * 512)
        ut = XBT[it % 2]
        s.dma("qsp", ut[:], ur[:, :, tsl], reads=[d["u"]], writes=[ut])
        for hc in range(32):
            ph = ctx.pb[hc % 2]
            for c in range(CH):
                ctx.mm(ph[:], W1V[:, c, hc * 128:(hc + 1) * 128], ut[:, c, :],
                       c == 0, c == CH - 1, reads=[WBUF, ut], writes=[ph])
            s.op("dve", lambda e, p=ph, hc=hc: e.tensor_scalar(
                HST[:, hc, :], p[:], B1C[:, hc:hc + 1], 0.0, ALU.add, ALU.max),
                reads=[ph, B1C], writes=[HST])
        s.dma("qsp", h1r[:, :, tsl], HST[:], reads=[HST], writes=[d["h1"]])

    load_w("w2", "(h p) o -> p h o")
    for it in range(8):
        tsl = slice(it * 512, (it + 1) * 512)
        ut = XBT[it % 2]
        s.dma("qsp", HST[:], h1r[:, :, tsl], reads=[d["h1"]], writes=[HST])
        s.dma("qsp", ut[:], ur[:, :, tsl], reads=[d["u"]], writes=[ut])
        for oc in range(CH):
            po = ctx.pb[2 + oc % 2]
            for hc in range(32):
                ctx.mm(po[:], WBUF[:, hc, oc * 128:(oc + 1) * 128], HST[:, hc, :],
                       hc == 0, hc == 31, reads=[WBUF, HST], writes=[po])
            tm = TMP[oc % 2]
            s.op("dve", lambda e, p=po, t=tm, oc=oc: e.tensor_scalar(
                t[:], p[:], b2_c(oc), None, ALU.add), reads=[po, COLS], writes=[tm])
            ob = OUTB[oc % 2]
            ctx.tt(ob[:], tm[:], ut[:, oc, :], ALU.add, reads=[tm, ut], writes=[ob])
            s.dma("qsp", v2r[:, oc, tsl], ob[:], reads=[ob], writes=[d["v2"]])

    emit_ln(ctx, d["v2"], d["s2"], src_dt=BF16, gcol=(COLS, 40), bcol=(COLS, 48),
            in_bufs=XBT)
    emit_resadd(ctx, d["x"], d["s2"], d["xo"])


def _build_l1a():
    nc = _mk_nc()
    d = {}
    d["x"] = nc.dram_tensor("x", [D, T], F32, kind="ExternalInput").ap()
    d["xo"] = nc.dram_tensor("xo", [D, T], F32, kind="ExternalOutput").ap()
    for nm, sh, dt in [("xn", [D, T], BF16), ("q", [D, T], BF16),
                       ("k", [T, D], BF16), ("v", [T, D], BF16),
                       ("a", [D, T], BF16), ("t1", [D, T], BF16),
                       ("u", [D, T], BF16), ("h1", [4096, T], BF16),
                       ("v2", [D, T], BF16), ("s2", [D, T], BF16)]:
        d[nm] = nc.dram_tensor(nm, sh, dt, kind="Internal").ap()
    for nm in ["wq", "wk", "wv", "wo"]:
        d[nm] = nc.dram_tensor(nm, [D, D], F32, kind="ExternalInput").ap()
    d["w1"] = nc.dram_tensor("w1", [D, 4096], F32, kind="ExternalInput").ap()
    d["w2"] = nc.dram_tensor("w2", [4096, D], F32, kind="ExternalInput").ap()
    d["cols"] = nc.dram_tensor("cols", [128, 64], F32, kind="ExternalInput").ap()
    d["cols2"] = nc.dram_tensor("cols2", [128, 32], F32, kind="ExternalInput").ap()
    d["brow"] = nc.dram_tensor("brow", [1, 2048], F32, kind="ExternalInput").ap()
    es = contextlib.ExitStack()
    with es:
        ctx = Ctx(nc, es)
        emit_l1a(ctx, d)
        ctx.s.build(es)
    return nc


def _col(v):
    v = np.asarray(v, np.float32)
    return np.ascontiguousarray(v.reshape(-1, 128).T)


def _prep_l1a(inp, p):
    cols = np.zeros((128, 64), np.float32)
    cols[:, 0:8] = _col(inp["L1_attn_bq"])
    cols[:, 8:16] = _col(inp["L1_attn_bo"])
    cols[:, 16:24] = _col(inp["L1_attn_b2"])
    cols[:, 24:32] = _col(inp["L1_attn_g1"])
    cols[:, 32:40] = _col(inp["L1_attn_beta1"])
    cols[:, 40:48] = _col(inp["L1_attn_g2"])
    cols[:, 48:56] = _col(inp["L1_attn_beta2"])
    p["l1a_cols"] = cols
    p["l1a_cols2"] = _col(inp["L1_attn_b1"])
    brow = np.zeros((1, 2048), np.float32)
    brow[0, 0:1024] = np.asarray(inp["L1_attn_bk"])
    brow[0, 1024:2048] = np.asarray(inp["L1_attn_bv"])
    p["l1a_brow"] = brow
    for nm, key in [("wq", "L1_attn_Wq"), ("wk", "L1_attn_Wk"),
                    ("wv", "L1_attn_Wv"), ("wo", "L1_attn_Wo"),
                    ("w1", "L1_attn_W1"), ("w2", "L1_attn_W2")]:
        p[f"l1a_{nm}"] = np.ascontiguousarray(np.asarray(inp[key], np.float32))
    return p


def run_l1a(xs, prep):
    n = len(xs)
    in_maps = [{"x": xs[i], "cols": prep["l1a_cols"], "cols2": prep["l1a_cols2"],
                "brow": prep["l1a_brow"],
                **{nm: prep[f"l1a_{nm}"] for nm in
                   ["wq", "wk", "wv", "wo", "w1", "w2"]}} for i in range(n)]
    res = _run_phase("l1a", _build_l1a, in_maps)
    return [np.asarray(res.results[i]["xo"]) for i in range(n)]


def emit_l3a(ctx, d):
    s = ctx.s
    emit_ln(ctx, d["x"], d["xn"])
    xnr = d["xn"].rearrange("(c p) t -> p c t", p=128)
    xr = d["x"].rearrange("(c p) t -> p c t", p=128)
    xor_ = d["xo"].rearrange("(c p) t -> p c t", p=128)

    WBUF = ctx.sbuf("l3a_wbuf", [128, 32, 1024], BF16)
    PJ = ctx.sbuf("l3a_pj", [64, 2048], BF16)
    QP = ctx.sbuf("l3a_qp", [128, 2, CH, 512], BF16)
    KP = ctx.sbuf("l3a_kp", [128, 32, 256], BF16)
    KVB = ctx.sbuf("l3a_kvb", [128, 128], BF16)
    KSB = ctx.sbuf("l3a_ksb", [128, 2], BF16)
    QF = [ctx.sbuf(f"l3a_qf{i}", [64, T], BF16) for i in range(2)]
    VT = [ctx.sbuf(f"l3a_vt{i}", [128, 128], BF16) for i in range(2)]
    TMP = [ctx.sbuf(f"l3a_tmp{i}", [128, 512], F32) for i in range(4)]
    OUTB = [ctx.sbuf(f"l3a_ob{i}", [128, 512], BF16) for i in range(2)]
    RZC = ctx.sbuf("l3a_rzc", [128, 64], F32)
    ATOK = ctx.sbuf("l3a_atok", [128, 64], BF16)
    EYEB3 = ctx.sbuf("l3a_eyeb", [128, 128], BF16)
    s.dma("qpool", EYEB3[:], d["eyeb"][:], writes=[EYEB3])
    AH = [ctx.sbuf(f"l3a_ah{i}", [64, 512], BF16) for i in range(2)]
    COLS = ctx.sbuf("l3a_cols", [128, 8], F32)

    s.dma("qpool", COLS[:], d["cols"][:], writes=[COLS])
    s.dma("qpool", PJ[:].rearrange("d (h f) -> d h f", f=128),
          d["proj"].rearrange("(h d) f -> d h f", d=64), writes=[PJ])
    wr = d["wqkvT"].rearrange("(c p) o -> p c o", p=128)
    for c in range(CH):
        s.dma("qpool", WBUF[:, 3 * c:3 * c + 3, :],
              wr[:, c, :].rearrange("p (a o) -> p a o", o=1024), writes=[WBUF])
    woT = d["woutT"].rearrange("(c p) o -> p c o", p=128)
    WOV = WBUF[:, 24:32, :]
    s.dma("qpool", WOV, woT[:], writes=[WBUF])

    def qkv_lhsT(c, col0, n):
        a, b = divmod(col0, 1024)
        return WBUF[:, 3 * c + a, b:b + n]

    def elu1_into(dst, psrc, reads_p):
        t1_, t2_ = TMP[0], TMP[1]
        a1, a2 = t1_[:], t2_[:]
        if dst.ndim == 3:
            a1 = a1.rearrange("p (a b) -> p a b", b=dst.shape[2])
            a2 = a2.rearrange("p (a b) -> p a b", b=dst.shape[2])
        s.op("dve", lambda e: e.tensor_scalar(t1_[:], psrc, 0.0, None, ALU.min),
             reads=reads_p, writes=[t1_])
        ctx.act(t1_[:], t1_[:], AF.Exp, reads=[t1_], writes=[t1_])
        s.op("dve", lambda e: e.tensor_scalar(t2_[:], psrc, 0.0, None, ALU.max),
             reads=reads_p, writes=[t2_])
        ctx.tt(dst, a1, a2, ALU.add, reads=[t1_, t2_], writes=[dst.tensor])

    for it in range(8):
        tsl = slice(it * 512, (it + 1) * 512)
        xn = ctx.xnt[it % 2]
        s.dma("qsp", xn[:], xnr[:, :, tsl], reads=[d["xn"]], writes=[xn])
        for which, dst in [(0, d["q"]), (1, d["k"])]:
            for oc in range(CH):
                pg = ctx.pb[oc % 2]
                for c in range(CH):
                    ctx.mm(pg[:], qkv_lhsT(c, which * 1024 + oc * 128, 128),
                           xn[:, c, :], c == 0, c == CH - 1,
                           reads=[WBUF, xn], writes=[pg])
                ob = OUTB[oc % 2]
                ctx.act(ob[:], pg[:], AF.Copy, reads=[pg], writes=[ob])
                s.dma("qsp", dst[oc * 128:(oc + 1) * 128, tsl], ob[:],
                      reads=[ob], writes=[dst])
        for st_ in range(4):
            ssl = slice(st_ * 128, st_ * 128 + 128)
            trow = it * 512 + st_ * 128
            for mh in range(2):
                pv = ctx.pb[4 + mh]
                for c in range(CH):
                    ctx.mm(pv[:], xn[:, c, ssl],
                           qkv_lhsT(c, 2048 + mh * 512, 512), c == 0, c == CH - 1,
                           reads=[xn, WBUF], writes=[pv])
                ob = OUTB[mh]
                ctx.act(ob[:], pv[:], AF.Copy, reads=[pv], writes=[ob])
                s.dma("qsp", d["v"][trow:trow + 128, mh * 512:(mh + 1) * 512],
                      ob[:], reads=[ob], writes=[d["v"]])

    for hg in range(8):
        for h2 in range(2):
            h = 2 * hg + h2
            qf = QF[0]
            s.dma("qsp", qf[:], d["q"][h * 64:h * 64 + 64, :],
                  reads=[d["q"]], writes=[qf])
            pj = PJ[0:64, h * 128:(h + 1) * 128]
            for it in range(8):
                tsl = slice(it * 512, (it + 1) * 512)
                pg = ctx.pb[it % 2]
                ctx.mm(pg[:], pj, qf[:, tsl], True, True,
                       reads=[PJ, qf], writes=[pg])
                elu1_into(QP[:, h2, it, :], pg[:], [pg])
            kf = QF[1]
            s.dma("qsp", kf[:], d["k"][h * 64:h * 64 + 64, :],
                  reads=[d["k"]], writes=[kf])
            for tg in range(8):
                pg = ctx.pb[2 + tg % 2]
                for j in range(4):
                    tc = 4 * tg + j
                    ctx.mm(pg[:, j * 128:(j + 1) * 128], kf[:, tc * 128:
                           (tc + 1) * 128], pj, True, True,
                           reads=[PJ, kf], writes=[pg])
                elu1_into(KP[:, 4 * tg:4 * tg + 4, h2 * 128:(h2 + 1) * 128], pg[:],
                          [pg])
        pkv, pks = ctx.pb[4], ctx.pb[5]
        for tc in range(32):
            vt = VT[tc % 2]
            s.dma("qsp", vt[:], d["v"][tc * 128:(tc + 1) * 128,
                  hg * 128:(hg + 1) * 128], reads=[d["v"]], writes=[vt])
            for h2 in range(2):
                ctx.mm(pkv[:, h2 * 64:(h2 + 1) * 64],
                       KP[:, tc, h2 * 128:(h2 + 1) * 128],
                       vt[:, h2 * 64:(h2 + 1) * 64], tc == 0, tc == 31,
                       reads=[KP, vt], writes=[pkv])
                ctx.mm(pks[:, h2:h2 + 1], KP[:, tc, h2 * 128:(h2 + 1) * 128],
                       ctx.ones_col_b[:], tc == 0, tc == 31,
                       reads=[KP, ctx.ones_col_b], writes=[pks])
        ctx.act(KVB[:], pkv[:, 0:128], AF.Copy, reads=[pkv], writes=[KVB])
        ctx.act(KSB[:], pks[:, 0:2], AF.Copy, reads=[pks], writes=[KSB])
        pzc = ctx.pb[6]
        for h2 in range(2):
            for it in range(8):
                for j in range(4):
                    idx = h2 * 32 + it * 4 + j
                    ctx.mm(pzc[:, idx:idx + 1],
                           QP[:, h2, it, j * 128:(j + 1) * 128],
                           KSB[:, h2:h2 + 1], True, True,
                           reads=[QP, KSB], writes=[pzc])
        ctx.act(RZC[:], pzc[:, 0:64], AF.Copy, reads=[pzc], writes=[RZC],
                bias=1e-8)
        s.op("dve", lambda e: e.reciprocal(RZC[:], RZC[:]), reads=[RZC],
             writes=[RZC])
        for h2 in range(2):
            h = 2 * hg + h2
            for it in range(8):
                tsl = slice(it * 512, (it + 1) * 512)
                ah = AH[it % 2]
                for j in range(4):
                    idx = h2 * 32 + it * 4 + j
                    pa = ctx.pb[2 + j % 2]
                    ctx.mm(pa[:, 0:64], QP[:, h2, it, j * 128:(j + 1) * 128],
                           KVB[:, h2 * 64:(h2 + 1) * 64], True, True,
                           reads=[QP, KVB], writes=[pa])
                    s.op("dve", lambda e, pa=pa, idx=idx: e.tensor_scalar(
                        ATOK[:], pa[:, 0:64], RZC[:, idx:idx + 1], None,
                        ALU.mult), reads=[pa, RZC], writes=[ATOK])
                    ptr = ctx.pb[j % 2]
                    pv = ptr[:, 0:64].bitcast(BF16)
                    s.op("pe", lambda e, pv=pv: e.transpose(
                        pv[0:64, 0:128], ATOK[:], EYEB3[:]),
                        reads=[ATOK, EYEB3], writes=[ptr])
                    ctx.act(ah[:, j * 128:(j + 1) * 128], pv[0:64, 0:128],
                            AF.Copy, reads=[ptr], writes=[(ah, j)])
                s.dma("qsp", d["a"][h * 64:h * 64 + 64, tsl], ah[:],
                      reads=[(ah, j_) for j_ in range(4)], writes=[d["a"]])

    ar = d["a"].rearrange("(c p) t -> p c t", p=128)
    for it in range(8):
        tsl = slice(it * 512, (it + 1) * 512)
        atile = ctx.xnt[it % 2]
        xt = ctx.xtf[it % 2]
        s.dma("qsp", atile[:], ar[:, :, tsl], reads=[d["a"]], writes=[atile])
        s.dma("qsp", xt[:], xr[:, :, tsl], reads=[d["x"]], writes=[xt])
        for oc in range(CH):
            po = ctx.pb[2 + oc % 2]
            for c in range(CH):
                ctx.mm(po[:], WBUF[:, 24 + c, oc * 128:(oc + 1) * 128],
                       atile[:, c, :], c == 0, c == CH - 1,
                       reads=[WBUF, atile], writes=[po])
            tm = TMP[oc % 2]
            s.op("dve", lambda e, p=po, t=tm, oc=oc: e.tensor_scalar(
                t[:], p[:], COLS[:, oc:oc + 1], None, ALU.add),
                reads=[po, COLS], writes=[tm])
            s.op("dve", lambda e, t=tm, x=xt, oc=oc: e.tensor_tensor(
                t[:], t[:], x[:, oc, :], ALU.add), reads=[tm, xt], writes=[tm])
            s.dma("qsp", xor_[:, oc, tsl], tm[:], reads=[tm], writes=[d["xo"]])


def _build_l3a():
    nc = _mk_nc()
    d = {}
    d["x"] = nc.dram_tensor("x", [D, T], F32, kind="ExternalInput").ap()
    d["xo"] = nc.dram_tensor("xo", [D, T], F32, kind="ExternalOutput").ap()
    for nm, sh in [("xn", [D, T]), ("q", [D, T]), ("k", [D, T]),
                   ("v", [T, D]), ("a", [D, T])]:
        d[nm] = nc.dram_tensor(nm, sh, BF16, kind="Internal").ap()
    d["wqkvT"] = nc.dram_tensor("wqkvT", [D, 3072], F32, kind="ExternalInput").ap()
    d["woutT"] = nc.dram_tensor("woutT", [D, D], F32, kind="ExternalInput").ap()
    d["proj"] = nc.dram_tensor("proj", [D, 128], F32, kind="ExternalInput").ap()
    d["cols"] = nc.dram_tensor("cols", [128, 8], F32, kind="ExternalInput").ap()
    d["eyeb"] = nc.dram_tensor("eyeb", [128, 128], BF16,
                               kind="ExternalInput").ap()
    es = contextlib.ExitStack()
    with es:
        ctx = Ctx(nc, es)
        emit_l3a(ctx, d)
        ctx.s.build(es)
    return nc


def _prep_l3a(inp, p):
    p["l3a_wqkvT"] = np.ascontiguousarray(np.asarray(inp["L3_attn_W_qkv"],
                                                     np.float32).T)
    p["l3a_woutT"] = np.ascontiguousarray(np.asarray(inp["L3_attn_W_out"],
                                                     np.float32).T)
    p["l3a_proj"] = np.ascontiguousarray(np.asarray(inp["L3_attn_proj"],
                                                    np.float32).reshape(1024, 128))
    p["l3a_cols"] = _col(inp["L3_attn_b_out"])
    p["l3a_eyeb"] = np.eye(128, dtype=BF)
    return p


def run_l3a(xs, prep):
    n = len(xs)
    in_maps = [{"x": xs[i], "wqkvT": prep["l3a_wqkvT"], "woutT": prep["l3a_woutT"],
                "proj": prep["l3a_proj"], "cols": prep["l3a_cols"],
                "eyeb": prep["l3a_eyeb"]}
               for i in range(n)]
    res = _run_phase("l3a", _build_l3a, in_maps)
    return [np.asarray(res.results[i]["xo"]) for i in range(n)]


def emit_l0a(ctx, d):
    s = ctx.s
    NH = 16
    emit_ln(ctx, d["x"], d["xn"])
    xnr = d["xn"].rearrange("(c p) t -> p c t", p=128)

    WT3 = ctx.sbuf("l0a_wt3", [128, CH, 3072], BF16)
    wr = d["wT"].rearrange("(c p) o -> p c o", p=128)
    for g in range(4):
        s.dma("qpool", WT3[:, :, g * 768:(g + 1) * 768],
              wr[:, :, g * 768:(g + 1) * 768], writes=[WT3])
    QKC = ctx.sbuf("l0a_qkc", [128, 16], F32)
    VROW = ctx.sbuf("l0a_vrow", [1, 1024], BF16)
    s.dma("qpool", QKC[:], d["qkcols"][:], writes=[QKC])
    s.dma("qpool", VROW[:], d["vrow"][:], writes=[VROW])
    MSK = ctx.sbuf("l0a_msk", [128, 4, 512], BF16)
    s.dma("qsp", MSK[:], d["mask"].rearrange("(a p) q -> p a q", p=128),
          writes=[MSK])
    ONF = ctx.sbuf("l0a_onf", [128, 64], F32)
    s.op("pool", lambda e: e.memset(ONF[:], 1.0), writes=[ONF])
    VRES = ctx.sbuf("l0a_vres", [128, 32, 1040], BF16)
    s.op("pool", lambda e: e.memset(VRES[:], 1.0), writes=[VRES])
    NRM = ctx.sbuf("l0a_nrm", [1, 2], F32)
    s.op("pool", lambda e: e.memset(NRM[:], 0.0), writes=[NRM])
    BIASC = ctx.sbuf("l0a_biasc", [128, 1], F32)
    R2 = ctx.sbuf("l0a_r2", [1, 2], F32)
    TMP0 = ctx.sbuf("l0a_tmp0", [128, 512], F32)
    TMP = [TMP0, TMP0]
    SQB = ctx.sbuf("l0a_sqb", [128, 512], BF16)
    OUTB = [ctx.sbuf(f"l0a_ob{i}", [128, 512], BF16) for i in range(2)]
    YH = [ctx.sbuf(f"l0a_yh{i}", [64, 512], BF16) for i in range(2)]

    for it in range(8):
        tsl = slice(it * 512, (it + 1) * 512)
        xn = ctx.xnt[it % 2]
        s.dma("qsp", xn[:], xnr[:, :, tsl], reads=[d["xn"]], writes=[xn])
        for wp, dst in ((0, d["q"]), (1, d["k"])):
            for oc in range(CH):
                pg = ctx.pb[oc % 2]
                for c in range(CH):
                    ctx.mm(pg[:], WT3[:, c, wp * 1024 + oc * 128:
                           wp * 1024 + (oc + 1) * 128],
                           xn[:, c, :], c == 0, c == CH - 1,
                           reads=[WT3, xn], writes=[pg])
                ob = OUTB[oc % 2]
                s.op("dve", lambda e, p=pg, o=ob, oc=oc, w=wp: e.tensor_scalar(
                    o[:], p[:], QKC[:, 8 * w + oc:8 * w + oc + 1], None,
                    ALU.add), reads=[pg, QKC], writes=[ob])
                s.dma("qsp", dst[oc * 128:(oc + 1) * 128, tsl], ob[:],
                      reads=[ob], writes=[dst])
                ctx.act(SQB[:], ob[:], AF.Square, reads=[ob], writes=[SQB])
                pn = ctx.pb[2]
                ctx.mm(pn[0:1, :], ctx.ones_col_b[:], SQB[:], True, True,
                       reads=[SQB, ctx.ones_col_b], writes=[pn])
                s.op("dve", lambda e, p=pn: e.reduce_max(
                    R2[0:1, 0:1], p[0:1, :], axis=AX.X),
                    reads=[pn], writes=[R2])
                ctx.tt(NRM[0:1, wp:wp + 1], NRM[0:1, wp:wp + 1],
                       R2[0:1, 0:1], ALU.max, reads=[NRM, R2], writes=[NRM])
        for st_ in range(4):
            ssl = slice(st_ * 128, st_ * 128 + 128)
            tc = it * 4 + st_
            for mh in range(2):
                pv = ctx.pb[4 + mh]
                for c in range(CH):
                    ctx.mm(pv[:], xn[:, c, ssl],
                           WT3[:, c, 2048 + mh * 512:2048 + (mh + 1) * 512],
                           c == 0, False, reads=[xn, WT3], writes=[pv])
                ctx.mm(pv[:], ctx.ones_row_b[:],
                       VROW[0:1, mh * 512:(mh + 1) * 512], False, True,
                       reads=[ctx.ones_row_b, VROW], writes=[pv])
                dd = VRES[:, tc, :].rearrange("p (h m) -> p h m", m=65)[
                    :, mh * 8:mh * 8 + 8, 0:64]
                ctx.act(dd, pv[:].rearrange("p (h m) -> p h m", m=64),
                        AF.Copy, reads=[pv], writes=[(VRES, tc, mh)])

    ctx.tt(R2[0:1, 0:1], NRM[0:1, 0:1], NRM[0:1, 1:2],
           ALU.mult, reads=[NRM], writes=[R2])
    ctx.act(R2[0:1, 0:1], R2[0:1, 0:1], AF.Sqrt, reads=[R2], writes=[R2],
            bias=0.0)
    s.op("dve", lambda e: e.tensor_scalar(
        R2[0:1, 0:1], R2[0:1, 0:1], -0.125, None, ALU.mult),
        reads=[R2], writes=[R2])
    pn = ctx.pb[2]
    ctx.mm(pn[:, 0:1], ctx.ones_row_f[:], R2[0:1, 0:1], True, True,
           reads=[R2, ctx.ones_row_f], writes=[pn])
    ctx.act(BIASC[:], pn[:, 0:1], AF.Copy, reads=[pn], writes=[BIASC])

    def pts(kt, n=1):
        buf = ctx.xtf[kt // 16]
        v = buf[:, :, :].rearrange("p c t -> p (c t)").bitcast(BF16)
        return v.rearrange("p (k q) -> p k q", q=512)[:, kt % 16:kt % 16 + n, :]

    QF = ctx.sbuf("l0a_qf", [64, T], BF16)
    KF = ctx.sbuf("l0a_kf", [64, T], BF16)
    for h in range(NH):
        s.dma("qsp", QF[:], d["q"][h * 64:h * 64 + 64, :],
              reads=[d["q"]], writes=[QF])
        s.dma("qsp", KF[:], d["k"][h * 64:h * 64 + 64, :],
              reads=[d["k"]], writes=[KF])
        for qg in range(8):
            qsl = slice(qg * 512, (qg + 1) * 512)
            nkt = 4 * (qg + 1)
            pa, pzb = ctx.pb[0], ctx.pb[1]

            def emit_qk_exp(kt2, pi):
                if pi % 2 == 0:
                    for j in range(2):
                        kt = kt2 + j
                        ctx.mm(ctx.pbw[:, j * 512:(j + 1) * 512],
                               KF[:, kt * 128:(kt + 1) * 128], QF[:, qsl],
                               True, True, reads=[KF, QF],
                               writes=[(ctx.pbw, j)])
                    ctx.act(pts(kt2, 2),
                            ctx.pbw[:].rearrange("p (a q) -> p a q", q=512),
                            AF.Exp,
                            reads=[(ctx.pbw, 0), (ctx.pbw, 1), BIASC],
                            writes=[(ctx.xtf[kt2 // 16], kt2),
                                    (ctx.xtf[kt2 // 16], kt2 + 1)],
                            bias=BIASC[:, 0:1], scale=0.125)
                else:
                    for j in range(2):
                        kt = kt2 + j
                        pq = ctx.pb[2 + j]
                        ctx.mm(pq[:], KF[:, kt * 128:(kt + 1) * 128],
                               QF[:, qsl], True, True, reads=[KF, QF],
                               writes=[pq])
                        ctx.act(pts(kt)[:, 0, :], pq[:], AF.Exp,
                                reads=[pq, BIASC],
                                writes=[(ctx.xtf[kt // 16], kt)],
                                bias=BIASC[:, 0:1], scale=0.125)

            def emit_av(kt2):
                for j in range(2):
                    kt = kt2 + j
                    pt = pts(kt)[:, 0, :]
                    if kt >= 4 * qg:
                        jj = kt - 4 * qg
                        ctx.tt(pt, pt, MSK[:, jj, :], ALU.mult,
                               reads=[(ctx.xtf[kt // 16], kt), MSK],
                               writes=[(ctx.xtf[kt // 16], kt)])
                    ctx.mm(pa[0:65, :], VRES[:, kt, h * 65:h * 65 + 65], pt,
                           kt == 0, kt == nkt - 1,
                           reads=[(ctx.xtf[kt // 16], kt), VRES], writes=[pa])

            prev = None
            for pi, kt2 in enumerate(range(0, nkt, 2)):
                emit_qk_exp(kt2, pi)
                if prev is not None:
                    emit_av(prev)
                prev = kt2
            emit_av(prev)
            ctx.act(TMP[0][64:65, :], pa[64:65, :], AF.Copy,
                    reads=[pa], writes=[TMP[0]])
            s.op("dve", lambda e: e.reciprocal(TMP[0][64:65, :],
                 TMP[0][64:65, :]), reads=[TMP[0]], writes=[TMP[0]])
            ctx.mm(pzb[0:64, :], ONF[64:65, 0:64], TMP[0][64:65, :], True, True,
                   reads=[TMP[0], ONF], writes=[pzb])
            ctx.act(TMP[1][0:64, :], pa[0:64, :], AF.Copy, reads=[pa],
                    writes=[TMP[1]])
            yh = YH[qg % 2]
            s.op("dve", lambda e, yh=yh: e.tensor_tensor(
                yh[:], TMP[1][0:64, :], pzb[0:64, :], ALU.mult),
                reads=[TMP[1], pzb], writes=[yh])
            s.dma("qsp", d["y"][h * 64:h * 64 + 64, qsl], yh[:],
                  reads=[yh], writes=[d["y"]])

    emit_resadd(ctx, d["x"], d["y"], d["xo"])


def _build_l0a():
    nc = _mk_nc()
    d = {}
    d["x"] = nc.dram_tensor("x", [D, T], F32, kind="ExternalInput").ap()
    d["xo"] = nc.dram_tensor("xo", [D, T], F32, kind="ExternalOutput").ap()
    for nm, sh in [("xn", [D, T]), ("q", [D, T]), ("k", [D, T]), ("y", [D, T])]:
        d[nm] = nc.dram_tensor(nm, sh, BF16, kind="Internal").ap()
    d["wT"] = nc.dram_tensor("wT", [D, 3072], F32, kind="ExternalInput").ap()
    d["qkcols"] = nc.dram_tensor("qkcols", [128, 16], F32,
                                 kind="ExternalInput").ap()
    d["vrow"] = nc.dram_tensor("vrow", [1, 1024], F32, kind="ExternalInput").ap()
    d["mask"] = nc.dram_tensor("mask", [512, 512], BF16,
                               kind="ExternalInput").ap()
    es = contextlib.ExitStack()
    with es:
        ctx = Ctx(nc, es)
        emit_l0a(ctx, d)
        ctx.s.build(es)
    return nc


def _prep_l0a(inp, p):
    p["l0a_wT"] = np.ascontiguousarray(np.asarray(inp["L0_attn_W"], np.float32).T)
    b = np.asarray(inp["L0_attn_b"], np.float32)
    qk = np.zeros((128, 16), np.float32)
    qk[:, 0:8] = _col(b[0:1024])
    qk[:, 8:16] = _col(b[1024:2048])
    p["l0a_qkcols"] = qk
    p["l0a_vrow"] = np.ascontiguousarray(b[2048:3072].reshape(1, 1024))
    m = np.zeros((4, 128, 512), np.float32)
    for jj in range(4):
        for i_ in range(128):
            m[jj, i_, 128 * jj + i_:] = 1.0
    p["l0a_mask"] = m.reshape(512, 512).astype(BF)
    return p


def run_l0a(xs, prep):
    n = len(xs)
    in_maps = [{"x": xs[i], "wT": prep["l0a_wT"], "qkcols": prep["l0a_qkcols"],
                "vrow": prep["l0a_vrow"], "mask": prep["l0a_mask"]}
               for i in range(n)]
    res = _run_phase("l0a", _build_l0a, in_maps)
    return [np.asarray(res.results[i]["xo"]) for i in range(n)]


def emit_l3f(ctx, d):
    s = ctx.s
    NE = 32
    emit_ln(ctx, d["x"], d["xn"])
    xnr = d["xn"].rearrange("(c p) t -> p c t", p=128)
    xr = d["x"].rearrange("(c p) t -> p c t", p=128)
    w1r = d["w1"].rearrange("e (c p) h -> p e c h", p=128)
    w2r = d["w2"].rearrange("e (hc p) o -> p e hc o", p=128)

    RWT = ctx.sbuf("l3f_rwt", [128, CH, NE], BF16)
    s.dma("qpool", RWT[:], d["rwT"].rearrange("(c p) e -> p c e", p=128),
          writes=[RWT])
    RBR = ctx.sbuf("l3f_rbr", [1, NE], BF16)
    s.dma("qpool", RBR[:], d["rbr"][:], writes=[RBR])
    B1A = ctx.sbuf("l3f_b1a", [128, 64], F32)
    s.dma("qpool", B1A[:], d["b1a"][:], writes=[B1A])
    EYEF = ctx.sbuf("l3f_eyef", [128, 128], F32)
    s.dma("qsp", EYEF[:], d["eyef"][:], writes=[EYEF])
    G = ctx.sbuf("l3f_g", [128, 32, NE], F32)
    LG = ctx.sbuf("l3f_lg", [128, NE], F32)
    MX = ctx.sbuf("l3f_mx", [128, 1], F32)
    XNH = ctx.sbuf("l3f_xnh", [128, CH, 2048], BF16)
    ACC = ctx.sbuf("l3f_acc", [128, 16, 1024], BF16)
    HB = ctx.sbuf("l3f_hb", [128, 2, 2048], BF16)
    W1E = [ctx.sbuf(f"l3f_w1e{i}", [128, CH, 256], BF16) for i in range(2)]
    W2E = [ctx.sbuf(f"l3f_w2e{i}", [128, 2, 1024], BF16) for i in range(2)]
    B2R = [ctx.sbuf(f"l3f_b2r{i}", [1, 1024], BF16) for i in range(2)]
    TMPB = ctx.sbuf("l3f_tmpb", [128, 1024], BF16)
    B2T = [ctx.sbuf(f"l3f_b2t{i}", [128, 1024], F32) for i in range(2)]
    TMPF = [ctx.sbuf(f"l3f_tmpf{i}", [128, 1024], F32) for i in range(2)]
    XTOK = ctx.sbuf("l3f_xtok", [128, 1024], F32)
    OUTF = [ctx.sbuf(f"l3f_outf{i}", [128, 1024], F32) for i in range(2)]

    for it in range(8):
        tsl = slice(it * 512, (it + 1) * 512)
        xn = ctx.xnt[it % 2]
        s.dma("qsp", xn[:], xnr[:, :, tsl], reads=[d["xn"]], writes=[xn])
        for st_ in range(4):
            tc = it * 4 + st_
            ssl = slice(st_ * 128, st_ * 128 + 128)
            pl = ctx.pb[st_ % 2]
            for c in range(CH):
                ctx.mm(pl[:, 0:NE], xn[:, c, ssl], RWT[:, c, :], c == 0, False,
                       reads=[xn, RWT], writes=[pl])
            ctx.mm(pl[:, 0:NE], ctx.ones_row_b[:], RBR[:], False, True,
                   reads=[ctx.ones_row_b, RBR], writes=[pl])
            ctx.act(LG[:], pl[:, 0:NE], AF.Copy, reads=[pl], writes=[LG])
            s.op("dve", lambda e: e.reduce_max(MX[:], LG[:], axis=AX.X),
                 reads=[LG], writes=[MX])
            s.op("dve", lambda e, tc=tc: e.tensor_scalar(
                G[:, tc, :], LG[:], MX[:, 0:1], None, ALU.is_ge),
                reads=[LG], strict=[MX], writes=[(G, tc)])
            s.op("dve", lambda e, tc=tc: e.tensor_scalar(
                G[:, tc, :], G[:, tc, :], 1.0, 1e30, ALU.subtract, ALU.mult),
                reads=[(G, tc)], writes=[(G, tc)])

    for th in range(2):
        hsl = slice(th * 2048, (th + 1) * 2048)
        s.dma("qsp", XNH[:], xnr[:, :, hsl], reads=[d["xn"]], writes=[XNH])
        for e in range(NE):
            w1, w2, b2 = W1E[e % 2], W2E[e % 2], B2R[e % 2]
            s.dma("qpool", w1[:], w1r[:, e, :, :], writes=[w1])
            s.dma("qpool", w2[:], w2r[:, e, :, :], writes=[w2])
            s.dma("qpool", b2[:], d["b2a"][e:e + 1, :], writes=[b2])
            b2t = B2T[e % 2]
            for oh in range(2):
                osl = slice(oh * 512, (oh + 1) * 512)
                pg = ctx.pb[oh]
                ctx.mm(pg[:], ctx.ones_row_b[:], b2[0:1, osl], True, True,
                       reads=[ctx.ones_row_b, b2], writes=[pg])
                ctx.act(b2t[:, osl], pg[:], AF.Copy, reads=[pg],
                        writes=[(b2t, oh)])
            for hc in range(2):
                for tq in range(4):
                    pg = ctx.pb[2 + tq % 2]
                    qsl = slice(tq * 512, (tq + 1) * 512)
                    for c in range(CH):
                        ctx.mm(pg[:], w1[:, c, hc * 128:(hc + 1) * 128],
                               XNH[:, c, qsl], c == 0, c == CH - 1,
                               reads=[w1, XNH], writes=[pg])
                    ctx.act(HB[:, hc, qsl], pg[:], AF.Relu, reads=[pg, B1A],
                            writes=[(HB, hc, tq)],
                            bias=B1A[:, 2 * e + hc:2 * e + hc + 1])
            for ts2 in range(16):
                ssl = slice(ts2 * 128, (ts2 + 1) * 128)
                even = ts2 % 2 == 0
                for oh in range(2):
                    osl = slice(oh * 512, (oh + 1) * 512)
                    tgt = ctx.pbw[:, osl] if even else ctx.pb[4 + oh][:]
                    tkey = [(ctx.pbw, oh)] if even else [ctx.pb[4 + oh]]
                    for hc in range(2):
                        ctx.mm(tgt, HB[:, hc, ssl], w2[:, hc, osl],
                               hc == 0, hc == 1,
                               reads=[(HB, hc, ts2 // 4), w2],
                               writes=tkey)
                gcol = G[:, (th * 2048 + ts2 * 128) // 128, e:e + 1]
                tmpf = TMPF[ts2 % 2]
                if even:
                    ctx.tt(tmpf[:], ctx.pbw[:], b2t[:], ALU.add,
                           reads=[(ctx.pbw, 0), (ctx.pbw, 1),
                                  (b2t, 0), (b2t, 1)],
                           writes=[tmpf])
                else:
                    for oh in range(2):
                        osl = slice(oh * 512, (oh + 1) * 512)
                        ctx.tt(tmpf[:, osl], ctx.pb[4 + oh][:], b2t[:, osl],
                               ALU.add, reads=[ctx.pb[4 + oh], (b2t, oh)],
                               writes=[(tmpf, oh)])
                if e == 0:
                    ctx.act(ACC[:, ts2, :], tmpf[:], AF.Relu,
                            reads=[tmpf, G], writes=[(ACC, ts2)], bias=gcol)
                else:
                    ctx.act(TMPB[:], tmpf[:], AF.Relu,
                            reads=[tmpf, G], writes=[TMPB], bias=gcol)
                    ctx.tt(ACC[:, ts2, :], ACC[:, ts2, :], TMPB[:], ALU.add,
                           reads=[(ACC, ts2), TMPB], writes=[(ACC, ts2)])
        for tt4 in range(4):
            xt = ctx.xtf[tt4 % 2]
            csl = slice(th * 2048 + tt4 * 512, th * 2048 + (tt4 + 1) * 512)
            s.dma("qsp", xt[:], xr[:, :, csl], reads=[d["x"]], writes=[xt])
            for j in range(4):
                ts2 = tt4 * 4 + j
                for c in range(CH):
                    ptr = ctx.pb[4 + c % 2]
                    ctx.s.op("pe", lambda e, p=ptr, c=c, xt=xt, j=j: e.transpose(
                        p[:, 0:128], xt[:, c, j * 128:(j + 1) * 128], EYEF[:]),
                        reads=[xt, EYEF], writes=[ptr])
                    ctx.act(XTOK[:, c * 128:(c + 1) * 128],
                            ptr[:, 0:128], AF.Copy,
                            reads=[ptr], writes=[(XTOK, c)])
                of = OUTF[j % 2]
                s.op("dve", lambda e, of=of, ts2=ts2: e.tensor_tensor(
                    of[:], ACC[:, ts2, :], XTOK[:], ALU.add),
                    reads=[(ACC, ts2)] + [(XTOK, c) for c in range(CH)],
                    writes=[of])
                trow = th * 2048 + ts2 * 128
                s.dma("qsp", d["xo"][trow:trow + 128, :], of[:],
                      reads=[of], writes=[d["xo"]])


def _build_l3f():
    nc = _mk_nc()
    d = {}
    d["x"] = nc.dram_tensor("x", [D, T], F32, kind="ExternalInput").ap()
    d["xo"] = nc.dram_tensor("xo", [T, D], F32, kind="ExternalOutput").ap()
    d["xn"] = nc.dram_tensor("xn", [D, T], BF16, kind="Internal").ap()
    d["w1"] = nc.dram_tensor("w1", [32, D, 256], F32, kind="ExternalInput").ap()
    d["w2"] = nc.dram_tensor("w2", [32, 256, D], F32, kind="ExternalInput").ap()
    d["rwT"] = nc.dram_tensor("rwT", [D, 32], F32, kind="ExternalInput").ap()
    d["rbr"] = nc.dram_tensor("rbr", [1, 32], F32, kind="ExternalInput").ap()
    d["b1a"] = nc.dram_tensor("b1a", [128, 64], F32, kind="ExternalInput").ap()
    d["b2a"] = nc.dram_tensor("b2a", [32, D], F32, kind="ExternalInput").ap()
    d["eyef"] = nc.dram_tensor("eyef", [128, 128], F32, kind="ExternalInput").ap()
    es = contextlib.ExitStack()
    with es:
        ctx = Ctx(nc, es)
        emit_l3f(ctx, d)
        ctx.s.build(es)
    return nc


def _prep_l3f(inp, p):
    p["l3f_w1"] = np.ascontiguousarray(np.asarray(inp["L3_ffn_weights1"],
                                                  np.float32))
    p["l3f_w2"] = np.ascontiguousarray(np.asarray(inp["L3_ffn_weights2"],
                                                  np.float32))
    p["l3f_rwT"] = np.ascontiguousarray(np.asarray(inp["L3_ffn_router_w"],
                                                   np.float32).T)
    p["l3f_rbr"] = np.asarray(inp["L3_ffn_router_b"],
                              np.float32).reshape(1, 32).copy()
    b1 = np.asarray(inp["L3_ffn_bias1"], np.float32)
    b1a = np.zeros((128, 64), np.float32)
    for e in range(32):
        b1a[:, 2 * e:2 * e + 2] = b1[e].reshape(2, 128).T
    p["l3f_b1a"] = b1a
    p["l3f_b2a"] = np.ascontiguousarray(np.asarray(inp["L3_ffn_bias2"],
                                                   np.float32))
    p["l3f_eyef"] = np.eye(128, dtype=np.float32)
    return p


def run_l3f(xs, prep):
    n = len(xs)
    in_maps = [{"x": xs[i], **{k: prep[f"l3f_{k}"] for k in
                ["w1", "w2", "rwT", "rbr", "b1a", "b2a", "eyef"]}}
               for i in range(n)]
    res = _run_phase("l3f", _build_l3f, in_maps)
    return [np.asarray(res.results[i]["xo"]) for i in range(n)]


def emit_l1f(ctx, d):
    s = ctx.s
    NE = 8
    emit_ln(ctx, d["x"], d["xn"])
    xnr = d["xn"].rearrange("(c p) t -> p c t", p=128)
    xr = d["x"].rearrange("(c p) t -> p c t", p=128)
    xor_ = d["xo"].rearrange("(c p) t -> p c t", p=128)
    hr = d["hs"].rearrange("(hc p) (e t) -> p hc e t", p=128, e=9)

    XN = ctx.sbuf("l1f_xn", [128, CH, T], BF16)
    for g in range(2):
        s.dma("qsp", XN[:, 4 * g:4 * g + 4, :], xnr[:, 4 * g:4 * g + 4, :],
              reads=[d["xn"]], writes=[XN])
    LNS = ctx.sbuf("l1f_lns", [128, 16], F32)
    s.dma("qpool", LNS[:], d["lns"][:], writes=[LNS])
    GWT = ctx.sbuf("l1f_gwt", [128, CH, NE], F32)
    s.dma("qpool", GWT[:], d["gwT"].rearrange("(c p) e -> p c e", p=128),
          writes=[GWT])
    EYEF = ctx.sbuf("l1f_eyef", [128, 128], F32)
    s.dma("qpool", EYEF[:], d["eyef"][:], writes=[EYEF])
    UNITS = ctx.sbuf("l1f_units", [8, 1024], F32)
    s.dma("qpool", UNITS[:], d["units"][:], writes=[UNITS])
    SGW = ctx.sbuf("l1f_sgw", [128, CH, NE], BF16)
    BGW = ctx.sbuf("l1f_bgw", [128, CH, NE], BF16)
    for c in range(CH):
        s.op("dve", lambda e, c=c: e.tensor_scalar(
            SGW[:, c, :], GWT[:, c, :], LNS[:, c:c + 1], None, ALU.mult),
            reads=[GWT, LNS], writes=[(SGW, c)])
        s.op("dve", lambda e, c=c: e.tensor_scalar(
            BGW[:, c, :], GWT[:, c, :], LNS[:, 8 + c:9 + c], None, ALU.mult),
            reads=[GWT, LNS], writes=[(BGW, c)])
    pS = ctx.pb[0]
    pC = ctx.pb[1]
    for c in range(CH):
        ctx.mm(pS[0:1, 0:NE], ctx.ones_col_b[:], SGW[:, c, :], c == 0,
               c == CH - 1, reads=[(SGW, c), ctx.ones_col_b], writes=[pS])
        ctx.mm(pC[0:1, 0:NE], ctx.ones_col_b[:], BGW[:, c, :], c == 0,
               c == CH - 1, reads=[(BGW, c), ctx.ones_col_b], writes=[pC])
    NS1 = ctx.sbuf("l1f_ns1", [1, NE], BF16)
    s.op("dve", lambda e: e.tensor_scalar(NS1[:], pS[0:1, 0:NE], -1.0, None,
         ALU.mult), reads=[pS], writes=[NS1])
    CER = ctx.sbuf("l1f_cer", [1, NE], F32)
    ctx.act(CER[:], pC[0:1, 0:NE], AF.Copy, reads=[pC], writes=[CER])
    CEB = ctx.sbuf("l1f_ceb", [128, NE], F32)
    ctx.mm(pS[:, 0:NE], ctx.ones_row_f[:], CER[:], True, True,
           reads=[CER, ctx.ones_row_f], writes=[pS])
    ctx.act(CEB[:], pS[:, 0:NE], AF.Copy, reads=[pS], writes=[CEB])

    _r3 = ctx.lnrow(3)
    _r3v = _r3[:].bitcast(BF16)
    MU2Bt = _r3v[:, 0:512]
    RSBt = _r3v[:, 512:1024]
    LG8 = ctx.sbuf("l1f_lg8", [128, NE], F32)
    RC = ctx.sbuf("l1f_rc", [128, 1], F32)
    M1 = ctx.sbuf("l1f_m1", [128, 1], F32)
    T8 = ctx.sbuf("l1f_t8", [128, NE], F32)
    GW = ctx.sbuf("l1f_gw", [128, 32, NE], F32)
    WFM = ctx.xtf[1][0:8, :, :].rearrange("p c t -> p (c t)")
    r0, r1, r2 = ctx.lnrow(0), ctx.lnrow(1), ctx.lnrow(2)
    for it in range(8):
        tsl = slice(it * 512, (it + 1) * 512)
        ps_, pq_ = ctx.pb[2], ctx.pb[3]
        ctx.act(ctx.sq[:, :, :], XN[:, :, tsl], AF.Square, reads=[XN],
                writes=[ctx.sq])
        for c in range(CH):
            ctx.mm(ps_[0:1, :], ctx.ones_col_b[:], XN[:, c, tsl], c == 0,
                   c == CH - 1, reads=[XN, ctx.ones_col_b], writes=[ps_])
        for c in range(CH):
            ctx.mm(pq_[0:1, :], ctx.ones_col_b[:], ctx.sq[:, c, :], c == 0,
                   c == CH - 1, reads=[ctx.sq, ctx.ones_col_b], writes=[pq_])
        ctx.act(r0[:], ps_[0:1, :], AF.Copy, reads=[ps_], writes=[r0],
                scale=1.0 / D)
        ctx.act(r1[:], pq_[0:1, :], AF.Copy, reads=[pq_], writes=[r1],
                scale=1.0 / D)
        ctx.tt(r2[:], r0[:], r0[:], ALU.mult, reads=[r0], writes=[r2])
        ctx.tt(r1[:], r1[:], r2[:], ALU.subtract, reads=[r1, r2], writes=[r1])
        ctx.act(r1[:], r1[:], AF.Sqrt, reads=[r1], writes=[r1], bias=1e-5)
        s.op("dve", lambda e: e.reciprocal(r1[:], r1[:]), reads=[r1],
             writes=[r1])
        s.op("dve", lambda e: e.tensor_copy(MU2Bt, r0[:]), reads=[r0],
             writes=[(_r3, 0)])
        s.op("dve", lambda e: e.tensor_copy(RSBt, r1[:]), reads=[r1],
             writes=[(_r3, 1)])
        for st_ in range(4):
            tc = it * 4 + st_
            ssl = slice(st_ * 128, st_ * 128 + 128)
            tgl = slice(tc * 128, (tc + 1) * 128)
            pl, pr = ctx.pb[4], ctx.pb[5]
            for c in range(CH):
                ctx.mm(pl[:, 0:NE], XN[:, c, tgl], SGW[:, c, :], c == 0, False,
                       reads=[XN, (SGW, c)], writes=[pl])
            ctx.mm(pl[:, 0:NE], MU2Bt[0:1, ssl], NS1[:], False, True,
                   reads=[(_r3, 0), NS1], writes=[pl])
            ctx.mm(pr[:, 0:1], RSBt[0:1, ssl], ctx.ones_col_b[0:1, :], True,
                   True, reads=[(_r3, 1), ctx.ones_col_b], writes=[pr])
            ctx.act(RC[:], pr[:, 0:1], AF.Copy, reads=[pr], writes=[RC])
            s.op("dve", lambda e: e.tensor_scalar(
                LG8[:], pl[:, 0:NE], RC[:, 0:1], None, ALU.mult),
                reads=[pl, RC], writes=[LG8])
            ctx.tt(LG8[:], LG8[:], CEB[:], ALU.add, reads=[LG8, CEB],
                   writes=[LG8])
            s.op("dve", lambda e: e.reduce_max(M1[:], LG8[:], axis=AX.X),
                 reads=[LG8], writes=[M1])
            s.op("dve", lambda e: e.tensor_scalar(M1[:], M1[:], -1.0, None,
                 ALU.mult), reads=[M1], writes=[M1])
            ctx.act(T8[:], LG8[:], AF.Exp, reads=[LG8, M1], writes=[T8],
                    bias=M1[:, 0:1])
            s.op("dve", lambda e: e.reduce_sum(RC[:], T8[:], axis=AX.X),
                 reads=[T8], writes=[RC])
            s.op("dve", lambda e: e.reciprocal(RC[:], RC[:]), reads=[RC],
                 writes=[RC])
            s.op("dve", lambda e: e.tensor_scalar(T8[:], T8[:], RC[:, 0:1],
                 None, ALU.mult), reads=[T8, RC], writes=[T8])
            s.op("dve", lambda e: e.reduce_max(M1[:], T8[:], axis=AX.X),
                 reads=[T8], writes=[M1])
            s.op("dve", lambda e: e.tensor_scalar(LG8[:], T8[:], M1[:, 0:1],
                 None, ALU.is_ge), reads=[T8, M1], writes=[LG8])
            s.op("dve", lambda e: e.scalar_tensor_tensor(LG8[:], LG8[:], -1e30,
                 T8[:], ALU.mult, ALU.add), reads=[LG8, T8], writes=[LG8])
            s.op("dve", lambda e: e.reduce_max(RC[:], LG8[:], axis=AX.X),
                 reads=[LG8], writes=[RC])
            s.op("dve", lambda e: e.tensor_scalar(LG8[:], T8[:], RC[:, 0:1],
                 None, ALU.is_ge), reads=[T8, RC], writes=[LG8])
            s.op("dve", lambda e: e.tensor_tensor(LG8[:], LG8[:], T8[:],
                 ALU.mult), reads=[LG8, T8], writes=[LG8])
            s.op("dve", lambda e: e.tensor_tensor(M1[:], M1[:], RC[:, 0:1],
                 ALU.add), reads=[M1, RC], writes=[M1])
            s.op("dve", lambda e: e.tensor_scalar(M1[:], M1[:], 1e-5, None,
                 ALU.add), reads=[M1], writes=[M1])
            s.op("dve", lambda e: e.reciprocal(M1[:], M1[:]), reads=[M1],
                 writes=[M1])
            s.op("dve", lambda e, tc=tc: e.tensor_scalar(GW[:, tc, :], LG8[:],
                 M1[:, 0:1], None, ALU.mult), reads=[LG8, M1],
                 writes=[(GW, tc)])
            pw_ = ctx.pb[6]
            s.op("pe", lambda e, tc=tc: e.transpose(pw_[0:8, 0:128],
                 GW[:, tc, :], EYEF[:]), reads=[(GW, tc), EYEF], writes=[pw_])
            ctx.act(WFM[0:8, tgl], pw_[0:8, 0:128], AF.Copy, reads=[pw_],
                    writes=[(ctx.xtf[1], "wfm", tc)])

    WD = [ctx.sbuf(f"l1f_wd{i}", [128, 4, 1024], BF16) for i in range(2)]
    wdbr = d["wdb"].rearrange("e (hc p) o -> p e hc o", p=128)
    for e9 in range(9):
        for g in range(4):
            wd = WD[g % 2]
            if e9 == 8:
                src_ = d["swd"].rearrange("(hc p) o -> p hc o", p=128)[
                    :, g * 4:g * 4 + 4, :]
            else:
                src_ = d["wd"].rearrange("e (hc p) o -> p e hc o", p=128)[
                    :, e9, g * 4:g * 4 + 4, :]
            s.dma("qpool", wd[:], src_, reads=[d["wd"], d["swd"]],
                  writes=[wd])
            s.dma("qsp", wdbr[:, e9, g * 4:g * 4 + 4, :], wd[:],
                  reads=[wd], writes=[d["wdb"]])

    WG = [ctx.sbuf(f"l1f_wg{i}", [128, CH, 512], BF16) for i in range(2)]
    WU = [ctx.sbuf(f"l1f_wu{i}", [128, CH, 512], BF16) for i in range(2)]
    WBALL = ctx.sbuf("l1f_wball", [128, CH, 512], F32)
    HB4S = [ctx.sbuf("l1f_hb40", [128, 4, 512], BF16)] * 2
    T1S = [ctx.sbuf(f"l1f_t1{i}", [128, 512], BF16) for i in range(2)] * 2
    T2S = [ctx.sbuf(f"l1f_t2{i}", [128, 512], BF16) for i in range(2)] * 2
    for e in range(9):
        shared = (e == 8)
        if not shared:
            for it in range(8):
                tsl = slice(it * 512, (it + 1) * 512)
                pw_ = ctx.pb[6]
                ctx.mm(pw_[:], UNITS[:, e * 128:(e + 1) * 128], WFM[0:8, tsl],
                       True, True,
                       reads=[UNITS] + [(ctx.xtf[1], "wfm", t)
                                        for t in range(32)],
                       writes=[pw_])
                ctx.act(WBALL[:, it, :].rearrange("p t -> p t"), pw_[:],
                        AF.Copy, reads=[pw_], writes=[(WBALL, it)])
        for hg in range(4):
            wgb, wub = WG[hg % 2], WU[hg % 2]
            hsl = slice(hg * 512, (hg + 1) * 512)
            if shared:
                s.dma("qpool", wgb[:], d["swg"].rearrange(
                    "(c p) h -> p c h", p=128)[:, :, hsl], writes=[wgb])
                s.dma("qpool", wub[:], d["swu"].rearrange(
                    "(c p) h -> p c h", p=128)[:, :, hsl], writes=[wub])
            else:
                s.dma("qpool", wgb[:], d["wg"].rearrange(
                    "e (c p) h -> p e c h", p=128)[:, e, :, hsl], writes=[wgb])
                s.dma("qpool", wub[:], d["wu"].rearrange(
                    "e (c p) h -> p e c h", p=128)[:, e, :, hsl], writes=[wub])
            for it in range(8):
                tsl = slice(it * 512, (it + 1) * 512)
                HB4 = HB4S[it % 2]
                for hi in range(4):
                    pg, pu = ctx.pb[hi % 3], ctx.pb[3 + hi % 3]
                    T1, T2 = T1S[hi % 3], T2S[hi % 3]
                    hs2 = slice(hi * 128, hi * 128 + 128)
                    for c in range(CH):
                        ctx.mm(pg[:], wgb[:, c, hs2], XN[:, c, tsl], c == 0,
                               c == CH - 1, reads=[wgb, XN], writes=[pg])
                    for c in range(CH):
                        ctx.mm(pu[:], wub[:, c, hs2], XN[:, c, tsl], c == 0,
                               c == CH - 1, reads=[wub, XN], writes=[pu])
                    ctx.act(T1[:], pg[:], AF.Silu, reads=[pg], writes=[T1])
                    ctx.act(T2[:], pu[:], AF.Copy, reads=[pu], writes=[T2])
                    ctx.tt(HB4[:, hi, :], T1[:], T2[:], ALU.mult,
                           reads=[T1, T2], writes=[(HB4, it % 2, hi)])
                    if not shared:
                        ctx.tt(HB4[:, hi, :], HB4[:, hi, :], WBALL[:, it, :],
                               ALU.mult,
                               reads=[(HB4, it % 2, hi), (WBALL, it)],
                               writes=[(HB4, it % 2, hi)], eng="pool")
                s.dma("qsp", hr[:, hg * 4:hg * 4 + 4, e, tsl],
                      HB4[:], reads=[(HB4, it % 2, hi_) for hi_ in range(4)],
                      writes=[(d["hs"], e, hg, it)])

    HT = ctx.xtf[1][:, :, :].rearrange("p c t -> p (c t)").bitcast(
        BF16).rearrange("p (c t) -> p c t", t=512)
    for tq in range(8):
        tsl = slice(tq * 512, (tq + 1) * 512)
        for e in range(9):
            s.dma("qsp", HT, hr[:, :, e, tsl],
                  reads=[(d["hs"], e, hg_, tq) for hg_ in range(4)],
                  writes=[ctx.xtf[1]])
            for q4 in range(4):
                wd = WD[q4 % 2]
                src_ = wdbr[:, e, q4 * 4:q4 * 4 + 4, :]
                s.dma("qpool" if (e * 4 + q4) % 2 else "qsp", wd[:], src_,
                      reads=[d["wdb"]], writes=[wd])
                for oc in range(CH):
                    for h4 in range(4):
                        hc = q4 * 4 + h4
                        ctx.mm(ctx.pb[oc][:], wd[:, h4,
                               oc * 128:(oc + 1) * 128], HT[:, hc, :],
                               e == 0 and hc == 0, e == 8 and hc == 15,
                               reads=[wd, ctx.xtf[1]],
                               writes=[ctx.pb[oc]] if oc < 6 else
                               [(ctx.pbw, oc - 6)])
        xt = ctx.xtf[0]
        s.dma("qsp", xt[:], xr[:, :, tsl], reads=[d["x"]], writes=[xt])
        for oc in range(CH):
            s.op("dve", lambda e, oc=oc: e.tensor_tensor(
                xt[:, oc, :], xt[:, oc, :], ctx.pb[oc][:], ALU.add),
                reads=[xt] + ([ctx.pb[oc]] if oc < 6 else [(ctx.pbw, oc - 6)]),
                writes=[xt])
        s.dma("qsp", xor_[:, :, tsl], xt[:], reads=[xt], writes=[d["xo"]])


def _build_l1f():
    nc = _mk_nc()
    d = {}
    d["x"] = nc.dram_tensor("x", [D, T], F32, kind="ExternalInput").ap()
    d["xo"] = nc.dram_tensor("xo", [D, T], F32, kind="ExternalOutput").ap()
    d["xn"] = nc.dram_tensor("xn", [D, T], BF16, kind="Internal").ap()
    d["hs"] = nc.dram_tensor("hs", [2048, 9 * T], BF16, kind="Internal").ap()
    d["wdb"] = nc.dram_tensor("wdb", [9, 2048, D], BF16, kind="Internal").ap()
    d["wg"] = nc.dram_tensor("wg", [8, D, 2048], F32, kind="ExternalInput").ap()
    d["wu"] = nc.dram_tensor("wu", [8, D, 2048], F32, kind="ExternalInput").ap()
    d["wd"] = nc.dram_tensor("wd", [8, 2048, D], F32, kind="ExternalInput").ap()
    d["swg"] = nc.dram_tensor("swg", [D, 2048], F32, kind="ExternalInput").ap()
    d["swu"] = nc.dram_tensor("swu", [D, 2048], F32, kind="ExternalInput").ap()
    d["swd"] = nc.dram_tensor("swd", [2048, D], F32, kind="ExternalInput").ap()
    d["gwT"] = nc.dram_tensor("gwT", [D, 8], F32, kind="ExternalInput").ap()
    d["lns"] = nc.dram_tensor("lns", [128, 16], F32, kind="ExternalInput").ap()
    d["eyeb"] = nc.dram_tensor("eyeb", [128, 128], BF16,
                               kind="ExternalInput").ap()
    d["units"] = nc.dram_tensor("units", [8, 1024], F32,
                                kind="ExternalInput").ap()
    d["eyef"] = nc.dram_tensor("eyef", [128, 128], F32,
                               kind="ExternalInput").ap()
    es = contextlib.ExitStack()
    with es:
        ctx = Ctx(nc, es)
        emit_l1f(ctx, d)
        ctx.s.build(es)
    return nc


def _prep_l1f(inp, p):
    p["l1f_wg"] = np.ascontiguousarray(np.asarray(inp["L1_ffn_Wg"], np.float32))
    p["l1f_wu"] = np.ascontiguousarray(np.asarray(inp["L1_ffn_Wu"], np.float32))
    p["l1f_wd"] = np.ascontiguousarray(np.asarray(inp["L1_ffn_Wd"], np.float32))
    p["l1f_swg"] = np.ascontiguousarray(np.asarray(inp["L1_ffn_sWg"], np.float32))
    p["l1f_swu"] = np.ascontiguousarray(np.asarray(inp["L1_ffn_sWu"], np.float32))
    p["l1f_swd"] = np.ascontiguousarray(np.asarray(inp["L1_ffn_sWd"], np.float32))
    p["l1f_gwT"] = np.ascontiguousarray(np.asarray(inp["L1_ffn_gate_w"],
                                                   np.float32).T)
    lns = np.zeros((128, 16), np.float32)
    lns[:, 0:8] = _col(inp["L1_ffn_ln_scale"])
    lns[:, 8:16] = _col(inp["L1_ffn_ln_bias"])
    p["l1f_lns"] = lns
    p["l1f_eyeb"] = np.eye(128, dtype=BF)
    p["l1f_units"] = np.repeat(np.eye(8, dtype=np.float32), 128, axis=1)
    p["l1f_eyef"] = np.eye(128, dtype=np.float32)
    return p


def run_l1f(xs, prep):
    n = len(xs)
    in_maps = [{"x": xs[i], **{k: prep[f"l1f_{k}"] for k in
                ["wg", "wu", "wd", "swg", "swu", "swd", "gwT", "lns", "eyeb",
                 "units", "eyef"]}}
               for i in range(n)]
    res = _run_phase("l1f", _build_l1f, in_maps)
    return [np.asarray(res.results[i]["xo"]) for i in range(n)]


def emit_l2a(ctx, d):
    s = ctx.s
    emit_ln(ctx, d["x"], d["xn"])
    COLS = ctx.sbuf("l2a_cols", [128, 32], F32)
    s.dma("qpool", COLS[:], d["cols"][:], writes=[COLS])
    xb0 = ctx.sbuf("l2a_xb0", [128, CH, 512], BF16)
    emit_ln(ctx, d["xn"], d["xn2"], src_dt=BF16, gcol=(COLS, 0), bcol=(COLS, 8),
            in_bufs=[xb0, xb0])
    xnr = d["xn"].rearrange("(c p) t -> p c t", p=128)
    xn2r = d["xn2"].rearrange("(c p) t -> p c t", p=128)
    xr = d["x"].rearrange("(c p) t -> p c t", p=128)
    xor_ = d["xo"].rearrange("(c p) t -> p c t", p=128)
    uvwr = d["uvwT"].rearrange("(c p) o -> p c o", p=128)
    ur = d["u"].rearrange("(m p) t -> p m t", p=128)
    p2r = d["p2"].rearrange("(m p) t -> p m t", p=128)

    UVW = [ctx.sbuf(f"l2a_uvw{i}", [128, CH, 512], BF16) for i in range(2)]
    UCOL = ctx.sbuf("l2a_ucol", [128, 16], F32)
    s.dma("qpool", UCOL[:], d["ucol"][:], writes=[UCOL])
    VBROW = ctx.sbuf("l2a_vbrow", [1, 2048], BF16)
    s.dma("qpool", VBROW[:], d["vbrow"][:], writes=[VBROW])
    BASE = ctx.sbuf("l2a_base", [128, T], BF16)
    QR = ctx.sbuf("l2a_q", [128, T], BF16)
    KR = ctx.sbuf("l2a_k", [128, T], BF16)
    RC2 = ctx.sbuf("l2a_rc2", [128, T], BF16)
    RSS = ctx.sbuf("l2a_rss", [128, T], BF16)
    SWP = ctx.sbuf("l2a_swp", [128, 128], BF16)
    s.dma("qpool", RC2[:], d["ropec"][:], writes=[RC2])
    s.dma("qpool", RSS[:], d["ropes"][:], writes=[RSS])
    s.dma("qpool", SWP[:], d["swp"][:], writes=[SWP])
    TMPB = ctx.sbuf("l2a_tmpb", [128, 512], BF16)
    TM = [ctx.sbuf(f"l2a_tm{i}", [128, 512], F32) for i in range(3)]
    OUTB = [ctx.sbuf(f"l2a_ob{i}", [128, 512], BF16) for i in range(2)]
    BIAS = [ctx.sbuf(f"l2a_bias{i}", [128, 512], F32) for i in range(2)]
    ARENA2 = ctx.sbuf("l2a_arena", [128, 16, 1024], BF16)
    VCOL = ARENA2[:].rearrange("p a b -> p (a b)").rearrange(
        "p (k n) -> p k n", n=512)
    UT = [ctx.sbuf(f"l2a_ut{i}", [128, 512], BF16) for i in range(2)]

    for ug in range(4):
        uvw = UVW[ug % 2]
        s.dma("qpool", uvw[:], uvwr[:, :, ug * 512:(ug + 1) * 512], writes=[uvw])
        for it in range(8):
            tsl = slice(it * 512, (it + 1) * 512)
            xb = ctx.xnt[it % 2]
            s.dma("qsp", xb[:], xn2r[:, :, tsl], reads=[d["xn2"]], writes=[xb])
            for ui in range(4):
                uc = ug * 4 + ui
                pg = ctx.pb[ui % 2]
                for c in range(CH):
                    ctx.mm(pg[:], uvw[:, c, ui * 128:(ui + 1) * 128],
                           xb[:, c, :], c == 0, c == CH - 1,
                           reads=[uvw, xb], writes=[pg])
                ob = OUTB[ui % 2]
                ctx.act(ob[:], pg[:], AF.Silu, reads=[pg, UCOL], writes=[ob],
                        bias=UCOL[:, uc:uc + 1])
                s.dma("qsp", ur[:, uc, tsl], ob[:], reads=[ob], writes=[d["u"]])
    for vg in range(4):
        uvw = UVW[vg % 2]
        s.dma("qpool", uvw[:], uvwr[:, :, 2048 + vg * 512:2560 + vg * 512],
              writes=[uvw])
        for it in range(8):
            tsl = slice(it * 512, (it + 1) * 512)
            xb = ctx.xnt[it % 2]
            s.dma("qsp", xb[:], xn2r[:, :, tsl], reads=[d["xn2"]], writes=[xb])
            for st_ in range(4):
                trow = it * 512 + st_ * 128
                pv = ctx.pb[2 + st_ % 2]
                for c in range(CH):
                    ctx.mm(pv[:], xb[:, c, st_ * 128:st_ * 128 + 128],
                           uvw[:, c, :], c == 0, False, reads=[xb, uvw],
                           writes=[pv])
                ctx.mm(pv[:], ctx.ones_row_b[:],
                       VBROW[0:1, vg * 512:(vg + 1) * 512], False, True,
                       reads=[ctx.ones_row_b, VBROW], writes=[pv])
                ob = OUTB[st_ % 2]
                ctx.act(ob[:], pv[:], AF.Silu, reads=[pv], writes=[ob])
                s.dma("qsp", d["v"][trow:trow + 128, vg * 512:(vg + 1) * 512],
                      ob[:], reads=[ob], writes=[d["v"]])
    uvw = UVW[0]
    s.dma("qpool", uvw[:, :, 0:128], uvwr[:, :, 4096:4224], writes=[uvw])
    for it in range(8):
        tsl = slice(it * 512, (it + 1) * 512)
        xb = ctx.xnt[it % 2]
        s.dma("qsp", xb[:], xn2r[:, :, tsl], reads=[d["xn2"]], writes=[xb])
        pg = ctx.pb[4]
        for c in range(CH):
            ctx.mm(pg[:], uvw[:, c, 0:128], xb[:, c, :], c == 0, c == CH - 1,
                   reads=[uvw, xb], writes=[pg])
        ctx.act(BASE[:, tsl], pg[:], AF.Silu, reads=[pg, COLS],
                writes=[(BASE, it)], bias=COLS[:, 16:17])
    for which, dst in [(0, QR), (1, KR)]:
        gc = COLS[:, 18 + 2 * which:19 + 2 * which]
        bc = COLS[:, 19 + 2 * which:20 + 2 * which]
        for it in range(8):
            tsl = slice(it * 512, (it + 1) * 512)
            s.op("dve", lambda e, tsl=tsl, gc=gc, bc=bc: e.tensor_scalar(
                TMPB[:], BASE[:, tsl], gc, bc, ALU.mult, ALU.add),
                reads=[(BASE, it), COLS], writes=[TMPB])
            psw = ctx.pb[5]
            ctx.mm(psw[:], SWP[:], TMPB[:], True, True, reads=[SWP, TMPB],
                   writes=[psw])
            ctx.tt(TM[0][:], TMPB[:], RC2[:, tsl], ALU.mult,
                   reads=[TMPB, RC2], writes=[TM[0]])
            ctx.tt(TM[1][:], psw[:], RSS[:, tsl], ALU.mult,
                   reads=[psw, RSS], writes=[TM[1]])
            s.op("dve", lambda e, dst=dst, tsl=tsl: e.tensor_tensor(
                dst[:, tsl], TM[0][:], TM[1][:], ALU.add),
                reads=[TM[0], TM[1]], writes=[(dst, it)])

    def pts(kt, n=1):
        buf = ctx.xtf[kt // 16]
        v = buf[:, :, :].rearrange("p c t -> p (c t)").bitcast(BF16)
        return v.rearrange("p (k q) -> p k q", q=512)[:, kt % 16:kt % 16 + n, :]

    ISQ = 1.0 / float(np.sqrt(128.0))
    for qg in range(8):
        qsl = slice(qg * 512, (qg + 1) * 512)
        for kt in range(32):
            pst = ctx.pbw[:, (kt % 2) * 512:(kt % 2) * 512 + 512]
            ctx.mm(pst, KR[:, kt * 128:(kt + 1) * 128], QR[:, qsl], True, True,
                   reads=[(KR, kt // 4), (QR, qg)], writes=[(ctx.pbw, kt % 2)])
            bt = BIAS[kt % 2]
            s.dma("qsp", bt[:], d["bias"][kt * 128:(kt + 1) * 128, qsl],
                  reads=[d["bias"]], writes=[bt])
            tm = TM[2]
            s.op("dve", lambda e, pst=pst, bt=bt, tm=tm: e.scalar_tensor_tensor(
                tm[:], pst, ISQ, bt[:], ALU.mult, ALU.add),
                reads=[(ctx.pbw, kt % 2), bt], writes=[tm])
            s.op("dve", lambda e, tm=tm: e.tensor_scalar(
                tm[:], tm[:], 0.0, None, ALU.max), reads=[tm], writes=[tm])
            ctx.act(pts(kt)[:, 0, :], tm[:], AF.Square, reads=[tm],
                    writes=[(ctx.xtf[kt // 16], kt)])
        for mg in range(4):
            vc = VCOL
            s.dma("qsp", vc[:], d["v"].rearrange("(a p) m -> p a m", p=128)[
                  :, :, mg * 512:(mg + 1) * 512], reads=[d["v"]], writes=[vc])
            for ms in range(4):
                mc = mg * 4 + ms
                pa = ctx.pb[2 + ms % 2]
                for kt in range(32):
                    ctx.mm(pa[:], vc[:, kt, ms * 128:(ms + 1) * 128],
                           pts(kt)[:, 0, :], kt == 0, kt == 31,
                           reads=[vc, (ctx.xtf[kt // 16], kt)], writes=[pa])
                ut = UT[mc % 2]
                s.dma("qsp", ut[:], ur[:, mc, qsl], reads=[d["u"]],
                      writes=[ut])
                ob = OUTB[mc % 2]
                s.op("dve", lambda e, pa=pa, ut=ut, ob=ob: e.tensor_tensor(
                    ob[:], ut[:], pa[:], ALU.mult), reads=[pa, ut],
                    writes=[ob])
                s.dma("qsp", p2r[:, mc, qsl], ob[:], reads=[ob],
                      writes=[d["p2"]])

    s.fence()
    OWT = ARENA2
    owr = d["owT"].rearrange("(m p) o -> p m o", p=128)
    for g in range(2):
        s.dma("qpool", OWT[:, 8 * g:8 * g + 8, :], owr[:, 8 * g:8 * g + 8, :],
              writes=[OWT])
    P2T = ctx.sbuf("l2a_p2t", [128, 16, 512], BF16)
    for it in range(8):
        tsl = slice(it * 512, (it + 1) * 512)
        s.dma("qsp", P2T[:], p2r[:, :, tsl], reads=[d["p2"]], writes=[P2T])
        xb = ctx.xnt[it % 2]
        s.dma("qsp", xb[:], xnr[:, :, tsl], reads=[d["xn"]], writes=[xb])
        xt = ctx.xtf[it % 2]
        s.dma("qsp", xt[:], xr[:, :, tsl], reads=[d["x"]], writes=[xt])
        for oc in range(CH):
            po = ctx.pb[4 + oc % 2]
            for m in range(16):
                ctx.mm(po[:], OWT[:, m, oc * 128:(oc + 1) * 128], P2T[:, m, :],
                       m == 0, m == 15, reads=[OWT, P2T], writes=[po])
            tm = TM[oc % 2]
            s.op("dve", lambda e, po=po, tm=tm, oc=oc: e.tensor_scalar(
                tm[:], po[:], COLS[:, 24 + oc:25 + oc], None, ALU.add),
                reads=[po, COLS], writes=[tm])
            ctx.tt(tm[:], tm[:], xb[:, oc, :], ALU.add, reads=[tm, xb],
                   writes=[tm])
            ctx.tt(xt[:, oc, :], xt[:, oc, :], tm[:], ALU.add,
                   reads=[xt, tm], writes=[xt])
        s.dma("qsp", xor_[:, :, tsl], xt[:], reads=[xt], writes=[d["xo"]])


def _build_l2a():
    nc = _mk_nc()
    d = {}
    d["x"] = nc.dram_tensor("x", [D, T], F32, kind="ExternalInput").ap()
    d["xo"] = nc.dram_tensor("xo", [D, T], F32, kind="ExternalOutput").ap()
    for nm, sh in [("xn", [D, T]), ("xn2", [D, T]), ("u", [2048, T]),
                   ("v", [T, 2048]), ("p2", [2048, T])]:
        d[nm] = nc.dram_tensor(nm, sh, BF16, kind="Internal").ap()
    d["uvwT"] = nc.dram_tensor("uvwT", [D, 4224], F32, kind="ExternalInput").ap()
    d["owT"] = nc.dram_tensor("owT", [2048, D], F32, kind="ExternalInput").ap()
    d["cols"] = nc.dram_tensor("cols", [128, 32], F32, kind="ExternalInput").ap()
    d["ucol"] = nc.dram_tensor("ucol", [128, 16], F32, kind="ExternalInput").ap()
    d["vbrow"] = nc.dram_tensor("vbrow", [1, 2048], F32, kind="ExternalInput").ap()
    d["ropec"] = nc.dram_tensor("ropec", [128, T], F32, kind="ExternalInput").ap()
    d["ropes"] = nc.dram_tensor("ropes", [128, T], F32, kind="ExternalInput").ap()
    d["swp"] = nc.dram_tensor("swp", [128, 128], BF16, kind="ExternalInput").ap()
    d["bias"] = nc.dram_tensor("bias", [T, T], F32, kind="ExternalInput").ap()
    es = contextlib.ExitStack()
    with es:
        ctx = Ctx(nc, es)
        emit_l2a(ctx, d)
        ctx.s.build(es)
    return nc


def _prep_l2a(inp, p):
    p["l2a_uvwT"] = np.ascontiguousarray(np.asarray(inp["L2_attn_uv_w"],
                                                    np.float32).T)
    p["l2a_owT"] = np.ascontiguousarray(np.asarray(inp["L2_attn_o_w"],
                                                   np.float32).T)
    uvb = np.asarray(inp["L2_attn_uv_b"], np.float32)
    cols = np.zeros((128, 32), np.float32)
    cols[:, 0:8] = _col(inp["L2_attn_ln_w"])
    cols[:, 8:16] = _col(inp["L2_attn_ln_b"])
    cols[:, 16:17] = uvb[4096:4224].reshape(1, 128).T
    g = np.asarray(inp["L2_attn_gamma"], np.float32)
    b = np.asarray(inp["L2_attn_beta"], np.float32)
    cols[:, 18] = g[0]
    cols[:, 19] = b[0]
    cols[:, 20] = g[1]
    cols[:, 21] = b[1]
    cols[:, 24:32] = _col(inp["L2_attn_o_b"])
    p["l2a_cols"] = cols
    p["l2a_ucol"] = _col(uvb[0:2048])
    p["l2a_vbrow"] = np.ascontiguousarray(uvb[2048:4096].reshape(1, 2048))
    pos = np.arange(T, dtype=np.float64)
    half = 64
    inv_freq = 10000.0 ** (np.arange(half, dtype=np.float64) / half)
    sinu = pos[None, :] * inv_freq[:, None]
    cosv = np.cos(sinu)
    sinv = np.sin(sinu)
    p["l2a_ropec"] = np.concatenate([cosv, cosv]).astype(np.float32)
    p["l2a_ropes"] = np.concatenate([-sinv, sinv]).astype(np.float32)
    swp = np.zeros((128, 128), np.float32)
    for i_ in range(128):
        swp[i_, (i_ + 64) % 128] = 1.0
    p["l2a_swp"] = swp.astype(BF)
    w = np.asarray(inp["L2_attn_w"], np.float64)
    idx = np.clip(np.arange(T)[:, None] - np.arange(T)[None, :] + 511, 0, 1022)
    p["l2a_bias"] = w[idx].astype(np.float32)
    return p


def run_l2a(xs, prep):
    n = len(xs)
    in_maps = [{"x": xs[i], **{k: prep[f"l2a_{k}"] for k in
                ["uvwT", "owT", "cols", "ucol", "vbrow", "ropec", "ropes",
                 "swp", "bias"]}} for i in range(n)]
    res = _run_phase("l2a", _build_l2a, in_maps)
    return [np.asarray(res.results[i]["xo"]) for i in range(n)]


def emit_rms(ctx, src, dst, gcol, src_dt=BF16, eps=1e-12, also_tok=None,
             eyeb=None, blk=False, dst2=None):
    s = ctx.s
    sr = src.rearrange("(c p) t -> p c t", p=128)
    dr = None if blk else dst.rearrange("(c p) t -> p c t", p=128)
    gb, go = gcol
    r0 = ctx.lnrow(0)
    XMS = ctx.xnt
    for i in range(T // 512):
        b = i % 2
        tsl = slice(i * 512, (i + 1) * 512)
        xt = XMS[b]
        ps, pbc = ctx.pb[0 + b], ctx.pb[2 + b]
        s.dma("qpool" if src_dt == F32 else "qsp", xt[:], sr[:, :, tsl],
              reads=[src], writes=[xt])
        ctx.act(ctx.sq[:, :, :], xt[:, :, :], AF.Square, reads=[xt],
                writes=[ctx.sq])
        for c in range(CH):
            ctx.mm(ps[0:1, :], ctx.ones_col_b[:], ctx.sq[:, c, :], c == 0,
                   c == CH - 1, reads=[ctx.sq, ctx.ones_col_b], writes=[ps])
        ctx.act(r0[:], ps[0:1, :], AF.Sqrt, reads=[ps], writes=[r0], bias=0.0)
        s.op("dve", lambda e: e.tensor_scalar(r0[:], r0[:], float(eps), None,
             ALU.max), reads=[r0], writes=[r0])
        s.op("dve", lambda e: e.reciprocal(r0[:], r0[:]), reads=[r0],
             writes=[r0])
        s.op("dve", lambda e: e.tensor_scalar(r0[:], r0[:], 32.0, None,
             ALU.mult), reads=[r0], writes=[r0])
        ctx.mm(pbc[:], ctx.ones_row_f[:], r0[:], True, True,
               reads=[r0, ctx.ones_row_f], writes=[pbc])
        for c in range(CH):
            ctx.tt(xt[:, c, :], xt[:, c, :], pbc[:], ALU.mult,
                   reads=[xt, pbc], writes=[xt])
        for c in range(CH):
            s.op("dve", lambda e, c=c, xt=xt: e.tensor_scalar(
                xt[:, c, :], xt[:, c, :], gb[:, go + c:go + c + 1], None,
                ALU.mult), reads=[xt, gb], writes=[xt])
        if blk:
            for j in range(4):
                dv = dst[i * 4 + j, :, :].rearrange("p (c n) -> p c n", n=128)
                s.dma("qsp", dv, xt[:, :, j * 128:(j + 1) * 128],
                      reads=[xt], writes=[dst])
            if dst2 is not None:
                s.dma("qsp", dst2.rearrange("(c p) t -> p c t", p=128)[
                      :, :, tsl], xt[:], reads=[xt], writes=[dst2])
        else:
            s.dma("qsp", dr[:, :, tsl], xt[:], reads=[xt], writes=[dst])
        if also_tok is not None:
            for j in range(4):
                for c in range(CH):
                    ptr = ctx.pb[4 + c % 2]
                    pv = ptr[:, 0:64].bitcast(BF16)
                    s.op("pe", lambda e, pv=pv, c=c, xt=xt, j=j: e.transpose(
                        pv[:, 0:128], xt[:, c, j * 128:(j + 1) * 128],
                        eyeb[:]), reads=[xt, eyeb], writes=[ptr])
                    ctx.act(TTK[:, c, :], pv[:, 0:128], AF.Copy,
                            reads=[ptr], writes=[(TTK, c)])
                s.dma("qsp", also_tok[i * 512 + j * 128:
                      i * 512 + j * 128 + 128, :],
                      TTK[:].rearrange("p c t -> p (c t)"),
                      reads=[(TTK, c_) for c_ in range(CH)], writes=[also_tok])


TTK = None


def emit_l2f(ctx, d):
    s = ctx.s
    COLS = ctx.sbuf("l2f_cols", [128, 16], F32)
    s.dma("qpool", COLS[:], d["cols"][:], writes=[COLS])
    EYEB = ctx.sbuf("l2f_eyeb", [128, 128], BF16)
    s.dma("qpool", EYEB[:], d["eyeb"][:], writes=[EYEB])
    EYEF = ctx.sbuf("l2f_eyef", [128, 128], F32)
    s.dma("qpool", EYEF[:], d["eyef"][:], writes=[EYEF])
    XMF = ctx.sbuf("l2f_xmf", [128, CH, T], BF16)
    XMT = ctx.sbuf("l2f_xmt", [128, 32, 1024], BF16)
    HB = lambda hs: XMT[:, hs, 0:512]
    CS = lambda nb: XMT[:, nb, 512:1024]
    xf0 = ctx.xtf[0][:].rearrange("p c t -> p (c t)")
    xf1 = ctx.xtf[1][:].rearrange("p c t -> p (c t)")
    xb0 = xf0.bitcast(BF16)
    xb1 = xf1.bitcast(BF16)
    PRB = [xb0[:, 0:1024], xb0[:, 1024:2048]]
    ETM = xb0[:, 2048:2560]
    BCT = xf0[:, 1536:2048]
    CTF = xf0[:, 2048:2560]
    RZB = xf0[:, 2560:3072]
    W1S = [xb1[:, 0:1024].rearrange("p (c n) -> p c n", n=128),
           xb1[:, 1024:2048].rearrange("p (c n) -> p c n", n=128)]
    W2C = [xb1[:, 2048:3072], xb1[:, 3072:4096]]
    SLE = [xb1[:, 4096:5120], xb1[:, 5120:6144]]

    w1r = d["w1"].rearrange("(c p) h -> p c h", p=128)
    w2r = d["w2"].rearrange("(hc p) o -> p hc o", p=128)
    w2br = d["w2b"].rearrange("(hc p) o -> p hc o", p=128)
    for g in range(8):
        t0 = ctx.xnt[g % 2]
        s.dma("qpool", t0[:], w1r[:, :, g * 512:(g + 1) * 512],
              reads=[d["w1"]], writes=[t0])
        for j in range(4):
            dv = d["w1b"][g * 4 + j, :, :].rearrange("p (c n) -> p c n", n=128)
            s.dma("qpool", dv, t0[:, :, j * 128:(j + 1) * 128],
                  reads=[t0], writes=[d["w1b"]])
    for g in range(8):
        t0 = ctx.xnt[g % 2]
        v = t0[:].rearrange("p c t -> p (c t)").rearrange(
            "p (a b) -> p a b", b=1024)
        s.dma("qpool", v, w2r[:, g * 4:(g + 1) * 4, :],
              reads=[d["w2"]], writes=[t0])
        s.dma("qpool", w2br[:, g * 4:(g + 1) * 4, :], v,
              reads=[t0], writes=[d["w2b"]])

    xr = d["x"].rearrange("(c p) t -> p c t", p=128)
    mu = ctx.lnrow(0)
    msq = ctx.lnrow(1)
    var = ctx.lnrow(2)
    rstd = ctx.lnrow(3)
    for i in range(8):
        b = i % 2
        xt = ctx.xtf[b]
        ps, pq2 = ctx.pb[0 + b], ctx.pb[2 + b]
        pmu, prs = ctx.pb[4 + b], ctx.pb[6 + b]
        tsl = slice(i * 512, (i + 1) * 512)
        s.dma("qsp", xt[:], xr[:, :, tsl], reads=[d["x"]], writes=[xt])
        ctx.act(ctx.sq[:, :, :], xt[:, :, :], AF.Square, reads=[xt],
                writes=[ctx.sq])
        for c in range(CH):
            ctx.mm(ps[0:1, :], ctx.ones_col_f[:], xt[:, c, :], c == 0,
                   c == CH - 1, reads=[xt, ctx.ones_col_f], writes=[ps])
        for c in range(CH):
            ctx.mm(pq2[0:1, :], ctx.ones_col_b[:], ctx.sq[:, c, :], c == 0,
                   c == CH - 1, reads=[ctx.sq, ctx.ones_col_b], writes=[pq2])
        ctx.act(mu[:], ps[0:1, :], AF.Copy, reads=[ps], writes=[mu],
                scale=1.0 / D)
        ctx.act(msq[:], pq2[0:1, :], AF.Copy, reads=[pq2], writes=[msq],
                scale=1.0 / D)
        ctx.tt(var[:], mu[:], mu[:], ALU.mult, reads=[mu], writes=[var])
        ctx.tt(var[:], msq[:], var[:], ALU.subtract, reads=[msq, var],
               writes=[var])
        ctx.act(var[:], var[:], AF.Sqrt, reads=[var], writes=[var], bias=1e-12)
        s.op("dve", lambda e: e.reciprocal(rstd[:], var[:]), reads=[var],
             writes=[rstd])
        ctx.mm(pmu[:], ctx.ones_row_f[:], mu[:], True, True,
               reads=[mu, ctx.ones_row_f], writes=[pmu])
        ctx.mm(prs[:], ctx.ones_row_f[:], rstd[:], True, True,
               reads=[rstd, ctx.ones_row_f], writes=[prs])
        for c in range(CH):
            ctx.tt(xt[:, c, :], xt[:, c, :], pmu[:], ALU.subtract,
                   reads=[xt, pmu], writes=[xt])
        for c in range(CH):
            ctx.tt(xt[:, c, :], xt[:, c, :], prs[:], ALU.mult,
                   reads=[xt, prs], writes=[xt])
        for c in range(CH):
            s.op("dve", lambda e, c=c, xt=xt, tsl=tsl: e.tensor_scalar(
                XMF[:, c, tsl], xt[:, c, :], COLS[:, c:c + 1], None, ALU.mult),
                reads=[xt, COLS], writes=[(XMF, i, c)])
        for j in range(4):
            tc = i * 4 + j
            for c in range(CH):
                ptr = ctx.pb[4 + c % 2]
                pv = ptr[:, 0:64].bitcast(BF16)
                s.op("pe", lambda e, pv=pv, c=c, i=i, j=j: e.transpose(
                    pv[:, 0:128],
                    XMF[:, c, i * 512 + j * 128:i * 512 + (j + 1) * 128],
                    EYEB[:]), reads=[(XMF, i, c), EYEB], writes=[ptr])
                ctx.act(XMT[:, tc, c * 128:(c + 1) * 128], pv[:, 0:128],
                        AF.Copy, reads=[ptr], writes=[(XMT, tc, c)])

    emit_rms(ctx, d["seT"], d["se"], (COLS, 8), src_dt=F32, blk=True,
             dst2=d["sef"])
    sefr = d["sef"].rearrange("(c p) n -> p c n", p=128)

    def seblk(nb):
        return d["se"][nb, :, :].rearrange("p (c n) -> p c n", n=128)

    SEB = [ctx.sbuf(f"l2f_seb{i}", [128, CH, 128], BF16) for i in range(2)]
    PTM = [ctx.sbuf(f"l2f_ptm{i}", [128, 128], BF16) for i in range(2)]
    TM = ctx.sbuf("l2f_tm", [128, 8], F32)
    ZT8 = ctx.sbuf("l2f_zt8", [128, 8], F32)
    MC = ctx.sbuf("l2f_mc", [128, 1], F32)
    NMC = ctx.sbuf("l2f_nmc", [128, 1], F32)
    RZ = ctx.sbuf("l2f_rz", [128, 1], F32)
    LS = ctx.xtf[0]
    PS = ctx.sq

    for nb in range(32):
        seb = SEB[nb % 2]
        s.dma("qsp", seb[:], seblk(nb), reads=[d["se"]], writes=[seb])
        for tt in range(8):
            pl = ctx.pb[tt % 2]
            for c in range(CH):
                ctx.mm(pl[:], seb[:, c, :], XMF[:, c, tt * 512:(tt + 1) * 512],
                       c == 0, c == CH - 1, reads=[seb, XMF], writes=[pl])
            ctx.act(LS[:, tt, :], pl[:], AF.Copy, reads=[pl], writes=[(LS, tt)])
            s.op("dve", lambda e, tt=tt: e.reduce_max(
                TM[:, tt:tt + 1], LS[:, tt, :], axis=AX.X),
                reads=[(LS, tt)], writes=[(TM, tt)])
        s.op("dve", lambda e: e.reduce_max(MC[:], TM[:], axis=AX.X),
             reads=[(TM, t_) for t_ in range(8)], writes=[MC])
        s.op("dve", lambda e: e.tensor_scalar(NMC[:], MC[:], -1.0, None,
             ALU.mult), reads=[MC], writes=[NMC])
        for tt in range(8):
            ctx.act(PS[:, tt, :], LS[:, tt, :], AF.Exp,
                    reads=[(LS, tt), NMC], writes=[(PS, tt)], bias=NMC[:, 0:1])
            s.op("dve", lambda e, tt=tt: e.reduce_sum(
                ZT8[:, tt:tt + 1], PS[:, tt, :], axis=AX.X),
                reads=[(PS, tt)], writes=[(ZT8, tt)])
        s.op("dve", lambda e: e.reduce_sum(MC[:], ZT8[:], axis=AX.X),
             reads=[(ZT8, t_) for t_ in range(8)], writes=[MC])
        s.op("dve", lambda e: e.reciprocal(RZ[:], MC[:]), reads=[MC],
             writes=[RZ])
        for tt in range(8):
            s.op("dve", lambda e, tt=tt: e.tensor_scalar(
                PS[:, tt, :], PS[:, tt, :], RZ[:, 0:1], None, ALU.mult),
                reads=[(PS, tt), RZ], writes=[(PS, tt)])
        for tt in range(8):
            for j in range(4):
                tc = tt * 4 + j
                ptr = ctx.pb[2 + tc % 2]
                pv = ptr[:, 0:64].bitcast(BF16)
                s.op("pe", lambda e, pv=pv, tt=tt, j=j: e.transpose(
                    pv[:, 0:128], PS[:, tt, j * 128:(j + 1) * 128], EYEB[:]),
                    reads=[(PS, tt), EYEB], writes=[ptr])
                pt = PTM[tc % 2]
                ctx.act(pt[:], pv[:, 0:128], AF.Copy, reads=[ptr], writes=[pt])
                ctx.mm(ctx.pb[4][:], pt[:], XMT[:, tc, 0:512], tc == 0,
                       tc == 31, reads=[pt, XMT], writes=[ctx.pb[4]])
                ctx.mm(ctx.pb[5][:], pt[:], XMT[:, tc, 512:1024], tc == 0,
                       tc == 31, reads=[pt, XMT], writes=[ctx.pb[5]])
        sle = SLE[nb % 2]
        ctx.act(sle[:, 0:512], ctx.pb[4][:], AF.Copy, reads=[ctx.pb[4]],
                writes=[(ctx.xtf[1], "sle", nb % 2, 0)])
        ctx.act(sle[:, 512:1024], ctx.pb[5][:], AF.Copy, reads=[ctx.pb[5]],
                writes=[(ctx.xtf[1], "sle", nb % 2, 1)])
        s.dma("qsp", d["sl"][nb * 128:(nb + 1) * 128, :], sle,
              reads=[(ctx.xtf[1], "sle", nb % 2, 0),
                     (ctx.xtf[1], "sle", nb % 2, 1)], writes=[d["sl"]])

    s.fence()
    for ng in range(8):
        sls = ctx.xnt[ng % 2]
        for k in range(4):
            slb = SLE[k % 2]
            s.dma("qsp", slb, d["sl"][(ng * 4 + k) * 128:
                  (ng * 4 + k + 1) * 128, :], reads=[d["sl"]],
                  writes=[(ctx.xtf[1], "sle", k % 2, 0),
                          (ctx.xtf[1], "sle", k % 2, 1)])
            for c in range(CH):
                ptr = ctx.pb[2 + c % 2]
                pv = ptr[:, 0:64].bitcast(BF16)
                s.op("pe", lambda e, pv=pv, slb=slb, c=c: e.transpose(
                    pv[:, 0:128], slb[:, c * 128:(c + 1) * 128], EYEB[:]),
                    reads=[(ctx.xtf[1], "sle", k % 2, 0),
                           (ctx.xtf[1], "sle", k % 2, 1), EYEB], writes=[ptr])
                ctx.act(sls[:, c, k * 128:(k + 1) * 128], pv[:, 0:128],
                        AF.Copy, reads=[ptr], writes=[(sls, k, c)])
        for hs in range(32):
            w1s = W1S[hs % 2]
            s.dma("qpool", w1s,
                  d["w1b"][hs, :, :].rearrange("p (c n) -> p c n", n=128),
                  reads=[d["w1b"]], writes=[(ctx.xtf[1], "w1", hs % 2)])
            pl = ctx.pb[hs % 2]
            for c in range(CH):
                ctx.mm(pl[:], w1s[:, c, :], sls[:, c, :], c == 0, c == CH - 1,
                       reads=[(ctx.xtf[1], "w1", hs % 2), sls], writes=[pl])
            ctx.act(HB(hs), pl[:], AF.Gelu, reads=[pl],
                    writes=[(XMT, "hb", hs)])
        for hc in range(32):
            w2c = W2C[hc % 2]
            s.dma("qpool", w2c, w2br[:, hc, :], reads=[d["w2b"]],
                  writes=[(ctx.xtf[1], "w2", hc % 2)])
            for ns in range(4):
                for half in range(2):
                    idx = ns * 2 + half
                    ap = (ctx.pb[idx][:] if idx < 6 else
                          ctx.pbw[:, (idx - 6) * 512:(idx - 5) * 512])
                    ctx.mm(ap, XMT[:, hc, ns * 128:(ns + 1) * 128],
                           w2c[:, half * 512:(half + 1) * 512],
                           hc == 0, hc == 31,
                           reads=[(XMT, "hb", hc), (ctx.xtf[1], "w2", hc % 2)],
                           writes=[ctx.pb[idx]] if idx < 6 else
                           [(ctx.pbw, idx - 6)])
        for ns in range(4):
            prb = PRB[ns % 2]
            for half in range(2):
                idx = ns * 2 + half
                ap = (ctx.pb[idx][:] if idx < 6 else
                      ctx.pbw[:, (idx - 6) * 512:(idx - 5) * 512])
                ctx.act(prb[:, half * 512:(half + 1) * 512], ap, AF.Copy,
                        reads=[ctx.pb[idx]] if idx < 6 else
                        [(ctx.pbw, idx - 6)],
                        writes=[(ctx.xtf[0], "prb", ns % 2, half)])
            s.dma("qsp", d["proc"][ng * 512 + ns * 128:
                  ng * 512 + (ns + 1) * 128, :], prb,
                  reads=[(ctx.xtf[0], "prb", ns % 2, 0),
                         (ctx.xtf[0], "prb", ns % 2, 1)], writes=[d["proc"]])

    M4 = ctx.sbuf("l2f_m4", [128, 4], F32)
    Z4 = ctx.sbuf("l2f_z4", [128, 4], F32)
    RM = ctx.sbuf("l2f_rmax", [128, 1], F32)
    SC1 = ctx.sbuf("l2f_sc1", [128, 1], F32)
    RS = ctx.sbuf("l2f_rs", [128, 1], F32)
    LSETR = ctx.sbuf("l2f_lsetr", [1, 512], F32)
    HIB = ctx.sbuf("l2f_hib", [1, 512], BF16)
    HIF = ctx.sbuf("l2f_hif", [1, 512], F32)
    LOB = ctx.sbuf("l2f_lob", [1, 512], BF16)
    xor_ = d["xo"].rearrange("(c p) t -> p c t", p=128)
    for tg in range(8):
        tsl = slice(tg * 512, (tg + 1) * 512)
        s.op("pool", lambda e: e.memset(M4[:], -1e30), writes=[M4])
        for ng2 in range(8):
            seg = ctx.xnt[ng2 % 2]
            s.dma("qsp", seg[:], sefr[:, :, ng2 * 512:(ng2 + 1) * 512],
                  reads=[d["sef"]], writes=[seg])
            for q in range(4):
                pl = ctx.pb[(ng2 * 4 + q) % 2]
                for c in range(CH):
                    ctx.mm(pl[:],
                           XMF[:, c, tg * 512 + q * 128:tg * 512 + (q + 1) * 128],
                           seg[:, c, :], c == 0, c == CH - 1,
                           reads=[XMF, seg], writes=[pl])
                s.op("dve", lambda e, pl=pl: e.reduce_max(RM[:], pl[:],
                     axis=AX.X), reads=[pl], writes=[RM])
                s.op("dve", lambda e, q=q: e.tensor_tensor(
                    M4[:, q:q + 1], M4[:, q:q + 1], RM[:], ALU.max),
                    reads=[(M4, q), RM], writes=[(M4, q)])
        for q in range(4):
            pq2 = ctx.pb[2]
            ctx.mm(pq2[0:1, 0:128], M4[:, q:q + 1], EYEF[:], True, True,
                   reads=[(M4, q), EYEF], writes=[pq2])
            ctx.act(LSETR[0:1, q * 128:(q + 1) * 128], pq2[0:1, 0:128],
                    AF.Copy, reads=[pq2], writes=[(LSETR, q)])
        ctx.act(HIB[:], LSETR[:], AF.Copy,
                reads=[(LSETR, q_) for q_ in range(4)], writes=[HIB])
        ctx.act(HIF[:], HIB[:], AF.Copy, reads=[HIB], writes=[HIF])
        s.op("dve", lambda e: e.tensor_tensor(
            HIF[:], LSETR[:], HIF[:], ALU.subtract),
            reads=[(LSETR, q_) for q_ in range(4)] + [HIF], writes=[HIF])
        ctx.act(LOB[:], HIF[:], AF.Copy, reads=[HIF], writes=[LOB])
        pq2 = ctx.pb[3]
        ctx.mm(pq2[:], ctx.ones_row_b[:], HIB[:], True, False,
               reads=[ctx.ones_row_b, HIB], writes=[pq2])
        ctx.mm(pq2[:], ctx.ones_row_b[:], LOB[:], False, True,
               reads=[ctx.ones_row_b, LOB], writes=[pq2])
        ctx.act(BCT, pq2[:], AF.Copy, reads=[pq2],
                writes=[(ctx.xtf[0], "bct")])
        for nb in range(32):
            seb = SEB[nb % 2]
            s.dma("qsp", seb[:], seblk(nb), reads=[d["se"]], writes=[seb])
            pl = ctx.pb[nb % 2]
            for c in range(CH):
                ctx.mm(pl[:], seb[:, c, :], XMF[:, c, tsl], c == 0,
                       c == CH - 1, reads=[seb, XMF], writes=[pl])
            s.op("dve", lambda e, pl=pl: e.tensor_tensor(
                CTF, pl[:], BCT, ALU.subtract),
                reads=[pl, (ctx.xtf[0], "bct")], writes=[(ctx.xtf[0], "ctf")])
            ctx.act(CS(nb), CTF, AF.Exp, reads=[(ctx.xtf[0], "ctf")],
                    writes=[(XMT, "cs", nb)])
            ctx.mm(ctx.pb[2][0:1, :], ctx.ones_col_b[:], CS(nb), nb == 0,
                   nb == 31, reads=[(XMT, "cs", nb), ctx.ones_col_b],
                   writes=[ctx.pb[2]])
        ctx.act(HIF[:], ctx.pb[2][0:1, :], AF.Copy, reads=[ctx.pb[2]],
                writes=[HIF])
        s.op("dve", lambda e: e.reciprocal(HIF[:], HIF[:]), reads=[HIF],
             writes=[HIF])
        ctx.act(HIB[:], HIF[:], AF.Copy, reads=[HIF], writes=[HIB])
        ctx.act(LSETR[:], HIB[:], AF.Copy, reads=[HIB], writes=[LSETR])
        s.op("dve", lambda e: e.tensor_tensor(
            LSETR[:], HIF[:], LSETR[:], ALU.subtract),
            reads=[HIF, LSETR], writes=[LSETR])
        ctx.act(LOB[:], LSETR[:], AF.Copy, reads=[LSETR], writes=[LOB])
        pq2 = ctx.pb[3]
        ctx.mm(pq2[:], ctx.ones_row_b[:], HIB[:], True, False,
               reads=[ctx.ones_row_b, HIB], writes=[pq2])
        ctx.mm(pq2[:], ctx.ones_row_b[:], LOB[:], False, True,
               reads=[ctx.ones_row_b, LOB], writes=[pq2])
        ctx.act(RZB, pq2[:], AF.Copy, reads=[pq2],
                writes=[(ctx.xtf[0], "rzb")])
        for nb in range(32):
            prb = PRB[nb % 2]
            s.dma("qsp", prb, d["proc"][nb * 128:(nb + 1) * 128, :],
                  reads=[d["proc"]],
                  writes=[(ctx.xtf[0], "prb", nb % 2, 0),
                          (ctx.xtf[0], "prb", nb % 2, 1)])
            for dc in range(CH):
                ap = (ctx.pb[dc][:] if dc < 6 else
                      ctx.pbw[:, (dc - 6) * 512:(dc - 5) * 512])
                ctx.mm(ap, prb[:, dc * 128:(dc + 1) * 128], CS(nb),
                       nb == 0, nb == 31,
                       reads=[(ctx.xtf[0], "prb", nb % 2, 0),
                              (ctx.xtf[0], "prb", nb % 2, 1), (XMT, "cs", nb)],
                       writes=[ctx.pb[dc]] if dc < 6 else [(ctx.pbw, dc - 6)])
        xt = ctx.xtf[1]
        s.dma("qsp", xt[:], xr[:, :, tsl], reads=[d["x"]], writes=[xt])
        for dc in range(CH):
            ap = (ctx.pb[dc][:] if dc < 6 else
                  ctx.pbw[:, (dc - 6) * 512:(dc - 5) * 512])
            s.op("dve", lambda e, ap=ap: e.tensor_tensor(
                CTF, ap, RZB, ALU.mult),
                reads=[(ctx.xtf[0], "rzb")] +
                ([ctx.pb[dc]] if dc < 6 else [(ctx.pbw, dc - 6)]),
                writes=[(ctx.xtf[0], "ctf")])
            ctx.tt(xt[:, dc, :], xt[:, dc, :], CTF, ALU.add,
                   reads=[xt, (ctx.xtf[0], "ctf")], writes=[xt])
        s.dma("qsp", xor_[:, :, tsl], xt[:], reads=[xt], writes=[d["xo"]])


def _build_l2f():
    nc = _mk_nc()
    d = {}
    d["x"] = nc.dram_tensor("x", [D, T], F32, kind="ExternalInput").ap()
    d["xo"] = nc.dram_tensor("xo", [D, T], F32, kind="ExternalOutput").ap()
    for nm, sh, dt in [("se", [32, 128, 1024], BF16), ("sef", [D, T], BF16),
                       ("sl", [T, D], BF16),
                       ("proc", [T, D], BF16), ("w1b", [32, 128, 1024], BF16),
                       ("w2b", [4096, D], BF16)]:
        d[nm] = nc.dram_tensor(nm, sh, dt, kind="Internal").ap()
    d["seT"] = nc.dram_tensor("seT", [D, T], F32, kind="ExternalInput").ap()
    d["w1"] = nc.dram_tensor("w1", [D, 4096], F32, kind="ExternalInput").ap()
    d["w2"] = nc.dram_tensor("w2", [4096, D], F32, kind="ExternalInput").ap()
    d["cols"] = nc.dram_tensor("cols", [128, 16], F32, kind="ExternalInput").ap()
    d["eyeb"] = nc.dram_tensor("eyeb", [128, 128], BF16, kind="ExternalInput").ap()
    d["eyef"] = nc.dram_tensor("eyef", [128, 128], F32, kind="ExternalInput").ap()
    es = contextlib.ExitStack()
    with es:
        ctx = Ctx(nc, es)
        emit_l2f(ctx, d)
        ctx.s.build(es)
    return nc


def _prep_l2f(inp, p):
    se = np.asarray(inp["L2_ffn_slot_embeds"], np.float32).reshape(T, D)
    p["l2f_seT"] = np.ascontiguousarray(se.T)
    p["l2f_w1"] = np.ascontiguousarray(np.asarray(inp["L2_ffn_w1"], np.float32))
    p["l2f_w2"] = np.ascontiguousarray(np.asarray(inp["L2_ffn_w2"], np.float32))
    cols = np.zeros((128, 16), np.float32)
    cols[:, 0:8] = _col(inp["L2_ffn_gamma_x"])
    cols[:, 8:16] = _col(inp["L2_ffn_gamma_slot"])
    p["l2f_cols"] = cols
    p["l2f_eyeb"] = np.eye(128, dtype=BF)
    p["l2f_eyef"] = np.eye(128, dtype=np.float32)
    return p


def run_l2f(xs, prep):
    n = len(xs)
    in_maps = [{"x": xs[i], **{k: prep[f"l2f_{k}"] for k in
                ["seT", "w1", "w2", "cols", "eyeb", "eyef"]}}
               for i in range(n)]
    res = _run_phase("l2f", _build_l2f, in_maps)
    return [np.asarray(res.results[i]["xo"]) for i in range(n)]


def _prep_shared(inp):
    p = {}
    p["l0f_wgT"] = np.ascontiguousarray(np.asarray(inp["L0_ffn_w_gate"]).T)
    p["l0f_wuT"] = np.ascontiguousarray(np.asarray(inp["L0_ffn_w_up"]).T)
    p["l0f_wdT"] = np.ascontiguousarray(np.asarray(inp["L0_ffn_w_down"]).T)
    return p


def _mk_nc():
    return bass.Bass("TRN2", target_bir_lowering=False, debug=False)


def _build_l0f():
    nc = _mk_nc()
    d = {}
    d["x"] = nc.dram_tensor("x", [D, T], F32, kind="ExternalInput").ap()
    d["xo"] = nc.dram_tensor("xo", [D, T], F32, kind="ExternalOutput").ap()
    d["xn"] = nc.dram_tensor("xn", [D, T], BF16, kind="Internal").ap()
    d["h"] = nc.dram_tensor("h", [3072, T], BF16, kind="Internal").ap()
    d["wgT"] = nc.dram_tensor("wgT", [D, 3072], I32, kind="ExternalInput").ap()
    d["wuT"] = nc.dram_tensor("wuT", [D, 3072], I32, kind="ExternalInput").ap()
    d["wdT"] = nc.dram_tensor("wdT", [3072, D], I32, kind="ExternalInput").ap()
    es = contextlib.ExitStack()
    with es:
        ctx = Ctx(nc, es)
        emit_l0f(ctx, d)
        ctx.s.build(es)
    return nc


_PROG_CACHE = {}


def _run_phase(name, builder, in_maps):
    if name not in _PROG_CACHE:
        _PROG_CACHE[name] = builder()
    return run_bass_kernel_spmd(_PROG_CACHE[name], in_maps,
                                core_ids=list(range(len(in_maps))))


def run_l0f(xs, prep):
    n = len(xs)
    in_maps = [{"x": xs[i], "wgT": prep["l0f_wgT"], "wuT": prep["l0f_wuT"],
                "wdT": prep["l0f_wdT"]} for i in range(n)]
    res = _run_phase("l0f", _build_l0f, in_maps)
    return [np.asarray(res.results[i]["xo"]) for i in range(n)]


_PH_INPUTS = {
    "l0a": [("wT", [D, 3072], F32), ("qkcols", [128, 16], F32),
            ("vrow", [1, 1024], F32), ("mask", [512, 512], BF16)],
    "l0f": [("wgT", [D, 3072], I32), ("wuT", [D, 3072], I32),
            ("wdT", [3072, D], I32)],
    "l1a": [("wq", [D, D], F32), ("wk", [D, D], F32), ("wv", [D, D], F32),
            ("wo", [D, D], F32), ("w1", [D, 4096], F32), ("w2", [4096, D], F32),
            ("cols", [128, 64], F32), ("cols2", [128, 32], F32),
            ("brow", [1, 2048], F32)],
    "l1f": [("wg", [8, D, 2048], F32), ("wu", [8, D, 2048], F32),
            ("wd", [8, 2048, D], F32), ("swg", [D, 2048], F32),
            ("swu", [D, 2048], F32), ("swd", [2048, D], F32),
            ("gwT", [D, 8], F32), ("lns", [128, 16], F32),
            ("eyeb", [128, 128], BF16), ("units", [8, 1024], F32),
            ("eyef", [128, 128], F32)],
    "l2a": [("uvwT", [D, 4224], F32), ("owT", [2048, D], F32),
            ("cols", [128, 32], F32), ("ucol", [128, 16], F32),
            ("vbrow", [1, 2048], F32), ("ropec", [128, T], F32),
            ("ropes", [128, T], F32), ("swp", [128, 128], BF16),
            ("bias", [T, T], F32)],
    "l2f": [("seT", [D, T], F32), ("w1", [D, 4096], F32),
            ("w2", [4096, D], F32), ("cols", [128, 16], F32),
            ("eyeb", [128, 128], BF16), ("eyef", [128, 128], F32)],
    "l3a": [("wqkvT", [D, 3072], F32), ("woutT", [D, D], F32),
            ("proj", [D, 128], F32), ("cols", [128, 8], F32),
            ("eyeb", [128, 128], BF16)],
    "l3f": [("w1", [32, D, 256], F32), ("w2", [32, 256, D], F32),
            ("rwT", [D, 32], F32), ("rbr", [1, 32], F32),
            ("b1a", [128, 64], F32), ("b2a", [32, D], F32),
            ("eyef", [128, 128], F32)],
}
_PH_SCRATCH = {
    "l0a": [("xn", [D, T], BF16), ("q", [D, T], BF16), ("k", [D, T], BF16),
            ("y", [D, T], BF16)],
    "l0f": [("xn", [D, T], BF16), ("h", [3072, T], BF16)],
    "l1a": [("xn", [D, T], BF16), ("q", [D, T], BF16), ("k", [T, D], BF16),
            ("v", [T, D], BF16), ("a", [D, T], BF16), ("t1", [D, T], BF16),
            ("u", [D, T], BF16), ("h1", [4096, T], BF16),
            ("v2", [D, T], BF16), ("s2", [D, T], BF16)],
    "l1f": [("xn", [D, T], BF16), ("hs", [2048, 9 * T], BF16),
            ("wdb", [9, 2048, D], BF16)],
    "l2a": [("xn", [D, T], BF16), ("xn2", [D, T], BF16), ("u", [2048, T], BF16),
            ("v", [T, 2048], BF16), ("p2", [2048, T], BF16)],
    "l2f": [("se", [32, 128, 1024], BF16), ("sef", [D, T], BF16),
            ("sl", [T, D], BF16),
            ("proc", [T, D], BF16), ("w1b", [32, 128, 1024], BF16),
            ("w2b", [4096, D], BF16)],
    "l3a": [("xn", [D, T], BF16), ("q", [D, T], BF16), ("k", [D, T], BF16),
            ("v", [T, D], BF16), ("a", [D, T], BF16)],
    "l3f": [("xn", [D, T], BF16)],
}
_PH_EMIT = None


def _build_fused():
    nc = _mk_nc()
    emits = {"l0a": emit_l0a, "l0f": emit_l0f, "l1a": emit_l1a,
             "l1f": emit_l1f, "l2a": emit_l2a, "l2f": emit_l2f,
             "l3a": emit_l3a, "l3f": emit_l3f}
    order = ["l0a", "l0f", "l1a", "l1f", "l2a", "l2f", "l3a", "l3f"]
    X = [nc.dram_tensor("x", [D, T], F32, kind="ExternalInput").ap()]
    for i in range(7):
        X.append(nc.dram_tensor(f"xmid{i}", [D, T], F32, kind="Internal").ap())
    XO = nc.dram_tensor("xo", [T, D], F32, kind="ExternalOutput").ap()
    ds = {}
    for ph in order:
        dd = {}
        for nm, sh, dt in _PH_INPUTS[ph]:
            dd[nm] = nc.dram_tensor(f"i_{ph}_{nm}", sh, dt,
                                    kind="ExternalInput").ap()
        for nm, sh, dt in _PH_SCRATCH[ph]:
            dd[nm] = nc.dram_tensor(f"{ph}s_{nm}", sh, dt, kind="Internal").ap()
        ds[ph] = dd
    es = contextlib.ExitStack()
    with es:
        ctx = Ctx(nc, es)
        for v in (0.0, 1e-5, 1e-6, 1e-8):
            ctx.constcol(v)
        ctx.lnrow(0)
        for i, ph in enumerate(order):
            dd = ds[ph]
            dd["x"] = X[i]
            dd["xo"] = XO if ph == "l3f" else X[i + 1]
            ctx.s.fence()
            with contextlib.ExitStack() as pes:
                ctx.es = pes
                emits[ph](ctx, dd)
        ctx.es = es
        ctx.s.build(es)
    return nc


def run_fused(xs, prep):
    n = len(xs)
    in_maps = []
    for i in range(n):
        m = {"x": xs[i]}
        for ph, keys in _PH_INPUTS.items():
            for nm, _, _ in keys:
                m[f"i_{ph}_{nm}"] = prep[f"{ph}_{nm}"]
        in_maps.append(m)
    res = _run_phase("fused", _build_fused, in_maps)
    return [np.asarray(res.results[i]["xo"]) for i in range(n)]


def kernel(**inp):
    prep = _prep_shared(inp)
    _prep_l0a(inp, prep)
    _prep_l1a(inp, prep)
    _prep_l1f(inp, prep)
    _prep_l2a(inp, prep)
    _prep_l2f(inp, prep)
    _prep_l3a(inp, prep)
    _prep_l3f(inp, prep)
    x = np.asarray(inp["L0_attn_x"])
    xs = [np.ascontiguousarray(x[i].T).astype(np.float32) for i in range(NC)]
    xs = run_fused(xs, prep)
    return np.ascontiguousarray(np.stack(xs)).astype(np.float32)
